# version 03
# baseline (speedup 1.0000x reference)
import numpy as np
import concourse.bass as bass
import concourse.mybir as mybir
from concourse.bass_utils import run_bass_kernel_spmd
from concourse.bass_core_rust import IndirectOffsetOnAxis

F32 = mybir.dt.float32
I32 = mybir.dt.int32
AF = mybir.ActivationFunctionType
ALU = mybir.AluOpType
AXX = mybir.AxisListType.X

T, D, R, H, HD = 2048, 1024, 64, 16, 64
NT, ND = T // 128, D // 128
KC, RS, FF, BPC = 255, 256, 4096, 2
FC = 512
NFC = FF // FC

ENGS = ("tensor", "vector", "scalar", "gpsimd", "sync")


def host_consts():
    f32 = np.float32
    c = {}
    c["IDENT"] = np.eye(128, dtype=f32)
    c["TRIT"] = np.triu(np.ones((128, 128), dtype=f32))
    c["TRIST"] = np.triu(np.ones((128, 128), dtype=f32), k=1)
    c["ONES128"] = np.ones((128, 128), dtype=f32)
    c["ONESROW"] = np.ones((1, 128), dtype=f32)
    c["ONESCOL"] = np.ones((128, 1), dtype=f32)
    c["IOTAR"] = np.broadcast_to(np.arange(RS, dtype=f32), (128, RS)).copy()
    c["ARANGE"] = np.arange(T, dtype=f32).reshape(NT, 128).T.copy()
    c["COUNTS"] = np.arange(1, T + 1, dtype=f32).reshape(NT, 128).T.copy()
    cb = np.zeros((128, 2 * RS), dtype=f32)
    for qt in range(2):
        q = 128 * qt + np.arange(128)[:, None]
        k = np.arange(RS)[None, :]
        cb[:, RS * qt:RS * qt + RS] = np.where(k <= q, 0.0, np.float32(-1e9))
    c["CAUSAL"] = cb
    r2 = np.zeros((128, 128), dtype=f32)
    for h0 in (0, 64):
        for i in range(32):
            r2[h0 + i, h0 + 32 + i] = -1.0
            r2[h0 + 32 + i, h0 + i] = 1.0
    c["ROTT"] = r2.T.copy()
    inv = (1.0 / (f32(10000.0) ** (np.arange(0, HD, 2, dtype=f32) / f32(HD)))).astype(f32)
    fr = np.arange(T, dtype=f32)[:, None] * inv[None, :]
    emb = np.concatenate([fr, fr], axis=1)
    cosd = np.concatenate([np.cos(emb), np.cos(emb)], axis=1)
    sind = np.concatenate([np.sin(emb), np.sin(emb)], axis=1)
    tab = np.concatenate([cosd, sind], axis=1).astype(f32)
    c["CSTAB"] = np.concatenate([tab, tab], axis=0)
    return c


CONSTS = host_consts()
SB_CONSTS = [k for k in CONSTS if k != "CSTAB"]


class Sem:
    def __init__(self, h):
        self.h, self.n = h, 0

    def bump(self, k=1):
        self.n += k
        return self.n


class B:
    def __init__(self, nc):
        self.nc = nc
        self.plan = {e: [] for e in ENGS}
        self.sems = {e: Sem(nc.ctx.enter_context(nc.semaphore("s_" + e))) for e in ENGS}
        self.qsems = {e: Sem(nc.ctx.enter_context(nc.semaphore("q_" + e)))
                      for e in ("sync", "scalar", "gpsimd")}
        self._fw = {e: {} for e in ENGS}

    def emit(self, eng, fn, inc=None, inck=1):
        if eng in ("vector", "scalar") and inc is self.sems[eng] and inc.n > 0:
            self.plan[eng].append(lambda e, s=inc.h, t=inc.n: e.wait_ge(s, t))
        tgt = inc.bump(inck) if inc is not None else None
        if inc is not None:
            self.plan[eng].append(lambda e, fn=fn, s=inc.h, k=inck: fn(e).then_inc(s, k))
        else:
            self.plan[eng].append(lambda e, fn=fn: fn(e))
        return tgt

    def w(self, eng, sem, tgt):
        if tgt:
            self.plan[eng].append(lambda e, s=sem.h, t=tgt: e.wait_ge(s, t))

    def t(self, fn):
        return self.emit("tensor", fn, self.sems["tensor"])

    def v(self, fn):
        return self.emit("vector", fn, self.sems["vector"])

    def s(self, fn):
        return self.emit("scalar", fn, self.sems["scalar"])

    def dma(self, eng, fn):
        return self.emit(eng, fn, self.qsems[eng], 16)

    def fence(self):
        snap = list(self.sems.values()) + list(self.qsems.values())
        for e in ENGS:
            for s in snap:
                if s is self.sems[e]:
                    continue
                if s.n > self._fw[e].get(id(s), 0):
                    self.w(e, s, s.n)
                    self._fw[e][id(s)] = s.n

    def finalize(self):
        with self.nc.Block() as blk:
            for name in ENGS:
                getattr(blk, name)((lambda lst: (lambda e: [f(e) for f in lst]))(self.plan[name]))


def build_nc(dbg_names=(), stop_after="all", end_dumps=()):
    nc = bass.Bass("TRN2", target_bir_lowering=False, detect_race_conditions=False)
    dram = {}

    def din(name, shape):
        dram[name] = nc.dram_tensor(name, list(shape), F32, kind="ExternalInput").ap()
        return dram[name]

    x_in = din("x", (BPC, T, D))
    out_t = nc.dram_tensor("out", [BPC, T, D], F32, kind="ExternalOutput").ap()
    for nm, sh in (("WQa", (D, D)), ("WKa", (D, D)), ("WOa", (D, D)), ("WINV", (D, D)),
                   ("UB", (D, R)), ("VB", (D, R)), ("WB", (D, R)),
                   ("UT", (D, R)), ("VT", (D, R)), ("WT", (D, R)), ("XTm", (D, R)),
                   ("BIASB", (D, 1)), ("BIAST", (D, 1)), ("ABI", (1, 1)), ("ATRI", (1, 1)),
                   ("RW", (D, 1)), ("RB", (1, 1)),
                   ("WQ", (D, D)), ("WK", (D, D)), ("WV", (D, D)), ("WOf", (D, D)),
                   ("W1", (D, FF)), ("W2", (FF, D)), ("LN1", (D, 1)), ("LN2", (D, 1))):
        din(nm, sh)
    for k in CONSTS:
        din(k, CONSTS[k].shape)
    DUMP = {}
    dbg = {}
    for name, width in dbg_names:
        dbg[name] = nc.dram_tensor("dbg_" + name, [128, width], F32, kind="ExternalOutput").ap()

    S = {}

    def sb(name, shape, dtype=F32):
        S[name] = nc.alloc_sbuf_tensor(name, list(shape), dtype).ap()
        return S[name]

    x_tok = sb("x_tok", (128, NT * D))
    acat = sb("acat", (128, ND * 448))
    outb = sb("outb", (128, D))
    wbias_row = sb("wbias_row", (1, D))
    wbias_rep = sb("wbias_rep", (128, D))
    for k in SB_CONSTS:
        sb("c" + k, CONSTS[k].shape)
    invc = sb("invc", (128, NT))
    invc2 = sb("invc2", (128, NT))
    abic = sb("abic", (128, 1))
    atric = sb("atric", (128, 1))
    orv = sb("orv", (128, 1))
    wrc = sb("wrc", (128, 1))
    srepc = sb("srepc", (128, 1))
    rbc = sb("rbc", (128, 1))
    rwsb = sb("rwsb", (128, ND))
    ln1sb = sb("ln1sb", (128, ND))
    ln2sb = sb("ln2sb", (128, ND))
    negs = sb("negs", (1, 448))
    scal = sb("scal", (1, 8))
    wraw = sb("wraw", (128, NT))
    stmu = sb("stmu", (128, NT))
    stro = sb("stro", (128, NT))
    b2c = sb("b2c", (128, NT))
    selm = sb("selm", (128, NT))
    ccum = sb("ccum", (128, NT))
    cols5 = sb("cols5", (128, 5 * NT))
    selcols = sb("selcols", (128, 10))
    selidx = sb("selidx", (128, 4), I32)
    carry = sb("carry", (128, 256))
    epsc = sb("epsc", (128, 4))
    XRS = sb("xrs", (128, 2 * D))
    CSG = sb("csg", (128, 2 * RS))
    ARW = 24000
    arena = sb("arena", (128, ARW))
    PS = [nc.alloc_psum_tensor("pb%d" % i, [128, 512], F32).ap() for i in range(8)]

    b = B(nc)
    sT, sV, sS, sG = (b.sems[e] for e in ("tensor", "vector", "scalar", "gpsimd"))
    qS, qA, qG = (b.qsems[e] for e in ("sync", "scalar", "gpsimd"))

    class Scope:
        def __init__(self):
            self.off, self.map = 0, {}

        def g(self, name, width):
            if name not in self.map:
                assert self.off + width <= ARW, (name, self.off, width)
                self.map[name] = (self.off, width)
                self.off += width
            o, wd = self.map[name]
            assert wd == width
            return arena[:, o:o + wd]

        def row(self, name, width):
            return self.g(name, width)[0:1, :]

    def tap(name, ap):
        if name in dbg:
            for sem in (sT, sV, sS, sG):
                b.w("sync", sem, sem.n)
            b.dma("sync", lambda e, ap=ap, name=name: e.dma_start(
                out=dbg[name][:ap.shape[0], :ap.shape[1]], in_=ap))

    for k in SB_CONSTS:
        b.dma("sync", lambda e, k=k: e.dma_start(out=S["c" + k][:], in_=dram[k][:, :]))
    for c in range(ND):
        b.dma("sync", lambda e, c=c: e.dma_start(out=rwsb[:, c:c + 1], in_=dram["RW"][c * 128:(c + 1) * 128, :]))
        b.dma("sync", lambda e, c=c: e.dma_start(out=ln1sb[:, c:c + 1], in_=dram["LN1"][c * 128:(c + 1) * 128, :]))
        b.dma("sync", lambda e, c=c: e.dma_start(out=ln2sb[:, c:c + 1], in_=dram["LN2"][c * 128:(c + 1) * 128, :]))
    b.dma("sync", lambda e: e.dma_start(out=scal[:, 0:1], in_=dram["ABI"][:, :]))
    b.dma("sync", lambda e: e.dma_start(out=scal[:, 1:2], in_=dram["ATRI"][:, :]))
    t_sc3 = b.dma("sync", lambda e: e.dma_start(out=scal[:, 2:3], in_=dram["RB"][:, :]))
    t_x0 = b.dma("sync", lambda e: e.dma_start(
        out=x_tok.rearrange("q (n d) -> q n d", d=D),
        in_=x_in[0].rearrange("(n p) d -> p n d", p=128)))
    t_outinit = b.dma("gpsimd", lambda e: e.dma_start(out=out_t[1], in_=x_in[1]))
    b.v(lambda e: e.memset(epsc[:, 0:1], 1e-5))
    b.v(lambda e: e.memset(epsc[:, 1:2], 1e-6))
    b.v(lambda e: e.memset(epsc[:, 2:3], 1.0 / D))

    pre = Scope()
    SW = {}
    for nm in ("UB", "VB", "WB", "UT", "VT", "WT", "XTm"):
        SW[nm] = pre.g("sw_" + nm, ND * R)
        for c in range(ND):
            b.dma("sync", lambda e, nm=nm, c=c: e.dma_start(
                out=SW[nm][:, c * R:(c + 1) * R], in_=dram[nm][c * 128:(c + 1) * 128, :]))
    bbcol = pre.g("biasb", ND)
    btcol = pre.g("biast", ND)
    for c in range(ND):
        b.dma("sync", lambda e, c=c: e.dma_start(out=bbcol[:, c:c + 1], in_=dram["BIASB"][c * 128:(c + 1) * 128, :]))
        b.dma("sync", lambda e, c=c: e.dma_start(out=btcol[:, c:c + 1], in_=dram["BIAST"][c * 128:(c + 1) * 128, :]))
    BW = pre.g("bigw", ND * D)
    WOT = pre.g("wot", ND * D)
    VB2 = pre.g("vb2", ND * R)
    WB2 = pre.g("wb2", ND * R)
    BCOMB = pre.g("bcomb", ND)
    WORC = pre.g("worc", ND)
    WBIC = pre.g("wbic", ND)
    DUMP["wbic"] = WBIC
    DUMP["bcomb"] = BCOMB
    DUMP["bbcol"] = bbcol
    DUMP["btcol"] = btcol
    DUMP["rwsb"] = rwsb
    DUMP["ln1sb"] = ln1sb
    t_pre_loads = qS.n

    _stopped = []

    def emit_dumps():
        for nm in end_dumps:
            ap = DUMP[nm]
            dd = nc.dram_tensor("dump_" + nm, [ap.shape[0], ap.shape[1]], F32,
                                kind="ExternalOutput").ap()
            b.dma("sync", lambda e, ap=ap, dd=dd: e.dma_start(out=dd[:, :], in_=ap))
        b.w("sync", qS, qS.n)

    def maybe_stop(tag):
        if stop_after == tag:
            emit_dumps()
            b.w("sync", qG, qG.n)
            b.finalize()
            _stopped.append(True)
        return bool(_stopped)

    if maybe_stop("load"):
        return nc, dram
    b.w("tensor", qS, t_sc3)
    b.t(lambda e: e.matmul(PS[0][:, 0:1], S["cONESROW"][:, :], scal[0:1, 0:1], start=True, stop=True))
    b.t(lambda e: e.matmul(PS[0][:, 1:2], S["cONESROW"][:, :], scal[0:1, 1:2], start=True, stop=True))
    tm = b.t(lambda e: e.matmul(PS[0][:, 2:3], S["cONESROW"][:, :], scal[0:1, 2:3], start=True, stop=True))
    b.w("vector", sT, tm)
    b.v(lambda e: e.tensor_copy(abic[:], PS[0][:, 0:1]))
    b.v(lambda e: e.tensor_copy(atric[:], PS[0][:, 1:2]))
    b.v(lambda e: e.tensor_copy(rbc[:], PS[0][:, 2:3]))
    b.w("vector", qS, t_pre_loads)
    b.v(lambda e: e.tensor_scalar(BCOMB[:, :], bbcol[:, :], abic[:, 0:1], None, ALU.mult))
    b.v(lambda e: e.tensor_add(BCOMB[:, :], BCOMB[:, :], bbcol[:, :]))
    b.v(lambda e: e.scalar_tensor_tensor(BCOMB[:, :], btcol[:, :], atric[:, 0:1], BCOMB[:, :], ALU.mult, ALU.add))
    b.v(lambda e: e.reciprocal(invc[:], S["cCOUNTS"][:, :]))
    tv = b.v(lambda e: e.tensor_mul(invc2[:], invc[:], invc[:]))
    b.w("tensor", sV, tv)

    if maybe_stop("p1"):
        return nc, dram

    def ld_bigw(nm):
        t = None
        b.w("sync", sT, sT.n)
        for c in range(ND):
            t = b.dma("sync", lambda e, nm=nm, c=c: e.dma_start(
                out=BW[:, c * D:(c + 1) * D], in_=dram[nm][c * 128:(c + 1) * 128, :]))
        return t

    def wsmall(dst_fn, vsm, scale_col=None, tilepos=None):
        ev = {}
        for m in range(ND):
            slot = (PS[1] if m % 2 == 0 else PS[6])[:, 0:R]
            oslot = slot if tilepos is None else slot[64:128, :]
            if m >= 2:
                b.w("tensor", sV, ev[m - 2])
            for j in range(ND):
                b.t(lambda e, m=m, j=j, oslot=oslot: e.matmul(
                    oslot, BW[:, j * D + m * 128:j * D + (m + 1) * 128],
                    vsm[:, j * R:(j + 1) * R], start=(j == 0), stop=(j == ND - 1),
                    tile_position=tilepos))
            tmm = sT.n
            b.w("vector", sT, tmm)
            dst = dst_fn(m)
            if scale_col is None:
                ev[m] = b.v(lambda e, dst=dst, oslot=oslot: e.tensor_copy(dst, oslot))
            else:
                ev[m] = b.v(lambda e, dst=dst, oslot=oslot, sc=scale_col: e.tensor_scalar(
                    dst, oslot, sc, None, ALU.mult))
        b.w("tensor", sV, sV.n)

    tw = ld_bigw("WINV")
    b.w("tensor", qS, tw)
    wsmall(lambda m: VB2[:, m * R:(m + 1) * R], SW["VB"])
    wsmall(lambda m: WB2[:, m * R:(m + 1) * R], SW["WB"])
    if maybe_stop("p2"):
        return nc, dram
    tw = ld_bigw("WQa")
    b.w("tensor", qS, tw)
    wsmall(lambda m: acat[:, 448 * m + 0:448 * m + 64], SW["VB"])
    wsmall(lambda m: acat[:, 448 * m + 64:448 * m + 128], WB2, scale_col=abic[:, 0:1])
    wsmall(lambda m: acat[:, 448 * m + 128:448 * m + 192], SW["VT"])
    tw = ld_bigw("WKa")
    b.w("tensor", qS, tw)
    wsmall(lambda m: acat[:, 448 * m + 192:448 * m + 256], SW["WB"])
    wsmall(lambda m: acat[:, 448 * m + 256:448 * m + 320], VB2)
    wsmall(lambda m: acat[:, 448 * m + 320:448 * m + 384], SW["WT"])
    for m in range(ND):
        b.v(lambda e, m=m: e.tensor_copy(acat[:, 448 * m + 384:448 * m + 448],
                                         SW["XTm"][:, m * R:(m + 1) * R]))
    if maybe_stop("p3"):
        return nc, dram
    tw = ld_bigw("WOa")
    b.w("tensor", qS, tw)
    ev = {}
    for m in range(ND):
        slot = (PS[1] if m % 2 == 0 else PS[6])[:, 128:129]
        if m >= 2:
            b.w("tensor", sV, ev[m - 2])
        for k in range(ND):
            b.t(lambda e, m=m, k=k, slot=slot: e.matmul(
                slot, BW[:, k * D + m * 128:k * D + (m + 1) * 128],
                rwsb[:, k:k + 1], start=(k == 0), stop=(k == ND - 1)))
        tmm = sT.n
        b.w("vector", sT, tmm)
        ev[m] = b.v(lambda e, m=m, slot=slot: e.tensor_copy(WORC[:, m:m + 1], slot))
    b.w("tensor", sV, sV.n)
    if maybe_stop("p3a"):
        return nc, dram
    WOTV = WOT.rearrange("p (j x) -> p j x", x=D)
    ev = {}
    for c in range(ND):
        bankA = PS[2] if c % 2 == 0 else PS[4]
        bankB = PS[3] if c % 2 == 0 else PS[5]
        if c >= 2:
            b.w("tensor", sV, ev[c - 2])
        for j in range(ND):
            ps = (bankA if j < 4 else bankB)[:, (j % 4) * 128:(j % 4) * 128 + 128]
            b.t(lambda e, c=c, j=j, ps=ps: e.transpose(
                ps, BW[:, c * D + j * 128:c * D + (j + 1) * 128], S["cIDENT"][:, :]))
        tmm = sT.n
        b.w("vector", sT, tmm)
        b.v(lambda e, c=c, bankA=bankA: e.tensor_copy(
            WOTV[:, 0:4, c * 128:(c + 1) * 128], bankA[:, 0:512]))
        ev[c] = b.v(lambda e, c=c, bankB=bankB: e.tensor_copy(
            WOTV[:, 4:8, c * 128:(c + 1) * 128], bankB[:, 0:512]))
    b.w("tensor", sV, sV.n)
    if isinstance(stop_after, str) and stop_after.startswith("wot"):
        emit_dumps()
        b.w("sync", qG, qG.n)
        b.finalize()
        return nc, dram
    if maybe_stop("p4"):
        return nc, dram
    for n2 in range(2):
        slot = (PS[3] if n2 == 0 else PS[4])[0:64, 0:512]
        for j in range(ND):
            b.t(lambda e, n2=n2, j=j, slot=slot: e.matmul(
                slot, SW["UB"][:, j * R:(j + 1) * R],
                WOT[:, j * D + n2 * 512:j * D + (n2 + 1) * 512],
                start=(j == 0), stop=(j == ND - 1)))
        tmm = sT.n
        b.w("vector", sT, tmm)
        b.v(lambda e, n2=n2, slot=slot: e.tensor_copy(outb[0:64, n2 * 512:(n2 + 1) * 512], slot))
    for n2 in range(2):
        slot = (PS[5] if n2 == 0 else PS[6])[64:128, 0:512]
        for j in range(ND):
            b.t(lambda e, n2=n2, j=j, slot=slot: e.matmul(
                slot, SW["UT"][:, j * R:(j + 1) * R],
                WOT[:, j * D + n2 * 512:j * D + (n2 + 1) * 512],
                start=(j == 0), stop=(j == ND - 1), tile_position=(0, 64)))
        tmm = sT.n
        b.w("vector", sT, tmm)
        b.v(lambda e, n2=n2, slot=slot: e.tensor_scalar(
            outb[64:128, n2 * 512:(n2 + 1) * 512], slot, atric[64:128, 0:1], None, ALU.mult))
    b.w("tensor", sV, sV.n)
    slot = PS[1][0:64, 130:131]
    for j in range(ND):
        b.t(lambda e, j=j, slot=slot: e.matmul(
            slot, SW["UB"][:, j * R:(j + 1) * R], WORC[:, j:j + 1],
            start=(j == 0), stop=(j == ND - 1)))
    tmm = sT.n
    b.w("vector", sT, tmm)
    b.v(lambda e, slot=slot: e.tensor_copy(orv[0:64, 0:1], slot))
    slot = PS[6][64:128, 132:133]
    for j in range(ND):
        b.t(lambda e, j=j, slot=slot: e.matmul(
            slot, SW["UT"][:, j * R:(j + 1) * R], WORC[:, j:j + 1],
            start=(j == 0), stop=(j == ND - 1), tile_position=(0, 64)))
    tmm = sT.n
    b.w("vector", sT, tmm)
    b.v(lambda e, slot=slot: e.tensor_scalar(orv[64:128, 0:1], slot, atric[64:128, 0:1], None, ALU.mult))
    b.w("tensor", sV, sV.n)
    if maybe_stop("p5"):
        return nc, dram
    ev = {}
    for m in range(ND):
        slot = (PS[1] if m % 2 == 0 else PS[6])[:, 134:135]
        if m >= 2:
            b.w("tensor", sV, ev[m - 2])
        for j in range(ND):
            b.t(lambda e, m=m, j=j, slot=slot: e.matmul(
                slot, WOT[:, j * D + m * 128:j * D + (m + 1) * 128],
                BCOMB[:, j:j + 1], start=(j == 0), stop=(j == ND - 1)))
        tmm = sT.n
        b.w("vector", sT, tmm)
        ev[m] = b.v(lambda e, m=m, slot=slot: e.tensor_copy(WBIC[:, m:m + 1], slot))
    b.w("tensor", sV, sV.n)
    for m in range(ND):
        slot = PS[2][0:1, 0:128]
        b.t(lambda e, m=m, slot=slot: e.transpose(slot, WBIC[:, m:m + 1], S["cIDENT"][:, :]))
        tmm = sT.n
        b.w("vector", sT, tmm)
        tv = b.v(lambda e, m=m, slot=slot: e.tensor_copy(wbias_row[0:1, m * 128:(m + 1) * 128], slot))
        b.w("tensor", sV, tv)
    for n2 in range(2):
        slot = (PS[3] if n2 == 0 else PS[4])[:, 0:512]
        b.t(lambda e, n2=n2, slot=slot: e.matmul(
            slot, S["cONESROW"][:, :], wbias_row[0:1, n2 * 512:(n2 + 1) * 512], start=True, stop=True))
        tmm = sT.n
        b.w("vector", sT, tmm)
        b.v(lambda e, n2=n2, slot=slot: e.tensor_copy(wbias_rep[:, n2 * 512:(n2 + 1) * 512], slot))
    slot = PS[1][0:1, 136:137]
    for m in range(ND):
        b.t(lambda e, m=m, slot=slot: e.matmul(
            slot, WBIC[:, m:m + 1], rwsb[:, m:m + 1], start=(m == 0), stop=(m == ND - 1)))
    slot2 = PS[1][0:1, 138:139]
    for m in range(ND):
        b.t(lambda e, m=m, slot2=slot2: e.matmul(
            slot2, rwsb[:, m:m + 1], S["cONESCOL"][:, 0:1], start=(m == 0), stop=(m == ND - 1)))
    tmm = sT.n
    b.w("vector", sT, tmm)
    b.v(lambda e, slot=slot: e.tensor_copy(scal[0:1, 3:4], slot))
    tv = b.v(lambda e, slot2=slot2: e.tensor_copy(scal[0:1, 4:5], slot2))
    b.w("tensor", sV, tv)
    b.t(lambda e: e.matmul(PS[1][:, 140:141], S["cONESROW"][:, :], scal[0:1, 3:4], start=True, stop=True))
    tmm = b.t(lambda e: e.matmul(PS[1][:, 141:142], S["cONESROW"][:, :], scal[0:1, 4:5], start=True, stop=True))
    b.w("vector", sT, tmm)
    b.v(lambda e: e.tensor_copy(wrc[:], PS[1][:, 140:141]))
    b.v(lambda e: e.tensor_copy(srepc[:], PS[1][:, 141:142]))
    if maybe_stop("p6"):
        return nc, dram
    slot = PS[7][0:1, 0:448]
    for c in range(ND):
        b.t(lambda e, c=c, slot=slot: e.matmul(
            slot, S["cONESCOL"][:, :], acat[:, c * 448:(c + 1) * 448],
            start=(c == 0), stop=(c == ND - 1)))
    tmm = sT.n
    b.w("vector", sT, tmm)
    b.v(lambda e, slot=slot: e.tensor_scalar(negs[:, :], slot, -1.0, None, ALU.mult))
    b.fence()
    tap("acat0", acat[:, 0:448])
    tap("acat7", acat[:, 7 * 448:8 * 448])
    tap("outb", outb[:, :])
    tap("wbias_rep", wbias_rep[:, 0:D])
    tap("orv", orv[:, 0:1])
    tap("negs", negs[0:1, :])

    aS = Scope()

    def phaseA(bi, t_xload):
        XT2 = [aS.g("xT0", D), aS.g("xT1", D)]
        XSQ = aS.g("xsq", D)
        PP = [aS.g("pp0", 448), aS.g("pp1", 448)]
        CUM = aS.g("cum", 256)
        RT2 = [aS.g("rt0", 128), aS.g("rt1", 128)]
        MUR = aS.row("murow", 128)
        STC = aS.g("stc", 8)
        b.w("tensor", qS, t_xload)
        tv = b.v(lambda e: e.memset(carry[:, :], 0.0))
        b.w("tensor", sV, tv)
        ev_prev = {}
        for t in range(NT):
            xT = XT2[t % 2]
            pp = PP[t % 2]
            rt = RT2[t % 2]
            if t >= 1:
                b.w("tensor", sV, ev_prev["xT"])
            for c in range(ND):
                ps = (PS[2] if c < 4 else PS[3])[:, (c % 4) * 128:(c % 4) * 128 + 128]
                b.t(lambda e, t=t, c=c, ps=ps: e.transpose(
                    ps, x_tok[:, t * D + c * 128:t * D + (c + 1) * 128], S["cIDENT"][:, :]))
            tm_tr = sT.n
            b.w("vector", sT, tm_tr)
            b.v(lambda e, xT=xT: e.tensor_copy(xT[:, 0:512], PS[2][:, 0:512]))
            ev_xT = b.v(lambda e, xT=xT: e.tensor_copy(xT[:, 512:1024], PS[3][:, 0:512]))
            ev_prev["xT"] = ev_xT
            b.w("gpsimd", sT, tm_tr)
            t_wb = b.dma("gpsimd", lambda e, t=t: e.dma_start(
                out=x_tok[:, t * D:(t + 1) * D], in_=wbias_rep[:, :], accum_op=ALU.add))
            b.w("scalar", sV, ev_xT)
            sq_t = b.s(lambda e, xT=xT: e.activation(XSQ[:, :], xT[:, :], AF.Square))
            if t >= 1:
                b.w("tensor", sV, ev_prev["stats"])
            b.w("tensor", sV, ev_xT)
            for c in range(ND):
                b.t(lambda e, c=c, xT=xT: e.matmul(
                    PS[1][:, 0:1], xT[:, c * 128:(c + 1) * 128], S["cONESCOL"][:, :],
                    start=(c == 0), stop=(c == ND - 1)))
            b.w("tensor", sS, sq_t)
            for c in range(ND):
                b.t(lambda e, c=c: e.matmul(
                    PS[1][:, 1:2], XSQ[:, c * 128:(c + 1) * 128], S["cONESCOL"][:, :],
                    start=(c == 0), stop=(c == ND - 1)))
            for c in range(ND):
                b.t(lambda e, c=c, xT=xT: e.matmul(
                    PS[5][:, 496:497], xT[:, c * 128:(c + 1) * 128], rwsb[:, c:c + 1],
                    start=(c == 0), stop=(c == ND - 1)))
            tm_st = sT.n
            if t >= 1:
                b.w("tensor", sV, ev_prev["pp"])
            for c in range(ND):
                b.t(lambda e, c=c, xT=xT: e.matmul(
                    PS[0][:, 0:448], xT[:, c * 128:(c + 1) * 128],
                    acat[:, c * 448:(c + 1) * 448], start=(c == 0), stop=False))
            b.w("vector", sT, tm_st)
            b.v(lambda e: e.tensor_scalar(STC[:, 0:1], PS[1][:, 0:1], 1.0 / D, None, ALU.mult))
            b.v(lambda e: e.tensor_scalar(STC[:, 2:3], PS[1][:, 1:2], 1.0 / D, None, ALU.mult))
            b.v(lambda e: e.tensor_mul(STC[:, 1:2], STC[:, 0:1], STC[:, 0:1]))
            ev_var = b.v(lambda e: e.tensor_sub(STC[:, 2:3], STC[:, 2:3], STC[:, 1:2]))
            b.w("scalar", sV, ev_var)
            sq2 = b.s(lambda e: e.activation(STC[:, 3:4], STC[:, 2:3], AF.Sqrt, bias=epsc[:, 0:1]))
            b.w("vector", sS, sq2)
            ev_rstd = b.v(lambda e: e.reciprocal(STC[:, 4:5], STC[:, 3:4]))
            b.w("tensor", sV, ev_var)
            tmu = b.t(lambda e: e.transpose(PS[5][0:1, 0:128], STC[:, 0:1], S["cIDENT"][:, :]))
            b.w("vector", sT, tmu)
            ev_mur = b.v(lambda e: e.tensor_copy(MUR[:, :], PS[5][0:1, 0:128]))
            b.w("tensor", sV, ev_mur)
            b.t(lambda e: e.matmul(PS[0][:, 0:448], MUR[:, :], negs[0:1, :], start=False, stop=True))
            tm_corr = sT.n
            b.w("vector", sT, tm_corr)
            b.w("vector", sV, ev_rstd)
            ev_pp = b.v(lambda e, pp=pp: e.tensor_scalar(
                pp[:, 0:448], PS[0][:, 0:448], STC[:, 4:5], None, ALU.mult))
            ev_prev["pp"] = ev_pp
            if t <= 1 and bi == 0:
                tap("pp%d" % t, pp[:, 0:448])
            if t >= 1:
                b.w("tensor", sV, ev_prev["cum"])
            b.w("tensor", sV, ev_pp)
            b.t(lambda e, pp=pp: e.matmul(PS[4][:, 0:256], S["cTRIT"][:, :], pp[:, 192:448], start=True, stop=True))
            tm_cum = b.t(lambda e, pp=pp: e.matmul(PS[4][:, 256:512], S["cONES128"][:, :], pp[:, 192:448], start=True, stop=True))
            b.w("vector", sT, tm_cum)
            b.v(lambda e: e.tensor_add(CUM[:, :], PS[4][:, 0:256], carry[:, :]))
            ev_cum = b.v(lambda e: e.tensor_add(carry[:, :], carry[:, :], PS[4][:, 256:512]))
            ev_prev["cum"] = ev_cum
            b.v(lambda e, pp=pp: e.tensor_mul(XSQ[:, 0:64], pp[:, 0:64], CUM[:, 0:64]))
            b.v(lambda e, pp=pp: e.tensor_mul(XSQ[:, 64:128], pp[:, 64:128], CUM[:, 64:128]))
            b.v(lambda e: e.tensor_add(XSQ[:, 0:64], XSQ[:, 0:64], XSQ[:, 64:128]))
            b.v(lambda e, rt=rt, t=t: e.tensor_scalar(
                rt[:, 0:64], XSQ[:, 0:64], invc[:, t:t + 1], None, ALU.mult))
            b.v(lambda e, pp=pp: e.tensor_mul(XSQ[:, 64:128], pp[:, 128:192], CUM[:, 128:192]))
            ev_R = b.v(lambda e, rt=rt, t=t: e.scalar_tensor_tensor(
                rt[:, 64:128], XSQ[:, 64:128], invc2[:, t:t + 1], CUM[:, 192:256], ALU.mult, ALU.mult))
            b.w("tensor", sV, ev_R)
            trt = b.t(lambda e, rt=rt: e.transpose(PS[5][:, 128:256], rt[:, :], S["cIDENT"][:, :]))
            b.w("vector", sT, trt)
            ev_rt = b.v(lambda e, rt=rt: e.tensor_copy(rt[:, :], PS[5][:, 128:256]))
            b.w("tensor", sV, ev_rt)
            if t <= 1 and bi == 0:
                tap("rtT%d" % t, rt[:, :])
            b.t(lambda e, rt=rt: e.matmul(PS[5][:, 497:498], rt[:, :], orv[:, 0:1], start=True, stop=True))
            tm_w = sT.n
            b.w("vector", sT, tm_w)
            b.v(lambda e, t=t: e.tensor_copy(b2c[:, t:t + 1], PS[5][:, 496:497]))
            b.v(lambda e, t=t: e.tensor_copy(ccum[:, t:t + 1], PS[5][:, 497:498]))
            b.v(lambda e, t=t: e.tensor_scalar(
                wraw[:, t:t + 1], PS[5][:, 496:497], wrc[:, 0:1], None, ALU.add))
            ev_wraw = b.v(lambda e, t=t: e.tensor_add(
                wraw[:, t:t + 1], wraw[:, t:t + 1], PS[5][:, 497:498]))
            ev_prev["stats"] = ev_wraw
            ev_prev["cum"] = ev_wraw
            if t >= 1:
                b.w("tensor", sV, ev_prev["out"])
            for n2 in range(2):
                b.t(lambda e, rt=rt, n2=n2: e.matmul(
                    (PS[6] if n2 == 0 else PS[7])[:, 0:512], rt[:, :],
                    outb[:, n2 * 512:(n2 + 1) * 512], start=True, stop=True))
            tm_out = sT.n
            b.w("vector", sT, tm_out)
            b.w("vector", qG, t_wb)
            b.v(lambda e, t=t: e.tensor_add(
                x_tok[:, t * D:t * D + 512], x_tok[:, t * D:t * D + 512], PS[6][:, 0:512]))
            ev_out = b.v(lambda e, t=t: e.tensor_add(
                x_tok[:, t * D + 512:(t + 1) * D], x_tok[:, t * D + 512:(t + 1) * D], PS[7][:, 0:512]))
            ev_prev["out"] = ev_out
        b.fence()
        tap("xA%d" % bi, x_tok[:, 0:4096])
        tap("wrawA%d" % bi, wraw[:, :])
        tap("xrA%d" % bi, b2c[:, :])
        tap("rorvA%d" % bi, ccum[:, :])
        tap("wrcA%d" % bi, wrc[:, :])
        tap("srepA%d" % bi, srepc[:, :])

    fS = Scope()

    def phaseFpre(bi):
        DUMP = fS.g("dump", D)
        WREP = fS.g("wrep", T)
        GT2 = [fS.g("gt0", T), fS.g("gt1", T)]
        ST2 = [fS.g("st0", RS), fS.g("st1", RS)]
        WTR = fS.g("wtr", 128)
        WROW = fS.row("wrow", T)
        BASE2 = [fS.g("base0", D), fS.g("base1", D)]
        AC = fS.g("acol", NT + 8)
        BC = fS.g("bcol", NT)
        TMP = fS.g("ftmp", NT)
        for t in range(NT):
            b.v(lambda e, t=t: e.reduce_sum(stmu[:, t:t + 1], x_tok[:, t * D:(t + 1) * D], axis=AXX))
            b.s(lambda e, t=t: e.activation(DUMP[:, :], x_tok[:, t * D:(t + 1) * D],
                                            AF.Square, accum_out=stro[:, t:t + 1]))
        b.v(lambda e: e.tensor_scalar(stmu[:, :], stmu[:, :], 1.0 / D, None, ALU.mult))
        b.w("vector", sS, sS.n)
        b.v(lambda e: e.tensor_scalar(stro[:, :], stro[:, :], 1.0 / D, None, ALU.mult))
        b.v(lambda e: e.tensor_mul(b2c[:, :], stmu[:, :], stmu[:, :]))
        tv = b.v(lambda e: e.tensor_sub(stro[:, :], stro[:, :], b2c[:, :]))
        b.w("scalar", sV, tv)
        ts_ = b.s(lambda e: e.activation(stro[:, :], stro[:, :], AF.Sqrt, bias=epsc[:, 0:1]))
        b.w("vector", sS, ts_)
        b.v(lambda e: e.reciprocal(stro[:, :], stro[:, :]))
        b.v(lambda e: e.tensor_mul(b2c[:, :], stmu[:, :], stro[:, :]))
        b.v(lambda e: e.tensor_scalar(b2c[:, :], b2c[:, :], -1.0, None, ALU.mult))
        b.v(lambda e: e.tensor_scalar(TMP[:, :], stmu[:, :], srepc[:, 0:1], None, ALU.mult))
        b.v(lambda e: e.tensor_sub(TMP[:, :], wraw[:, :], TMP[:, :]))
        b.v(lambda e: e.tensor_mul(TMP[:, :], TMP[:, :], stro[:, :]))
        tv_w = b.v(lambda e: e.tensor_scalar(TMP[:, :], TMP[:, :], rbc[:, 0:1], None, ALU.add))
        b.w("tensor", sV, tv_w)
        tmm = b.t(lambda e: e.transpose(PS[5][0:16, 0:128], TMP[:, 0:16], S["cIDENT"][:, :]))
        b.w("vector", sT, tmm)
        tv = b.v(lambda e: e.tensor_copy(WTR[0:16, :], PS[5][0:16, 0:128]))
        b.w("gpsimd", sV, tv)
        tg = b.dma("gpsimd", lambda e: e.dma_start(out=WROW[:, :], in_=WTR[0:16, :]))
        b.w("tensor", qG, tg)
        wrep_ev = {}
        for n4 in range(4):
            ps = (PS[2] if n4 % 2 == 0 else PS[3])[:, 0:512]
            if n4 >= 2:
                b.w("tensor", sV, wrep_ev[n4 - 2])
            b.t(lambda e, n4=n4, ps=ps: e.matmul(
                ps, S["cONESROW"][:, :], WROW[0:1, n4 * 512:(n4 + 1) * 512], start=True, stop=True))
            tmm = sT.n
            b.w("vector", sT, tmm)
            wrep_ev[n4] = b.v(lambda e, n4=n4, ps=ps: e.tensor_copy(WREP[:, n4 * 512:(n4 + 1) * 512], ps))
        ev_gt = {}
        for j in range(NT):
            gt = GT2[j % 2]
            if j >= 2:
                b.w("vector", sT, ev_gt[j - 2])
            tv = b.v(lambda e, gt=gt, j=j: e.tensor_scalar(
                gt[:, :], WREP[:, :], TMP[:, j:j + 1], None, ALU.is_lt))
            b.w("tensor", sV, tv)
            for i in range(NT):
                b.t(lambda e, gt=gt, i=i, j=j: e.matmul(
                    PS[1][:, i:i + 1], gt[:, i * 128:(i + 1) * 128], S["cONESCOL"][:, :],
                    start=(j == 0 and i == 0), stop=(j == NT - 1 and i == NT - 1)))
            ev_gt[j] = sT.n
        b.w("vector", sT, sT.n)
        tv = b.v(lambda e: e.tensor_scalar(selm[:, :], PS[1][:, 0:16], 254.5, None, ALU.is_le))
        b.w("tensor", sV, tv)
        b.t(lambda e: e.matmul(PS[4][:, 0:16], S["cTRIST"][:, :], selm[:, :], start=True, stop=False))
        tmm = b.t(lambda e: e.matmul(PS[5][0:1, 0:16], S["cONESCOL"][:, :], selm[:, :], start=True, stop=True))
        b.w("vector", sT, tmm)
        tv = b.v(lambda e: e.tensor_copy(WTR[0:1, 0:16], PS[5][0:1, 0:16]))
        b.w("tensor", sV, tv)
        tmm = b.t(lambda e: e.transpose(PS[5][0:16, 16:17], WTR[0:1, 0:16], S["cIDENT"][0:1, 0:1]))
        b.w("vector", sT, tmm)
        tv = b.v(lambda e: e.tensor_copy(WTR[0:16, 16:17], PS[5][0:16, 16:17]))
        b.w("tensor", sV, tv)
        tmm = b.t(lambda e: e.matmul(PS[5][0:16, 17:18], S["cTRIST"][0:16, 0:16], WTR[0:16, 16:17], start=True, stop=True))
        b.w("vector", sT, tmm)
        tv = b.v(lambda e: e.tensor_copy(WTR[0:16, 18:19], PS[5][0:16, 17:18]))
        b.w("tensor", sV, tv)
        tmm = b.t(lambda e: e.transpose(PS[5][0:1, 19:35], WTR[0:16, 18:19], S["cIDENT"][0:16, 0:16]))
        b.w("vector", sT, tmm)
        tv = b.v(lambda e: e.tensor_copy(WTR[0:1, 20:36], PS[5][0:1, 19:35]))
        b.w("tensor", sV, tv)
        b.t(lambda e: e.matmul(PS[4][:, 0:16], S["cONESROW"][:, :], WTR[0:1, 20:36], start=False, stop=True))
        tmm = sT.n
        b.w("vector", sT, tmm)
        tv_cc = b.v(lambda e: e.tensor_copy(ccum[:, :], PS[4][:, 0:16]))
        c4v = cols5.rearrange("p (t f) -> p t f", f=5)
        b.v(lambda e: e.tensor_copy(c4v[:, :, 0], S["cARANGE"][:, :]))
        b.v(lambda e: e.tensor_copy(c4v[:, :, 1], TMP[:, :]))
        b.v(lambda e: e.tensor_copy(c4v[:, :, 2], stro[:, :]))
        b.v(lambda e: e.tensor_copy(c4v[:, :, 3], b2c[:, :]))
        b.v(lambda e: e.tensor_copy(c4v[:, :, 4], S["cONES128"][:, 0:NT]))
        ev_st = {}
        for j in range(NT):
            st = ST2[j % 2]
            if j >= 2:
                b.w("vector", sT, ev_st[j - 2])
            b.v(lambda e, st=st, j=j: e.tensor_scalar(
                st[:, :], S["cIOTAR"][:, :], ccum[:, j:j + 1], None, ALU.is_equal))
            tv = b.v(lambda e, st=st, j=j: e.tensor_scalar(
                st[:, :], st[:, :], selm[:, j:j + 1], None, ALU.mult))
            b.w("tensor", sV, tv)
            for rb in range(2):
                b.t(lambda e, st=st, rb=rb, j=j: e.matmul(
                    PS[2][:, 5 * rb:5 * rb + 5], st[:, rb * 128:(rb + 1) * 128],
                    cols5[:, j * 5:(j + 1) * 5], start=(j == 0 and rb == 0),
                    stop=(j == NT - 1 and rb == 1)))
            ev_st[j] = sT.n
        b.w("vector", sT, sT.n)
        tv = b.v(lambda e: e.tensor_copy(selcols[:, :], PS[2][:, 0:10]))
        b.w("vector", sV, tv)
        for rb in range(2):
            b.v(lambda e, rb=rb, bi=bi: e.tensor_scalar(
                AC[:, 16 + rb:17 + rb], selcols[:, 5 * rb:5 * rb + 1], float(bi * T), None, ALU.add))
            b.v(lambda e, rb=rb: e.tensor_scalar(
                AC[:, 18 + rb:19 + rb], selcols[:, 5 * rb + 4:5 * rb + 5], 1.0, -4096.0, ALU.subtract, ALU.mult))
            b.v(lambda e, rb=rb: e.tensor_add(AC[:, 18 + rb:19 + rb], AC[:, 18 + rb:19 + rb], AC[:, 16 + rb:17 + rb]))
            b.v(lambda e, rb=rb: e.tensor_copy(selidx[:, rb:rb + 1], AC[:, 16 + rb:17 + rb]))
        tv_idx = b.v(lambda e: e.tensor_copy(selidx[:, 2:4], AC[:, 18:20]))
        b.v(lambda e: e.tensor_mul(AC[:, 0:NT], selm[:, :], stro[:, :]))
        b.v(lambda e: e.tensor_sub(AC[:, 0:NT], stro[:, :], AC[:, 0:NT]))
        b.v(lambda e: e.tensor_scalar(AC[:, 0:NT], AC[:, 0:NT], 1.0, None, ALU.add))
        b.v(lambda e: e.tensor_mul(BC[:, :], selm[:, :], b2c[:, :]))
        tv_ab = b.v(lambda e: e.tensor_sub(BC[:, :], b2c[:, :], BC[:, :]))
        b.w("scalar", sV, tv_ab)
        st_prev = {}
        t_store = None
        for t in range(NT):
            bb = BASE2[t % 2]
            if t >= 2:
                b.w("scalar", qS, st_prev[t - 2])
            ts_ = b.s(lambda e, t=t, bb=bb: e.activation(
                bb[:, :], x_tok[:, t * D:(t + 1) * D], AF.Identity,
                bias=BC[:, t:t + 1], scale=AC[:, t:t + 1]))
            b.w("sync", sS, ts_)
            t_store = b.dma("sync", lambda e, t=t, bb=bb, bi=bi: e.dma_start(
                out=out_t[bi, t * 128:(t + 1) * 128, :], in_=bb[:, :]))
            st_prev[t] = t_store
        b.w("gpsimd", qS, t_store)
        b.w("gpsimd", sV, tv_idx)
        b.w("gpsimd", qG, t_outinit)
        out_flat = out_t.rearrange("b t d -> (b t) d")
        for rb in range(2):
            b.emit("gpsimd", lambda e, rb=rb, out_flat=out_flat: e.indirect_dma_start(
                out=XRS[:, rb * D:(rb + 1) * D], out_offset=None,
                in_=out_flat, in_offset=IndirectOffsetOnAxis(ap=selidx[:, rb:rb + 1], axis=0),
            ), qG, 16)
            b.emit("gpsimd", lambda e, rb=rb: e.indirect_dma_start(
                out=CSG[:, rb * RS:(rb + 1) * RS], out_offset=None,
                in_=dram["CSTAB"][:, :], in_offset=IndirectOffsetOnAxis(ap=selidx[:, rb:rb + 1], axis=0),
            ), qG, 16)
        b.fence()
        tap("w%d" % bi, TMP[:, :])
        tap("selm%d" % bi, selm[:, :])
        tap("ccum%d" % bi, ccum[:, :])
        tap("selcols%d" % bi, selcols[:, :])
        tap("xrs%d" % bi, XRS[:, :])

    fmS = Scope()

    def phaseFmain(bi, xload_next):
        m = fmS
        XLN = m.g("xln", 2 * D)
        HTOK = m.g("htok", 2 * D)
        HT = m.g("hT", 2048)
        QT = m.g("qT", 2048)
        KT = m.g("kT", 2048)
        VT_ = m.g("vT", 2048)
        CTX = m.g("ctxT", 2048)
        COS2 = m.g("cos2", RS)
        SIN2 = m.g("sin2", RS)
        ASB = m.g("asb", RS)
        ATB = m.g("atb", 2 * RS)
        VTOK = m.g("vtok", 2 * 128)
        H1G = m.g("h1g", 4 * RS)
        WBUF = [m.g("wb0", D), m.g("wb1", D)]
        WBUF2 = [m.g("w20", D), m.g("w21", D)]
        RCOL = m.g("rcol", 8)
        FIN = m.g("fin", D)
        DUMP.update(dict(xln=XLN, htok=HTOK, hT=HT, qT=QT, kT=KT, vT=VT_,
                         ctx=CTX, cos2=COS2, sin2=SIN2, h1g=H1G, fin=FIN, rcol=RCOL))
        if xload_next is not None:
            b.w("sync", sS, sS.n)
            t_xn = b.dma("sync", lambda e, xload_next=xload_next: e.dma_start(
                out=x_tok.rearrange("q (n d) -> q n d", d=D),
                in_=x_in[xload_next].rearrange("(n p) d -> p n d", p=128)))
        else:
            t_xn = None
        for rb in range(2):
            b.s(lambda e, rb=rb: e.activation(
                XLN[:, rb * D:(rb + 1) * D], XRS[:, rb * D:(rb + 1) * D], AF.Identity,
                scale=selcols[:, 5 * rb + 2:5 * rb + 3], bias=selcols[:, 5 * rb + 3:5 * rb + 4]))
        for rb in range(2):
            b.s(lambda e, rb=rb: e.activation(HTOK[:, rb * D:(rb + 1) * D], XLN[:, rb * D:(rb + 1) * D],
                                              AF.Square, accum_out=RCOL[:, rb:rb + 1]))
        ts_ = b.s(lambda e: e.activation(RCOL[:, 2:4], RCOL[:, 0:2], AF.Sqrt, scale=epsc[:, 2:3], bias=epsc[:, 1:2]))
        b.w("vector", sS, ts_)
        b.v(lambda e: e.reciprocal(RCOL[:, 2:4], RCOL[:, 2:4]))
        for rb in range(2):
            b.v(lambda e, rb=rb: e.tensor_scalar(
                HTOK[:, rb * D:(rb + 1) * D], XLN[:, rb * D:(rb + 1) * D],
                RCOL[:, 2 + rb:3 + rb], None, ALU.mult))
        tv_hn = sV.n

        def tok2feat(src, dst, scale_cols, tv_ready):
            b.w("tensor", sV, tv_ready)
            evs = {}
            banks = (PS[2], PS[3], PS[4], PS[5])
            for dc in range(ND):
                bank = banks[dc % 4]
                if dc >= 4:
                    b.w("tensor", sV, evs[dc - 4])
                for rb in range(2):
                    b.t(lambda e, dc=dc, rb=rb, bank=bank, src=src: e.transpose(
                        bank[:, rb * 128:(rb + 1) * 128],
                        src[:, rb * D + dc * 128:rb * D + (dc + 1) * 128], S["cIDENT"][:, :]))
                tmm = sT.n
                b.w("vector", sT, tmm)
                if scale_cols is not None:
                    evs[dc] = b.v(lambda e, dc=dc, bank=bank, dst=dst, sc=scale_cols:
                                  e.tensor_scalar(dst[:, dc * RS:(dc + 1) * RS],
                                                  bank[:, 0:256], sc[:, dc:dc + 1], None, ALU.mult))
                else:
                    evs[dc] = b.v(lambda e, dc=dc, bank=bank, dst=dst:
                                  e.tensor_copy(dst[:, dc * RS:(dc + 1) * RS], bank[:, 0:256]))
            b.w("tensor", sV, sV.n)

        tok2feat(HTOK, HT, ln1sb, tv_hn)
        if bi == 0:
            tap("hnT", HT[:, :])
        for rb in range(2):
            for which, dstb in ((0, COS2), (1, SIN2)):
                ps = PS[4][:, 0:128]
                b.t(lambda e, rb=rb, which=which, ps=ps: e.transpose(
                    ps, CSG[:, rb * RS + which * 128:rb * RS + (which + 1) * 128], S["cIDENT"][:, :]))
                tmm = sT.n
                b.w("vector", sT, tmm)
                tv = b.v(lambda e, rb=rb, ps=ps, dstb=dstb: e.tensor_copy(
                    dstb[:, rb * 128:(rb + 1) * 128], ps))
                b.w("tensor", sV, tv)

        def proj(wname, dst, rope):
            mm_done = {}
            b.w("sync", sT, sT.n)
            for k in range(ND):
                if k >= 2:
                    b.w("sync", sT, mm_done[k - 2])
                tl = b.dma("sync", lambda e, wname=wname, k=k: e.dma_start(
                    out=WBUF[k % 2][:, :], in_=dram[wname][k * 128:(k + 1) * 128, :]))
                b.w("tensor", qS, tl)
                for mt in range(ND):
                    ps = PS[mt // 2][:, (mt % 2) * 256:(mt % 2) * 256 + 256]
                    b.t(lambda e, k=k, mt=mt, ps=ps: e.matmul(
                        ps, WBUF[k % 2][:, mt * 128:(mt + 1) * 128], HT[:, k * RS:(k + 1) * RS],
                        start=(k == 0 and mt % 2 == 0), stop=(k == ND - 1 and mt % 2 == 1)))
                mm_done[k] = sT.n
            b.w("vector", sT, sT.n)
            for mt in range(ND):
                ps = PS[mt // 2][:, (mt % 2) * 256:(mt % 2) * 256 + 256]
                b.v(lambda e, mt=mt, ps=ps, dst=dst: e.tensor_copy(dst[:, mt * RS:(mt + 1) * RS], ps))
            b.w("tensor", sV, sV.n)
            if rope:
                rope_ev = {}
                for mt in range(ND):
                    ps = (PS[6] if mt % 2 == 0 else PS[7])[:, 0:256]
                    if mt >= 2:
                        b.w("tensor", sV, rope_ev[mt - 2])
                    b.t(lambda e, mt=mt, ps=ps, dst=dst: e.matmul(
                        ps, S["cROTT"][:, :], dst[:, mt * RS:(mt + 1) * RS], start=True, stop=True))
                    tmm = sT.n
                    b.w("vector", sT, tmm)
                    b.v(lambda e, mt=mt, dst=dst: e.tensor_mul(
                        ASB[:, :], dst[:, mt * RS:(mt + 1) * RS], COS2[:, :]))
                    b.v(lambda e, mt=mt, ps=ps: e.tensor_mul(ATB[:, 0:RS], ps, SIN2[:, :]))
                    rope_ev[mt] = b.v(lambda e, mt=mt, dst=dst: e.tensor_add(
                        dst[:, mt * RS:(mt + 1) * RS], ASB[:, :], ATB[:, 0:RS]))
                b.w("tensor", sV, sV.n)

        if bi == 0:
            tap("xlnF", XLN[:, :])
        proj("WQ", QT, True)
        if bi == 0:
            tap("qTrope", QT[:, :])
            tap("cos2", COS2[:, :])
            tap("sin2", SIN2[:, :])
        proj("WK", KT, True)
        if bi == 0:
            tap("qT_afterK", QT[:, :])
        proj("WV", VT_, False)
        if bi == 0:
            tap("qT_afterV", QT[:, :])
        if bi == 0:
            tap("vT", VT_[:, :])
        for h in range(H):
            mt, po = h // 2, 64 * (h % 2)
            if bi == 0 and h in (0, 2, 4, 6):
                tap("qT_at%d" % h, QT[:, :])
            if h % 2 == 0:
                for kc in range(2):
                    ps = PS[5][:, kc * 128:(kc + 1) * 128]
                    b.t(lambda e, mt=mt, kc=kc, ps=ps: e.transpose(
                        ps, VT_[:, mt * RS + kc * 128:mt * RS + (kc + 1) * 128], S["cIDENT"][:, :]))
                tmm = sT.n
                b.w("vector", sT, tmm)
                b.v(lambda e: e.tensor_copy(VTOK[:, 0:128], PS[5][:, 0:128]))
                tv = b.v(lambda e: e.tensor_copy(VTOK[:, 128:256], PS[5][:, 128:256]))
                b.w("tensor", sV, tv)
            for qc in range(2):
                ps = (PS[4] if qc == 0 else PS[7])[:, 0:256]
                b.t(lambda e, mt=mt, po=po, qc=qc, ps=ps: e.matmul(
                    ps, QT[po:po + 64, mt * RS + qc * 128:mt * RS + (qc + 1) * 128],
                    KT[po:po + 64, mt * RS:(mt + 1) * RS], start=True, stop=True))
                tmm = sT.n
                b.w("vector", sT, tmm)
                b.v(lambda e, qc=qc, ps=ps: e.scalar_tensor_tensor(
                    ASB[:, :], ps, 0.125, S["cCAUSAL"][:, qc * RS:(qc + 1) * RS], ALU.mult, ALU.add))
                tv = b.v(lambda e: e.reduce_max(RCOL[:, 6:7], ASB[:, :], axis=AXX, negate=True))
                b.w("scalar", sV, tv)
                ts_ = b.s(lambda e: e.activation(ASB[:, :], ASB[:, :], AF.Exp, bias=RCOL[:, 6:7]))
                b.w("vector", sS, ts_)
                b.v(lambda e: e.reduce_sum(RCOL[:, 7:8], ASB[:, :], axis=AXX))
                b.v(lambda e: e.reciprocal(RCOL[:, 7:8], RCOL[:, 7:8]))
                tv = b.v(lambda e: e.tensor_scalar(ASB[:, :], ASB[:, :], RCOL[:, 7:8], None, ALU.mult))
                if bi == 0 and h in (0, 7) and qc == 0:
                    tap("asb_h%d" % h, ASB[:, :])
                b.w("tensor", sV, tv)
                for kc in range(2):
                    ps2 = PS[5][:, 256:384]
                    b.t(lambda e, kc=kc, ps2=ps2: e.transpose(ps2, ASB[:, kc * 128:(kc + 1) * 128], S["cIDENT"][:, :]))
                    tmm = sT.n
                    b.w("vector", sT, tmm)
                    tv = b.v(lambda e, kc=kc, qc=qc, ps2=ps2: e.tensor_copy(
                        ATB[:, kc * RS + qc * 128:kc * RS + (qc + 1) * 128], ps2))
                    b.w("tensor", sV, tv)
            for kc in range(2):
                if po == 0:
                    b.t(lambda e, kc=kc: e.matmul(
                        PS[6][0:64, 0:256], VTOK[:, kc * 128:kc * 128 + 64],
                        ATB[:, kc * RS:(kc + 1) * RS], start=(kc == 0), stop=(kc == 1)))
                else:
                    b.t(lambda e, kc=kc: e.matmul(
                        PS[6][64:128, 0:256], VTOK[:, kc * 128 + 64:kc * 128 + 128],
                        ATB[:, kc * RS:(kc + 1) * RS], start=(kc == 0), stop=(kc == 1),
                        tile_position=(0, 64)))
            tmm = sT.n
            b.w("vector", sT, tmm)
            tv = b.v(lambda e, mt=mt, po=po: e.tensor_copy(
                CTX[po:po + 64, mt * RS:(mt + 1) * RS], PS[6][po:po + 64, 0:256]))
            if bi == 0 and h in (0, 7):
                tap("atb_h%d" % h, ATB[:, :])
            b.w("tensor", sV, tv)
        if bi == 0:
            tap("ctxT", CTX[:, :])
        mm_done = {}
        b.w("sync", sT, sT.n)
        for k in range(ND):
            if k >= 2:
                b.w("sync", sT, mm_done[k - 2])
            tl = b.dma("sync", lambda e, k=k: e.dma_start(
                out=WBUF[k % 2][:, :], in_=dram["WOf"][k * 128:(k + 1) * 128, :]))
            b.w("tensor", qS, tl)
            for mt in range(ND):
                ps = PS[mt // 2][:, (mt % 2) * 256:(mt % 2) * 256 + 256]
                b.t(lambda e, k=k, mt=mt, ps=ps: e.matmul(
                    ps, WBUF[k % 2][:, mt * 128:(mt + 1) * 128], CTX[:, k * RS:(k + 1) * RS],
                    start=(k == 0 and mt % 2 == 0), stop=(k == ND - 1 and mt % 2 == 1)))
            mm_done[k] = sT.n
        b.w("vector", sT, sT.n)
        for mt in range(ND):
            ps = PS[mt // 2][:, (mt % 2) * 256:(mt % 2) * 256 + 256]
            b.v(lambda e, mt=mt, ps=ps: e.tensor_copy(KT[:, mt * RS:(mt + 1) * RS], ps))
        b.w("tensor", sV, sV.n)
        evs = {}
        banks = (PS[2], PS[3], PS[4], PS[5])
        XLNV = XLN.rearrange("p (r x) -> p r x", x=D)
        for dc in range(ND):
            bank = banks[dc % 4]
            if dc >= 4:
                b.w("tensor", sV, evs[dc - 4])
            for rb in range(2):
                b.t(lambda e, dc=dc, rb=rb, bank=bank: e.transpose(
                    bank[:, rb * 128:(rb + 1) * 128],
                    KT[:, dc * RS + rb * 128:dc * RS + (rb + 1) * 128], S["cIDENT"][:, :]))
            tmm = sT.n
            b.w("vector", sT, tmm)
            evs[dc] = b.v(lambda e, dc=dc, bank=bank: e.tensor_add(
                XLNV[:, :, dc * 128:(dc + 1) * 128],
                XLNV[:, :, dc * 128:(dc + 1) * 128], bank[:, 0:256]))
        if bi == 0:
            tap("x1tok", XLN[:, :])
        b.w("scalar", sV, sV.n)
        for rb in range(2):
            b.s(lambda e, rb=rb: e.activation(HTOK[:, rb * D:(rb + 1) * D], XLN[:, rb * D:(rb + 1) * D],
                                              AF.Square, accum_out=RCOL[:, 4 + rb:5 + rb]))
        ts_ = b.s(lambda e: e.activation(RCOL[:, 4:6], RCOL[:, 4:6], AF.Sqrt, scale=epsc[:, 2:3], bias=epsc[:, 1:2]))
        b.w("vector", sS, ts_)
        b.v(lambda e: e.reciprocal(RCOL[:, 4:6], RCOL[:, 4:6]))
        for rb in range(2):
            b.v(lambda e, rb=rb: e.tensor_scalar(
                HTOK[:, rb * D:(rb + 1) * D], XLN[:, rb * D:(rb + 1) * D],
                RCOL[:, 4 + rb:5 + rb], None, ALU.mult))
        tok2feat(HTOK, QT, ln2sb, sV.n)
        h1_done = {}
        last_w2_done = 0
        b.w("sync", sT, sT.n)
        b.w("scalar", sT, sT.n)
        for fc in range(NFC):
            for k in range(ND):
                if k >= 2:
                    b.w("sync", sT, h1_done.get((fc, k - 2), 0))
                elif fc > 0 and k == 0:
                    b.w("sync", sT, h1_done[(fc - 1, ND - 2)])
                elif fc > 0:
                    b.w("sync", sT, h1_done[(fc - 1, ND - 1)])
                tl = b.dma("sync", lambda e, fc=fc, k=k: e.dma_start(
                    out=WBUF[k % 2][:, 0:FC], in_=dram["W1"][k * 128:(k + 1) * 128, fc * FC:(fc + 1) * FC]))
                b.w("tensor", qS, tl)
                for mf in range(FC // 128):
                    ps = PS[4 + mf // 2][:, (mf % 2) * 256:(mf % 2) * 256 + 256]
                    b.t(lambda e, k=k, mf=mf, ps=ps: e.matmul(
                        ps, WBUF[k % 2][:, mf * 128:(mf + 1) * 128], QT[:, k * RS:(k + 1) * RS],
                        start=(k == 0 and mf % 2 == 0), stop=(k == ND - 1 and mf % 2 == 1)))
                h1_done[(fc, k)] = sT.n
            b.w("scalar", sT, sT.n)
            for mf in range(FC // 128):
                ps = PS[4 + mf // 2][:, (mf % 2) * 256:(mf % 2) * 256 + 256]
                b.s(lambda e, mf=mf, ps=ps: e.activation(H1G[:, mf * RS:(mf + 1) * RS], ps, AF.Gelu_apprx_tanh))
            b.w("tensor", sS, sS.n)
            w2_done = {}
            for kf in range(FC // 128):
                if kf >= 2:
                    b.w("scalar", sT, w2_done[kf - 2])
                elif last_w2_done:
                    b.w("scalar", sT, last_w2_done)
                tl2 = b.dma("scalar", lambda e, fc=fc, kf=kf: e.dma_start(
                    out=WBUF2[kf % 2][:, :], in_=dram["W2"][fc * FC + kf * 128:fc * FC + (kf + 1) * 128, :]))
                b.w("tensor", qA, tl2)
                for mt in range(ND):
                    ps = PS[mt // 2][:, (mt % 2) * 256:(mt % 2) * 256 + 256]
                    b.t(lambda e, kf=kf, mt=mt, ps=ps, fc=fc: e.matmul(
                        ps, WBUF2[kf % 2][:, mt * 128:(mt + 1) * 128], H1G[:, kf * RS:(kf + 1) * RS],
                        start=(fc == 0 and kf == 0 and mt % 2 == 0),
                        stop=(fc == NFC - 1 and kf == FC // 128 - 1 and mt % 2 == 1)))
                w2_done[kf] = sT.n
            last_w2_done = w2_done[FC // 128 - 1]
        b.w("vector", sT, sT.n)
        for mt in range(ND):
            ps = PS[mt // 2][:, (mt % 2) * 256:(mt % 2) * 256 + 256]
            b.v(lambda e, mt=mt, ps=ps: e.tensor_copy(KT[:, mt * RS:(mt + 1) * RS], ps))
        b.w("tensor", sV, sV.n)
        t_sc0 = None
        evs = {}
        fbanks = (PS[2], PS[3], PS[4], PS[5])
        for rb in range(2):
            for dc in range(ND):
                k = rb * ND + dc
                ps = fbanks[k % 4][:, 0:128]
                if k >= 4:
                    b.w("tensor", sV, evs[k - 4])
                b.t(lambda e, dc=dc, rb=rb, ps=ps: e.transpose(
                    ps, KT[:, dc * RS + rb * 128:dc * RS + (rb + 1) * 128], S["cIDENT"][:, :]))
                tmm = sT.n
                b.w("vector", sT, tmm)
                b.v(lambda e, dc=dc, rb=rb, ps=ps: e.tensor_add(
                    FIN[:, dc * 128:(dc + 1) * 128],
                    XLN[:, rb * D + dc * 128:rb * D + (dc + 1) * 128], ps))
                b.v(lambda e, dc=dc, rb=rb: e.tensor_scalar(
                    FIN[:, dc * 128:(dc + 1) * 128], FIN[:, dc * 128:(dc + 1) * 128],
                    selcols[:, 5 * rb + 1:5 * rb + 2], None, ALU.mult))
                evs[k] = b.v(lambda e, dc=dc, rb=rb: e.tensor_add(
                    FIN[:, dc * 128:(dc + 1) * 128], FIN[:, dc * 128:(dc + 1) * 128],
                    XRS[:, rb * D + dc * 128:rb * D + (dc + 1) * 128]))
            tvf = sV.n
            if bi == 0:
                tap("fin%d" % rb, FIN[:, :])
            b.w("gpsimd", sV, tvf)
            out_flat = out_t.rearrange("b t d -> (b t) d")
            t_sc = b.emit("gpsimd", lambda e, rb=rb, out_flat=out_flat: e.indirect_dma_start(
                out=out_flat, out_offset=IndirectOffsetOnAxis(ap=selidx[:, 2 + rb:3 + rb], axis=0),
                in_=FIN[:, :], in_offset=None,
                bounds_check=BPC * T - 1, oob_is_err=False,
            ), qG, 16)
            if rb == 0:
                t_sc0 = t_sc
                b.w("vector", qG, t_sc0)
        b.fence()
        return t_xn

    stages = ["pre", "A0", "F0", "M0", "A1", "F1", "M1"]
    upto = stages.index(stop_after) if stop_after in stages else len(stages)
    if upto >= 1:
        phaseA(0, t_x0)
    if upto >= 2:
        phaseFpre(0)
    if upto >= 3:
        t_x1 = phaseFmain(0, 1)
    if upto >= 4:
        phaseA(1, t_x1)
    if upto >= 5:
        phaseFpre(1)
    if upto >= 6:
        phaseFmain(1, None)
    DUMP.update(dict(x_tok=x_tok, selcols=selcols, selm=selm, stmu=stmu, stro=stro, wbias_rep=wbias_rep, negs=negs, orvd=orv, invc=invc, wbias_row=wbias_row, wrcd=wrc, srepd=srepc, abic=abic, atric=atric,
                     wraw=wraw, ccum=ccum, acat=acat, outb=outb, xrs=XRS, csg=CSG))
    emit_dumps()
    b.w("sync", qG, qG.n)
    b.w("sync", qA, qA.n)
    b.w("sync", qS, qS.n)

    b.finalize()
    return nc, dram


_NC_CACHE = {}


def get_nc(dbg_names=(), stop_after="all", end_dumps=()):
    key = (tuple(dbg_names), stop_after, tuple(end_dumps))
    if key not in _NC_CACHE:
        _NC_CACHE[key] = build_nc(dbg_names, stop_after, end_dumps)
    return _NC_CACHE[key]


def make_in_map(inputs, core):
    x = np.ascontiguousarray(np.asarray(inputs["L0_attn_x"], dtype=np.float32))
    base = {
        "WQa": inputs["L0_attn_WQ"], "WKa": inputs["L0_attn_WK"],
        "WOa": inputs["L0_attn_WO"], "WINV": inputs["L0_attn_Winv"],
        "UB": inputs["L0_attn_U_b"], "VB": inputs["L0_attn_V_b"], "WB": inputs["L0_attn_W_b"],
        "UT": inputs["L0_attn_U_t"], "VT": inputs["L0_attn_V_t"], "WT": inputs["L0_attn_W_t"],
        "XTm": inputs["L0_attn_X_t"],
        "BIASB": np.asarray(inputs["L0_attn_bias_b"]).reshape(D, 1),
        "BIAST": np.asarray(inputs["L0_attn_bias_t"]).reshape(D, 1),
        "ABI": np.asarray(inputs["L0_attn_alpha_bi"]).reshape(1, 1),
        "ATRI": np.asarray(inputs["L0_attn_alpha_tri"]).reshape(1, 1),
        "RW": np.asarray(inputs["L0_ffn_router_w"]).reshape(D, 1),
        "RB": np.asarray(inputs["L0_ffn_router_b"]).reshape(1, 1),
        "WQ": inputs["L0_ffn_wq"], "WK": inputs["L0_ffn_wk"], "WV": inputs["L0_ffn_wv"],
        "WOf": inputs["L0_ffn_wo"], "W1": inputs["L0_ffn_w1"], "W2": inputs["L0_ffn_w2"],
        "LN1": np.asarray(inputs["L0_ffn_ln1"]).reshape(D, 1),
        "LN2": np.asarray(inputs["L0_ffn_ln2"]).reshape(D, 1),
    }
    base = {k: np.ascontiguousarray(np.asarray(v, dtype=np.float32)) for k, v in base.items()}
    for k, v in CONSTS.items():
        base[k] = v
    base["x"] = np.ascontiguousarray(x[2 * core:2 * core + 2])
    return base


def kernel(**inputs):
    nc, _ = get_nc()
    in_maps = [make_in_map(inputs, c) for c in range(8)]
    res = run_bass_kernel_spmd(nc, in_maps, core_ids=list(range(8)))
    return np.concatenate([r["out"] for r in res.results], axis=0)


# version 04
# speedup vs baseline: 1.0031x; 1.0031x over previous
import numpy as np
import concourse.bass as bass
import concourse.mybir as mybir
from concourse.bass_utils import run_bass_kernel_spmd
from concourse.bass_core_rust import IndirectOffsetOnAxis

F32 = mybir.dt.float32
I32 = mybir.dt.int32
AF = mybir.ActivationFunctionType
ALU = mybir.AluOpType
AXX = mybir.AxisListType.X

T, D, R, H, HD = 2048, 1024, 64, 16, 64
NT, ND = T // 128, D // 128
KC, RS, FF, BPC = 255, 256, 4096, 2
FC = 512
NFC = FF // FC

ENGS = ("tensor", "vector", "scalar", "gpsimd", "sync")


def host_consts():
    f32 = np.float32
    c = {}
    c["IDENT"] = np.eye(128, dtype=f32)
    c["TRIT"] = np.triu(np.ones((128, 128), dtype=f32))
    c["TRIST"] = np.triu(np.ones((128, 128), dtype=f32), k=1)
    c["ONES128"] = np.ones((128, 128), dtype=f32)
    c["ONESROW"] = np.ones((1, 128), dtype=f32)
    c["ONESCOL"] = np.ones((128, 1), dtype=f32)
    c["IOTAR"] = np.broadcast_to(np.arange(RS, dtype=f32), (128, RS)).copy()
    c["ARANGE"] = np.arange(T, dtype=f32).reshape(NT, 128).T.copy()
    c["COUNTS"] = np.arange(1, T + 1, dtype=f32).reshape(NT, 128).T.copy()
    cb = np.zeros((128, 2 * RS), dtype=f32)
    for qt in range(2):
        q = 128 * qt + np.arange(128)[:, None]
        k = np.arange(RS)[None, :]
        cb[:, RS * qt:RS * qt + RS] = np.where(k <= q, 0.0, np.float32(-1e9))
    c["CAUSAL"] = cb
    r2 = np.zeros((128, 128), dtype=f32)
    for h0 in (0, 64):
        for i in range(32):
            r2[h0 + i, h0 + 32 + i] = -1.0
            r2[h0 + 32 + i, h0 + i] = 1.0
    c["ROTT"] = r2.T.copy()
    import jax
    import jax.numpy as jnp
    cpu = jax.devices("cpu")[0]
    with jax.default_device(cpu):
        invj = 1.0 / 10000.0 ** (jnp.arange(0, HD, 2, dtype=jnp.float32) / HD)
        posj = jnp.arange(T, dtype=jnp.int32).astype(jnp.float32)
        frj = posj[:, None] * invj[None, :]
        embj = jnp.concatenate([frj, frj], axis=-1)
        cosj, sinj = jnp.cos(embj), jnp.sin(embj)
    cosd = np.concatenate([np.asarray(cosj), np.asarray(cosj)], axis=1)
    sind = np.concatenate([np.asarray(sinj), np.asarray(sinj)], axis=1)
    tab = np.concatenate([cosd, sind], axis=1).astype(f32)
    c["CSTAB"] = np.concatenate([tab, tab], axis=0)
    return c


CONSTS = host_consts()
SB_CONSTS = [k for k in CONSTS if k != "CSTAB"]


class Sem:
    def __init__(self, h):
        self.h, self.n = h, 0

    def bump(self, k=1):
        self.n += k
        return self.n


class B:
    def __init__(self, nc):
        self.nc = nc
        self.plan = {e: [] for e in ENGS}
        self.sems = {e: Sem(nc.ctx.enter_context(nc.semaphore("s_" + e))) for e in ENGS}
        self.qsems = {e: Sem(nc.ctx.enter_context(nc.semaphore("q_" + e)))
                      for e in ("sync", "scalar", "gpsimd")}
        self._fw = {e: {} for e in ENGS}

    def emit(self, eng, fn, inc=None, inck=1):
        if eng in ("vector", "scalar") and inc is self.sems[eng] and inc.n > 0:
            self.plan[eng].append(lambda e, s=inc.h, t=inc.n: e.wait_ge(s, t))
        tgt = inc.bump(inck) if inc is not None else None
        if inc is not None:
            self.plan[eng].append(lambda e, fn=fn, s=inc.h, k=inck: fn(e).then_inc(s, k))
        else:
            self.plan[eng].append(lambda e, fn=fn: fn(e))
        return tgt

    def w(self, eng, sem, tgt):
        if tgt:
            self.plan[eng].append(lambda e, s=sem.h, t=tgt: e.wait_ge(s, t))

    def t(self, fn):
        return self.emit("tensor", fn, self.sems["tensor"])

    def v(self, fn):
        return self.emit("vector", fn, self.sems["vector"])

    def s(self, fn):
        return self.emit("scalar", fn, self.sems["scalar"])

    def dma(self, eng, fn):
        return self.emit(eng, fn, self.qsems[eng], 16)

    def fence(self):
        snap = list(self.sems.values()) + list(self.qsems.values())
        for e in ENGS:
            for s in snap:
                if s is self.sems[e]:
                    continue
                if s.n > self._fw[e].get(id(s), 0):
                    self.w(e, s, s.n)
                    self._fw[e][id(s)] = s.n

    def finalize(self):
        with self.nc.Block() as blk:
            for name in ENGS:
                getattr(blk, name)((lambda lst: (lambda e: [f(e) for f in lst]))(self.plan[name]))


def build_nc(dbg_names=(), stop_after="all", end_dumps=()):
    nc = bass.Bass("TRN2", target_bir_lowering=False, detect_race_conditions=False)
    dram = {}

    def din(name, shape):
        dram[name] = nc.dram_tensor(name, list(shape), F32, kind="ExternalInput").ap()
        return dram[name]

    x_in = din("x", (BPC, T, D))
    out_t = nc.dram_tensor("out", [BPC, T, D], F32, kind="ExternalOutput").ap()
    for nm, sh in (("WQa", (D, D)), ("WKa", (D, D)), ("WOa", (D, D)), ("WINV", (D, D)),
                   ("UB", (D, R)), ("VB", (D, R)), ("WB", (D, R)),
                   ("UT", (D, R)), ("VT", (D, R)), ("WT", (D, R)), ("XTm", (D, R)),
                   ("BIASB", (D, 1)), ("BIAST", (D, 1)), ("ABI", (1, 1)), ("ATRI", (1, 1)),
                   ("RW", (D, 1)), ("RB", (1, 1)),
                   ("WQ", (D, D)), ("WK", (D, D)), ("WV", (D, D)), ("WOf", (D, D)),
                   ("W1", (D, FF)), ("W2", (FF, D)), ("LN1", (D, 1)), ("LN2", (D, 1))):
        din(nm, sh)
    for k in CONSTS:
        din(k, CONSTS[k].shape)
    DUMP = {}
    dbg = {}
    for name, width in dbg_names:
        dbg[name] = nc.dram_tensor("dbg_" + name, [128, width], F32, kind="ExternalOutput").ap()

    S = {}

    def sb(name, shape, dtype=F32):
        S[name] = nc.alloc_sbuf_tensor(name, list(shape), dtype).ap()
        return S[name]

    x_tok = sb("x_tok", (128, NT * D))
    acat = sb("acat", (128, ND * 448))
    outb = sb("outb", (128, D))
    wbias_row = sb("wbias_row", (1, D))
    wbias_rep = sb("wbias_rep", (128, D))
    for k in SB_CONSTS:
        sb("c" + k, CONSTS[k].shape)
    invc = sb("invc", (128, NT))
    invc2 = sb("invc2", (128, NT))
    abic = sb("abic", (128, 1))
    atric = sb("atric", (128, 1))
    orv = sb("orv", (128, 1))
    wrc = sb("wrc", (128, 1))
    srepc = sb("srepc", (128, 1))
    rbc = sb("rbc", (128, 1))
    rwsb = sb("rwsb", (128, ND))
    ln1sb = sb("ln1sb", (128, ND))
    ln2sb = sb("ln2sb", (128, ND))
    negs = sb("negs", (1, 448))
    scal = sb("scal", (1, 8))
    wraw = sb("wraw", (128, NT))
    stmu = sb("stmu", (128, NT))
    stro = sb("stro", (128, NT))
    b2c = sb("b2c", (128, NT))
    selm = sb("selm", (128, NT))
    ccum = sb("ccum", (128, NT))
    cols5 = sb("cols5", (128, 5 * NT))
    selcols = sb("selcols", (128, 10))
    selidx = sb("selidx", (128, 4), I32)
    carry = sb("carry", (128, 256))
    epsc = sb("epsc", (128, 4))
    XRS = sb("xrs", (128, 2 * D))
    CSG = sb("csg", (128, 2 * RS))
    ARW = 24000
    arena = sb("arena", (128, ARW))
    PS = [nc.alloc_psum_tensor("pb%d" % i, [128, 512], F32).ap() for i in range(8)]

    b = B(nc)
    sT, sV, sS, sG = (b.sems[e] for e in ("tensor", "vector", "scalar", "gpsimd"))
    qS, qA, qG = (b.qsems[e] for e in ("sync", "scalar", "gpsimd"))

    class Scope:
        def __init__(self):
            self.off, self.map = 0, {}

        def g(self, name, width):
            if name not in self.map:
                assert self.off + width <= ARW, (name, self.off, width)
                self.map[name] = (self.off, width)
                self.off += width
            o, wd = self.map[name]
            assert wd == width
            return arena[:, o:o + wd]

        def row(self, name, width):
            return self.g(name, width)[0:1, :]

    def tap(name, ap):
        if name in dbg:
            for sem in (sT, sV, sS, sG):
                b.w("sync", sem, sem.n)
            b.dma("sync", lambda e, ap=ap, name=name: e.dma_start(
                out=dbg[name][:ap.shape[0], :ap.shape[1]], in_=ap))

    for k in SB_CONSTS:
        b.dma("sync", lambda e, k=k: e.dma_start(out=S["c" + k][:], in_=dram[k][:, :]))
    for c in range(ND):
        b.dma("sync", lambda e, c=c: e.dma_start(out=rwsb[:, c:c + 1], in_=dram["RW"][c * 128:(c + 1) * 128, :]))
        b.dma("sync", lambda e, c=c: e.dma_start(out=ln1sb[:, c:c + 1], in_=dram["LN1"][c * 128:(c + 1) * 128, :]))
        b.dma("sync", lambda e, c=c: e.dma_start(out=ln2sb[:, c:c + 1], in_=dram["LN2"][c * 128:(c + 1) * 128, :]))
    b.dma("sync", lambda e: e.dma_start(out=scal[:, 0:1], in_=dram["ABI"][:, :]))
    b.dma("sync", lambda e: e.dma_start(out=scal[:, 1:2], in_=dram["ATRI"][:, :]))
    t_sc3 = b.dma("sync", lambda e: e.dma_start(out=scal[:, 2:3], in_=dram["RB"][:, :]))
    t_x0 = b.dma("sync", lambda e: e.dma_start(
        out=x_tok.rearrange("q (n d) -> q n d", d=D),
        in_=x_in[0].rearrange("(n p) d -> p n d", p=128)))
    t_outinit = b.dma("gpsimd", lambda e: e.dma_start(out=out_t[1], in_=x_in[1]))
    b.v(lambda e: e.memset(epsc[:, 0:1], 1e-5))
    b.v(lambda e: e.memset(epsc[:, 1:2], 1e-6))
    b.v(lambda e: e.memset(epsc[:, 2:3], 1.0 / D))

    pre = Scope()
    SW = {}
    for nm in ("UB", "VB", "WB", "UT", "VT", "WT", "XTm"):
        SW[nm] = pre.g("sw_" + nm, ND * R)
        for c in range(ND):
            b.dma("sync", lambda e, nm=nm, c=c: e.dma_start(
                out=SW[nm][:, c * R:(c + 1) * R], in_=dram[nm][c * 128:(c + 1) * 128, :]))
    bbcol = pre.g("biasb", ND)
    btcol = pre.g("biast", ND)
    for c in range(ND):
        b.dma("sync", lambda e, c=c: e.dma_start(out=bbcol[:, c:c + 1], in_=dram["BIASB"][c * 128:(c + 1) * 128, :]))
        b.dma("sync", lambda e, c=c: e.dma_start(out=btcol[:, c:c + 1], in_=dram["BIAST"][c * 128:(c + 1) * 128, :]))
    BW = pre.g("bigw", ND * D)
    WOT = pre.g("wot", ND * D)
    VB2 = pre.g("vb2", ND * R)
    WB2 = pre.g("wb2", ND * R)
    BCOMB = pre.g("bcomb", ND)
    WORC = pre.g("worc", ND)
    WBIC = pre.g("wbic", ND)
    DUMP["wbic"] = WBIC
    DUMP["bcomb"] = BCOMB
    DUMP["bbcol"] = bbcol
    DUMP["btcol"] = btcol
    DUMP["rwsb"] = rwsb
    DUMP["ln1sb"] = ln1sb
    t_pre_loads = qS.n

    _stopped = []

    def emit_dumps():
        for nm in end_dumps:
            ap = DUMP[nm]
            dd = nc.dram_tensor("dump_" + nm, [ap.shape[0], ap.shape[1]], F32,
                                kind="ExternalOutput").ap()
            b.dma("sync", lambda e, ap=ap, dd=dd: e.dma_start(out=dd[:, :], in_=ap))
        b.w("sync", qS, qS.n)

    def maybe_stop(tag):
        if stop_after == tag:
            emit_dumps()
            b.w("sync", qG, qG.n)
            b.finalize()
            _stopped.append(True)
        return bool(_stopped)

    if maybe_stop("load"):
        return nc, dram
    b.w("tensor", qS, t_sc3)
    b.t(lambda e: e.matmul(PS[0][:, 0:1], S["cONESROW"][:, :], scal[0:1, 0:1], start=True, stop=True))
    b.t(lambda e: e.matmul(PS[0][:, 1:2], S["cONESROW"][:, :], scal[0:1, 1:2], start=True, stop=True))
    tm = b.t(lambda e: e.matmul(PS[0][:, 2:3], S["cONESROW"][:, :], scal[0:1, 2:3], start=True, stop=True))
    b.w("vector", sT, tm)
    b.v(lambda e: e.tensor_copy(abic[:], PS[0][:, 0:1]))
    b.v(lambda e: e.tensor_copy(atric[:], PS[0][:, 1:2]))
    b.v(lambda e: e.tensor_copy(rbc[:], PS[0][:, 2:3]))
    b.w("vector", qS, t_pre_loads)
    b.v(lambda e: e.tensor_scalar(BCOMB[:, :], bbcol[:, :], abic[:, 0:1], None, ALU.mult))
    b.v(lambda e: e.tensor_add(BCOMB[:, :], BCOMB[:, :], bbcol[:, :]))
    b.v(lambda e: e.scalar_tensor_tensor(BCOMB[:, :], btcol[:, :], atric[:, 0:1], BCOMB[:, :], ALU.mult, ALU.add))
    b.v(lambda e: e.reciprocal(invc[:], S["cCOUNTS"][:, :]))
    tv = b.v(lambda e: e.tensor_mul(invc2[:], invc[:], invc[:]))
    b.w("tensor", sV, tv)

    if maybe_stop("p1"):
        return nc, dram

    def ld_bigw(nm):
        t = None
        b.w("sync", sT, sT.n)
        for c in range(ND):
            t = b.dma("sync", lambda e, nm=nm, c=c: e.dma_start(
                out=BW[:, c * D:(c + 1) * D], in_=dram[nm][c * 128:(c + 1) * 128, :]))
        return t

    def wsmall(dst_fn, vsm, scale_col=None, tilepos=None):
        ev = {}
        for m in range(ND):
            slot = (PS[1] if m % 2 == 0 else PS[6])[:, 0:R]
            oslot = slot if tilepos is None else slot[64:128, :]
            if m >= 2:
                b.w("tensor", sV, ev[m - 2])
            for j in range(ND):
                b.t(lambda e, m=m, j=j, oslot=oslot: e.matmul(
                    oslot, BW[:, j * D + m * 128:j * D + (m + 1) * 128],
                    vsm[:, j * R:(j + 1) * R], start=(j == 0), stop=(j == ND - 1),
                    tile_position=tilepos))
            tmm = sT.n
            b.w("vector", sT, tmm)
            dst = dst_fn(m)
            if scale_col is None:
                ev[m] = b.v(lambda e, dst=dst, oslot=oslot: e.tensor_copy(dst, oslot))
            else:
                ev[m] = b.v(lambda e, dst=dst, oslot=oslot, sc=scale_col: e.tensor_scalar(
                    dst, oslot, sc, None, ALU.mult))
        b.w("tensor", sV, sV.n)

    tw = ld_bigw("WINV")
    b.w("tensor", qS, tw)
    wsmall(lambda m: VB2[:, m * R:(m + 1) * R], SW["VB"])
    wsmall(lambda m: WB2[:, m * R:(m + 1) * R], SW["WB"])
    if maybe_stop("p2"):
        return nc, dram
    tw = ld_bigw("WQa")
    b.w("tensor", qS, tw)
    wsmall(lambda m: acat[:, 448 * m + 0:448 * m + 64], SW["VB"])
    wsmall(lambda m: acat[:, 448 * m + 64:448 * m + 128], WB2, scale_col=abic[:, 0:1])
    wsmall(lambda m: acat[:, 448 * m + 128:448 * m + 192], SW["VT"])
    tw = ld_bigw("WKa")
    b.w("tensor", qS, tw)
    wsmall(lambda m: acat[:, 448 * m + 192:448 * m + 256], SW["WB"])
    wsmall(lambda m: acat[:, 448 * m + 256:448 * m + 320], VB2)
    wsmall(lambda m: acat[:, 448 * m + 320:448 * m + 384], SW["WT"])
    for m in range(ND):
        b.v(lambda e, m=m: e.tensor_copy(acat[:, 448 * m + 384:448 * m + 448],
                                         SW["XTm"][:, m * R:(m + 1) * R]))
    if maybe_stop("p3"):
        return nc, dram
    tw = ld_bigw("WOa")
    b.w("tensor", qS, tw)
    ev = {}
    for m in range(ND):
        slot = (PS[1] if m % 2 == 0 else PS[6])[:, 128:129]
        if m >= 2:
            b.w("tensor", sV, ev[m - 2])
        for k in range(ND):
            b.t(lambda e, m=m, k=k, slot=slot: e.matmul(
                slot, BW[:, k * D + m * 128:k * D + (m + 1) * 128],
                rwsb[:, k:k + 1], start=(k == 0), stop=(k == ND - 1)))
        tmm = sT.n
        b.w("vector", sT, tmm)
        ev[m] = b.v(lambda e, m=m, slot=slot: e.tensor_copy(WORC[:, m:m + 1], slot))
    b.w("tensor", sV, sV.n)
    if maybe_stop("p3a"):
        return nc, dram
    WOTV = WOT.rearrange("p (j x) -> p j x", x=D)
    ev = {}
    for c in range(ND):
        bankA = PS[2] if c % 2 == 0 else PS[4]
        bankB = PS[3] if c % 2 == 0 else PS[5]
        if c >= 2:
            b.w("tensor", sV, ev[c - 2])
        for j in range(ND):
            ps = (bankA if j < 4 else bankB)[:, (j % 4) * 128:(j % 4) * 128 + 128]
            b.t(lambda e, c=c, j=j, ps=ps: e.transpose(
                ps, BW[:, c * D + j * 128:c * D + (j + 1) * 128], S["cIDENT"][:, :]))
        tmm = sT.n
        b.w("vector", sT, tmm)
        b.v(lambda e, c=c, bankA=bankA: e.tensor_copy(
            WOTV[:, 0:4, c * 128:(c + 1) * 128], bankA[:, 0:512]))
        ev[c] = b.v(lambda e, c=c, bankB=bankB: e.tensor_copy(
            WOTV[:, 4:8, c * 128:(c + 1) * 128], bankB[:, 0:512]))
    b.w("tensor", sV, sV.n)
    if isinstance(stop_after, str) and stop_after.startswith("wot"):
        emit_dumps()
        b.w("sync", qG, qG.n)
        b.finalize()
        return nc, dram
    if maybe_stop("p4"):
        return nc, dram
    for n2 in range(2):
        slot = (PS[3] if n2 == 0 else PS[4])[0:64, 0:512]
        for j in range(ND):
            b.t(lambda e, n2=n2, j=j, slot=slot: e.matmul(
                slot, SW["UB"][:, j * R:(j + 1) * R],
                WOT[:, j * D + n2 * 512:j * D + (n2 + 1) * 512],
                start=(j == 0), stop=(j == ND - 1)))
        tmm = sT.n
        b.w("vector", sT, tmm)
        b.v(lambda e, n2=n2, slot=slot: e.tensor_copy(outb[0:64, n2 * 512:(n2 + 1) * 512], slot))
    for n2 in range(2):
        slot = (PS[5] if n2 == 0 else PS[6])[64:128, 0:512]
        for j in range(ND):
            b.t(lambda e, n2=n2, j=j, slot=slot: e.matmul(
                slot, SW["UT"][:, j * R:(j + 1) * R],
                WOT[:, j * D + n2 * 512:j * D + (n2 + 1) * 512],
                start=(j == 0), stop=(j == ND - 1), tile_position=(0, 64)))
        tmm = sT.n
        b.w("vector", sT, tmm)
        b.v(lambda e, n2=n2, slot=slot: e.tensor_scalar(
            outb[64:128, n2 * 512:(n2 + 1) * 512], slot, atric[64:128, 0:1], None, ALU.mult))
    b.w("tensor", sV, sV.n)
    slot = PS[1][0:64, 130:131]
    for j in range(ND):
        b.t(lambda e, j=j, slot=slot: e.matmul(
            slot, SW["UB"][:, j * R:(j + 1) * R], WORC[:, j:j + 1],
            start=(j == 0), stop=(j == ND - 1)))
    tmm = sT.n
    b.w("vector", sT, tmm)
    b.v(lambda e, slot=slot: e.tensor_copy(orv[0:64, 0:1], slot))
    slot = PS[6][64:128, 132:133]
    for j in range(ND):
        b.t(lambda e, j=j, slot=slot: e.matmul(
            slot, SW["UT"][:, j * R:(j + 1) * R], WORC[:, j:j + 1],
            start=(j == 0), stop=(j == ND - 1), tile_position=(0, 64)))
    tmm = sT.n
    b.w("vector", sT, tmm)
    b.v(lambda e, slot=slot: e.tensor_scalar(orv[64:128, 0:1], slot, atric[64:128, 0:1], None, ALU.mult))
    b.w("tensor", sV, sV.n)
    if maybe_stop("p5"):
        return nc, dram
    ev = {}
    for m in range(ND):
        slot = (PS[1] if m % 2 == 0 else PS[6])[:, 134:135]
        if m >= 2:
            b.w("tensor", sV, ev[m - 2])
        for j in range(ND):
            b.t(lambda e, m=m, j=j, slot=slot: e.matmul(
                slot, WOT[:, j * D + m * 128:j * D + (m + 1) * 128],
                BCOMB[:, j:j + 1], start=(j == 0), stop=(j == ND - 1)))
        tmm = sT.n
        b.w("vector", sT, tmm)
        ev[m] = b.v(lambda e, m=m, slot=slot: e.tensor_copy(WBIC[:, m:m + 1], slot))
    b.w("tensor", sV, sV.n)
    for m in range(ND):
        slot = PS[2][0:1, 0:128]
        b.t(lambda e, m=m, slot=slot: e.transpose(slot, WBIC[:, m:m + 1], S["cIDENT"][:, :]))
        tmm = sT.n
        b.w("vector", sT, tmm)
        tv = b.v(lambda e, m=m, slot=slot: e.tensor_copy(wbias_row[0:1, m * 128:(m + 1) * 128], slot))
        b.w("tensor", sV, tv)
    for n2 in range(2):
        slot = (PS[3] if n2 == 0 else PS[4])[:, 0:512]
        b.t(lambda e, n2=n2, slot=slot: e.matmul(
            slot, S["cONESROW"][:, :], wbias_row[0:1, n2 * 512:(n2 + 1) * 512], start=True, stop=True))
        tmm = sT.n
        b.w("vector", sT, tmm)
        b.v(lambda e, n2=n2, slot=slot: e.tensor_copy(wbias_rep[:, n2 * 512:(n2 + 1) * 512], slot))
    slot = PS[1][0:1, 136:137]
    for m in range(ND):
        b.t(lambda e, m=m, slot=slot: e.matmul(
            slot, WBIC[:, m:m + 1], rwsb[:, m:m + 1], start=(m == 0), stop=(m == ND - 1)))
    slot2 = PS[1][0:1, 138:139]
    for m in range(ND):
        b.t(lambda e, m=m, slot2=slot2: e.matmul(
            slot2, rwsb[:, m:m + 1], S["cONESCOL"][:, 0:1], start=(m == 0), stop=(m == ND - 1)))
    tmm = sT.n
    b.w("vector", sT, tmm)
    b.v(lambda e, slot=slot: e.tensor_copy(scal[0:1, 3:4], slot))
    tv = b.v(lambda e, slot2=slot2: e.tensor_copy(scal[0:1, 4:5], slot2))
    b.w("tensor", sV, tv)
    b.t(lambda e: e.matmul(PS[1][:, 140:141], S["cONESROW"][:, :], scal[0:1, 3:4], start=True, stop=True))
    tmm = b.t(lambda e: e.matmul(PS[1][:, 141:142], S["cONESROW"][:, :], scal[0:1, 4:5], start=True, stop=True))
    b.w("vector", sT, tmm)
    b.v(lambda e: e.tensor_copy(wrc[:], PS[1][:, 140:141]))
    b.v(lambda e: e.tensor_copy(srepc[:], PS[1][:, 141:142]))
    if maybe_stop("p6"):
        return nc, dram
    slot = PS[7][0:1, 0:448]
    for c in range(ND):
        b.t(lambda e, c=c, slot=slot: e.matmul(
            slot, S["cONESCOL"][:, :], acat[:, c * 448:(c + 1) * 448],
            start=(c == 0), stop=(c == ND - 1)))
    tmm = sT.n
    b.w("vector", sT, tmm)
    b.v(lambda e, slot=slot: e.tensor_scalar(negs[:, :], slot, -1.0, None, ALU.mult))
    b.fence()
    tap("acat0", acat[:, 0:448])
    tap("acat7", acat[:, 7 * 448:8 * 448])
    tap("outb", outb[:, :])
    tap("wbias_rep", wbias_rep[:, 0:D])
    tap("orv", orv[:, 0:1])
    tap("negs", negs[0:1, :])

    aS = Scope()

    def phaseA(bi, t_xload):
        XT2 = [aS.g("xT0", D), aS.g("xT1", D)]
        XSQ = aS.g("xsq", D)
        PP = [aS.g("pp0", 448), aS.g("pp1", 448)]
        CUM = aS.g("cum", 256)
        RT2 = [aS.g("rt0", 128), aS.g("rt1", 128)]
        MUR = aS.row("murow", 128)
        STC = aS.g("stc", 8)
        b.w("tensor", qS, t_xload)
        tv = b.v(lambda e: e.memset(carry[:, :], 0.0))
        b.w("tensor", sV, tv)
        ev_prev = {}
        for t in range(NT):
            xT = XT2[t % 2]
            pp = PP[t % 2]
            rt = RT2[t % 2]
            if t >= 1:
                b.w("tensor", sV, ev_prev["xT"])
            for c in range(ND):
                ps = (PS[2] if c < 4 else PS[3])[:, (c % 4) * 128:(c % 4) * 128 + 128]
                b.t(lambda e, t=t, c=c, ps=ps: e.transpose(
                    ps, x_tok[:, t * D + c * 128:t * D + (c + 1) * 128], S["cIDENT"][:, :]))
            tm_tr = sT.n
            b.w("vector", sT, tm_tr)
            b.v(lambda e, xT=xT: e.tensor_copy(xT[:, 0:512], PS[2][:, 0:512]))
            ev_xT = b.v(lambda e, xT=xT: e.tensor_copy(xT[:, 512:1024], PS[3][:, 0:512]))
            ev_prev["xT"] = ev_xT
            b.w("gpsimd", sT, tm_tr)
            t_wb = b.dma("gpsimd", lambda e, t=t: e.dma_start(
                out=x_tok[:, t * D:(t + 1) * D], in_=wbias_rep[:, :], accum_op=ALU.add))
            b.w("scalar", sV, ev_xT)
            sq_t = b.s(lambda e, xT=xT: e.activation(XSQ[:, :], xT[:, :], AF.Square))
            if t >= 1:
                b.w("tensor", sV, ev_prev["stats"])
            b.w("tensor", sV, ev_xT)
            for c in range(ND):
                b.t(lambda e, c=c, xT=xT: e.matmul(
                    PS[1][:, 0:1], xT[:, c * 128:(c + 1) * 128], S["cONESCOL"][:, :],
                    start=(c == 0), stop=(c == ND - 1)))
            b.w("tensor", sS, sq_t)
            for c in range(ND):
                b.t(lambda e, c=c: e.matmul(
                    PS[1][:, 1:2], XSQ[:, c * 128:(c + 1) * 128], S["cONESCOL"][:, :],
                    start=(c == 0), stop=(c == ND - 1)))
            for c in range(ND):
                b.t(lambda e, c=c, xT=xT: e.matmul(
                    PS[5][:, 496:497], xT[:, c * 128:(c + 1) * 128], rwsb[:, c:c + 1],
                    start=(c == 0), stop=(c == ND - 1)))
            tm_st = sT.n
            if t >= 1:
                b.w("tensor", sV, ev_prev["pp"])
            for c in range(ND):
                b.t(lambda e, c=c, xT=xT: e.matmul(
                    PS[0][:, 0:448], xT[:, c * 128:(c + 1) * 128],
                    acat[:, c * 448:(c + 1) * 448], start=(c == 0), stop=False))
            b.w("vector", sT, tm_st)
            b.v(lambda e: e.tensor_scalar(STC[:, 0:1], PS[1][:, 0:1], 1.0 / D, None, ALU.mult))
            b.v(lambda e: e.tensor_scalar(STC[:, 2:3], PS[1][:, 1:2], 1.0 / D, None, ALU.mult))
            b.v(lambda e: e.tensor_mul(STC[:, 1:2], STC[:, 0:1], STC[:, 0:1]))
            ev_var = b.v(lambda e: e.tensor_sub(STC[:, 2:3], STC[:, 2:3], STC[:, 1:2]))
            b.w("scalar", sV, ev_var)
            sq2 = b.s(lambda e: e.activation(STC[:, 3:4], STC[:, 2:3], AF.Sqrt, bias=epsc[:, 0:1]))
            b.w("vector", sS, sq2)
            ev_rstd = b.v(lambda e: e.reciprocal(STC[:, 4:5], STC[:, 3:4]))
            b.w("tensor", sV, ev_var)
            tmu = b.t(lambda e: e.transpose(PS[5][0:1, 0:128], STC[:, 0:1], S["cIDENT"][:, :]))
            b.w("vector", sT, tmu)
            ev_mur = b.v(lambda e: e.tensor_copy(MUR[:, :], PS[5][0:1, 0:128]))
            b.w("tensor", sV, ev_mur)
            b.t(lambda e: e.matmul(PS[0][:, 0:448], MUR[:, :], negs[0:1, :], start=False, stop=True))
            tm_corr = sT.n
            b.w("vector", sT, tm_corr)
            b.w("vector", sV, ev_rstd)
            ev_pp = b.v(lambda e, pp=pp: e.tensor_scalar(
                pp[:, 0:448], PS[0][:, 0:448], STC[:, 4:5], None, ALU.mult))
            ev_prev["pp"] = ev_pp
            if t <= 1 and bi == 0:
                tap("pp%d" % t, pp[:, 0:448])
            if t >= 1:
                b.w("tensor", sV, ev_prev["cum"])
            b.w("tensor", sV, ev_pp)
            b.t(lambda e, pp=pp: e.matmul(PS[4][:, 0:256], S["cTRIT"][:, :], pp[:, 192:448], start=True, stop=True))
            tm_cum = b.t(lambda e, pp=pp: e.matmul(PS[4][:, 256:512], S["cONES128"][:, :], pp[:, 192:448], start=True, stop=True))
            b.w("vector", sT, tm_cum)
            b.v(lambda e: e.tensor_add(CUM[:, :], PS[4][:, 0:256], carry[:, :]))
            ev_cum = b.v(lambda e: e.tensor_add(carry[:, :], carry[:, :], PS[4][:, 256:512]))
            ev_prev["cum"] = ev_cum
            b.v(lambda e, pp=pp: e.tensor_mul(XSQ[:, 0:64], pp[:, 0:64], CUM[:, 0:64]))
            b.v(lambda e, pp=pp: e.tensor_mul(XSQ[:, 64:128], pp[:, 64:128], CUM[:, 64:128]))
            b.v(lambda e: e.tensor_add(XSQ[:, 0:64], XSQ[:, 0:64], XSQ[:, 64:128]))
            b.v(lambda e, rt=rt, t=t: e.tensor_scalar(
                rt[:, 0:64], XSQ[:, 0:64], invc[:, t:t + 1], None, ALU.mult))
            b.v(lambda e, pp=pp: e.tensor_mul(XSQ[:, 64:128], pp[:, 128:192], CUM[:, 128:192]))
            ev_R = b.v(lambda e, rt=rt, t=t: e.scalar_tensor_tensor(
                rt[:, 64:128], XSQ[:, 64:128], invc2[:, t:t + 1], CUM[:, 192:256], ALU.mult, ALU.mult))
            b.w("tensor", sV, ev_R)
            trt = b.t(lambda e, rt=rt: e.transpose(PS[5][:, 128:256], rt[:, :], S["cIDENT"][:, :]))
            b.w("vector", sT, trt)
            ev_rt = b.v(lambda e, rt=rt: e.tensor_copy(rt[:, :], PS[5][:, 128:256]))
            b.w("tensor", sV, ev_rt)
            if t <= 1 and bi == 0:
                tap("rtT%d" % t, rt[:, :])
            b.t(lambda e, rt=rt: e.matmul(PS[5][:, 497:498], rt[:, :], orv[:, 0:1], start=True, stop=True))
            tm_w = sT.n
            b.w("vector", sT, tm_w)
            b.v(lambda e, t=t: e.tensor_copy(b2c[:, t:t + 1], PS[5][:, 496:497]))
            b.v(lambda e, t=t: e.tensor_copy(ccum[:, t:t + 1], PS[5][:, 497:498]))
            b.v(lambda e, t=t: e.tensor_scalar(
                wraw[:, t:t + 1], PS[5][:, 496:497], wrc[:, 0:1], None, ALU.add))
            ev_wraw = b.v(lambda e, t=t: e.tensor_add(
                wraw[:, t:t + 1], wraw[:, t:t + 1], PS[5][:, 497:498]))
            ev_prev["stats"] = ev_wraw
            ev_prev["cum"] = ev_wraw
            if t >= 1:
                b.w("tensor", sV, ev_prev["out"])
            for n2 in range(2):
                b.t(lambda e, rt=rt, n2=n2: e.matmul(
                    (PS[6] if n2 == 0 else PS[7])[:, 0:512], rt[:, :],
                    outb[:, n2 * 512:(n2 + 1) * 512], start=True, stop=True))
            tm_out = sT.n
            b.w("vector", sT, tm_out)
            b.w("vector", qG, t_wb)
            b.v(lambda e, t=t: e.tensor_add(
                x_tok[:, t * D:t * D + 512], x_tok[:, t * D:t * D + 512], PS[6][:, 0:512]))
            ev_out = b.v(lambda e, t=t: e.tensor_add(
                x_tok[:, t * D + 512:(t + 1) * D], x_tok[:, t * D + 512:(t + 1) * D], PS[7][:, 0:512]))
            ev_prev["out"] = ev_out
        b.fence()
        tap("xA%d" % bi, x_tok[:, 0:4096])
        tap("wrawA%d" % bi, wraw[:, :])
        tap("xrA%d" % bi, b2c[:, :])
        tap("rorvA%d" % bi, ccum[:, :])
        tap("wrcA%d" % bi, wrc[:, :])
        tap("srepA%d" % bi, srepc[:, :])

    fS = Scope()

    def phaseFpre(bi):
        DUMP = fS.g("dump", D)
        WREP = fS.g("wrep", T)
        GT2 = [fS.g("gt0", T), fS.g("gt1", T)]
        ST2 = [fS.g("st0", RS), fS.g("st1", RS)]
        WTR = fS.g("wtr", 128)
        WROW = fS.row("wrow", T)
        BASE2 = [fS.g("base0", D), fS.g("base1", D)]
        AC = fS.g("acol", NT + 8)
        BC = fS.g("bcol", NT)
        TMP = fS.g("ftmp", NT)
        for t in range(NT):
            b.v(lambda e, t=t: e.reduce_sum(stmu[:, t:t + 1], x_tok[:, t * D:(t + 1) * D], axis=AXX))
            b.s(lambda e, t=t: e.activation(DUMP[:, :], x_tok[:, t * D:(t + 1) * D],
                                            AF.Square, accum_out=stro[:, t:t + 1]))
        b.v(lambda e: e.tensor_scalar(stmu[:, :], stmu[:, :], 1.0 / D, None, ALU.mult))
        b.w("vector", sS, sS.n)
        b.v(lambda e: e.tensor_scalar(stro[:, :], stro[:, :], 1.0 / D, None, ALU.mult))
        b.v(lambda e: e.tensor_mul(b2c[:, :], stmu[:, :], stmu[:, :]))
        tv = b.v(lambda e: e.tensor_sub(stro[:, :], stro[:, :], b2c[:, :]))
        b.w("scalar", sV, tv)
        ts_ = b.s(lambda e: e.activation(stro[:, :], stro[:, :], AF.Sqrt, bias=epsc[:, 0:1]))
        b.w("vector", sS, ts_)
        b.v(lambda e: e.reciprocal(stro[:, :], stro[:, :]))
        b.v(lambda e: e.tensor_mul(b2c[:, :], stmu[:, :], stro[:, :]))
        b.v(lambda e: e.tensor_scalar(b2c[:, :], b2c[:, :], -1.0, None, ALU.mult))
        b.v(lambda e: e.tensor_scalar(TMP[:, :], stmu[:, :], srepc[:, 0:1], None, ALU.mult))
        b.v(lambda e: e.tensor_sub(TMP[:, :], wraw[:, :], TMP[:, :]))
        b.v(lambda e: e.tensor_mul(TMP[:, :], TMP[:, :], stro[:, :]))
        tv_w = b.v(lambda e: e.tensor_scalar(TMP[:, :], TMP[:, :], rbc[:, 0:1], None, ALU.add))
        b.w("tensor", sV, tv_w)
        tmm = b.t(lambda e: e.transpose(PS[5][0:16, 0:128], TMP[:, 0:16], S["cIDENT"][:, :]))
        b.w("vector", sT, tmm)
        tv = b.v(lambda e: e.tensor_copy(WTR[0:16, :], PS[5][0:16, 0:128]))
        b.w("gpsimd", sV, tv)
        tg = b.dma("gpsimd", lambda e: e.dma_start(out=WROW[:, :], in_=WTR[0:16, :]))
        b.w("tensor", qG, tg)
        wrep_ev = {}
        for n4 in range(4):
            ps = (PS[2] if n4 % 2 == 0 else PS[3])[:, 0:512]
            if n4 >= 2:
                b.w("tensor", sV, wrep_ev[n4 - 2])
            b.t(lambda e, n4=n4, ps=ps: e.matmul(
                ps, S["cONESROW"][:, :], WROW[0:1, n4 * 512:(n4 + 1) * 512], start=True, stop=True))
            tmm = sT.n
            b.w("vector", sT, tmm)
            wrep_ev[n4] = b.v(lambda e, n4=n4, ps=ps: e.tensor_copy(WREP[:, n4 * 512:(n4 + 1) * 512], ps))
        ev_gt = {}
        for j in range(NT):
            gt = GT2[j % 2]
            if j >= 2:
                b.w("vector", sT, ev_gt[j - 2])
            tv = b.v(lambda e, gt=gt, j=j: e.tensor_scalar(
                gt[:, :], WREP[:, :], TMP[:, j:j + 1], None, ALU.is_lt))
            b.w("tensor", sV, tv)
            for i in range(NT):
                b.t(lambda e, gt=gt, i=i, j=j: e.matmul(
                    PS[1][:, i:i + 1], gt[:, i * 128:(i + 1) * 128], S["cONESCOL"][:, :],
                    start=(j == 0 and i == 0), stop=(j == NT - 1 and i == NT - 1)))
            ev_gt[j] = sT.n
        b.w("vector", sT, sT.n)
        tv = b.v(lambda e: e.tensor_scalar(selm[:, :], PS[1][:, 0:16], 254.5, None, ALU.is_le))
        b.w("tensor", sV, tv)
        b.t(lambda e: e.matmul(PS[4][:, 0:16], S["cTRIST"][:, :], selm[:, :], start=True, stop=False))
        tmm = b.t(lambda e: e.matmul(PS[5][0:1, 0:16], S["cONESCOL"][:, :], selm[:, :], start=True, stop=True))
        b.w("vector", sT, tmm)
        tv = b.v(lambda e: e.tensor_copy(WTR[0:1, 0:16], PS[5][0:1, 0:16]))
        b.w("tensor", sV, tv)
        tmm = b.t(lambda e: e.transpose(PS[5][0:16, 16:17], WTR[0:1, 0:16], S["cIDENT"][0:1, 0:1]))
        b.w("vector", sT, tmm)
        tv = b.v(lambda e: e.tensor_copy(WTR[0:16, 16:17], PS[5][0:16, 16:17]))
        b.w("tensor", sV, tv)
        tmm = b.t(lambda e: e.matmul(PS[5][0:16, 17:18], S["cTRIST"][0:16, 0:16], WTR[0:16, 16:17], start=True, stop=True))
        b.w("vector", sT, tmm)
        tv = b.v(lambda e: e.tensor_copy(WTR[0:16, 18:19], PS[5][0:16, 17:18]))
        b.w("tensor", sV, tv)
        tmm = b.t(lambda e: e.transpose(PS[5][0:1, 19:35], WTR[0:16, 18:19], S["cIDENT"][0:16, 0:16]))
        b.w("vector", sT, tmm)
        tv = b.v(lambda e: e.tensor_copy(WTR[0:1, 20:36], PS[5][0:1, 19:35]))
        b.w("tensor", sV, tv)
        b.t(lambda e: e.matmul(PS[4][:, 0:16], S["cONESROW"][:, :], WTR[0:1, 20:36], start=False, stop=True))
        tmm = sT.n
        b.w("vector", sT, tmm)
        tv_cc = b.v(lambda e: e.tensor_copy(ccum[:, :], PS[4][:, 0:16]))
        c4v = cols5.rearrange("p (t f) -> p t f", f=5)
        b.v(lambda e: e.tensor_copy(c4v[:, :, 0], S["cARANGE"][:, :]))
        b.v(lambda e: e.tensor_copy(c4v[:, :, 1], TMP[:, :]))
        b.v(lambda e: e.tensor_copy(c4v[:, :, 2], stro[:, :]))
        b.v(lambda e: e.tensor_copy(c4v[:, :, 3], b2c[:, :]))
        b.v(lambda e: e.tensor_copy(c4v[:, :, 4], S["cONES128"][:, 0:NT]))
        ev_st = {}
        for j in range(NT):
            st = ST2[j % 2]
            if j >= 2:
                b.w("vector", sT, ev_st[j - 2])
            b.v(lambda e, st=st, j=j: e.tensor_scalar(
                st[:, :], S["cIOTAR"][:, :], ccum[:, j:j + 1], None, ALU.is_equal))
            tv = b.v(lambda e, st=st, j=j: e.tensor_scalar(
                st[:, :], st[:, :], selm[:, j:j + 1], None, ALU.mult))
            b.w("tensor", sV, tv)
            for rb in range(2):
                b.t(lambda e, st=st, rb=rb, j=j: e.matmul(
                    PS[2][:, 5 * rb:5 * rb + 5], st[:, rb * 128:(rb + 1) * 128],
                    cols5[:, j * 5:(j + 1) * 5], start=(j == 0 and rb == 0),
                    stop=(j == NT - 1 and rb == 1)))
            ev_st[j] = sT.n
        b.w("vector", sT, sT.n)
        tv = b.v(lambda e: e.tensor_copy(selcols[:, :], PS[2][:, 0:10]))
        b.w("vector", sV, tv)
        for rb in range(2):
            b.v(lambda e, rb=rb, bi=bi: e.tensor_scalar(
                AC[:, 16 + rb:17 + rb], selcols[:, 5 * rb:5 * rb + 1], float(bi * T), None, ALU.add))
            b.v(lambda e, rb=rb: e.tensor_scalar(
                AC[:, 18 + rb:19 + rb], selcols[:, 5 * rb + 4:5 * rb + 5], 1.0, -4096.0, ALU.subtract, ALU.mult))
            b.v(lambda e, rb=rb: e.tensor_add(AC[:, 18 + rb:19 + rb], AC[:, 18 + rb:19 + rb], AC[:, 16 + rb:17 + rb]))
            b.v(lambda e, rb=rb: e.tensor_copy(selidx[:, rb:rb + 1], AC[:, 16 + rb:17 + rb]))
        tv_idx = b.v(lambda e: e.tensor_copy(selidx[:, 2:4], AC[:, 18:20]))
        b.v(lambda e: e.tensor_mul(AC[:, 0:NT], selm[:, :], stro[:, :]))
        b.v(lambda e: e.tensor_sub(AC[:, 0:NT], stro[:, :], AC[:, 0:NT]))
        b.v(lambda e: e.tensor_scalar(AC[:, 0:NT], AC[:, 0:NT], 1.0, None, ALU.add))
        b.v(lambda e: e.tensor_mul(BC[:, :], selm[:, :], b2c[:, :]))
        tv_ab = b.v(lambda e: e.tensor_sub(BC[:, :], b2c[:, :], BC[:, :]))
        b.w("scalar", sV, tv_ab)
        st_prev = {}
        t_store = None
        for t in range(NT):
            bb = BASE2[t % 2]
            if t >= 2:
                b.w("scalar", qS, st_prev[t - 2])
            ts_ = b.s(lambda e, t=t, bb=bb: e.activation(
                bb[:, :], x_tok[:, t * D:(t + 1) * D], AF.Identity,
                bias=BC[:, t:t + 1], scale=AC[:, t:t + 1]))
            b.w("sync", sS, ts_)
            t_store = b.dma("sync", lambda e, t=t, bb=bb, bi=bi: e.dma_start(
                out=out_t[bi, t * 128:(t + 1) * 128, :], in_=bb[:, :]))
            st_prev[t] = t_store
        b.w("gpsimd", qS, t_store)
        b.w("gpsimd", sV, tv_idx)
        b.w("gpsimd", qG, t_outinit)
        out_flat = out_t.rearrange("b t d -> (b t) d")
        for rb in range(2):
            b.emit("gpsimd", lambda e, rb=rb, out_flat=out_flat: e.indirect_dma_start(
                out=XRS[:, rb * D:(rb + 1) * D], out_offset=None,
                in_=out_flat, in_offset=IndirectOffsetOnAxis(ap=selidx[:, rb:rb + 1], axis=0),
            ), qG, 16)
            b.emit("gpsimd", lambda e, rb=rb: e.indirect_dma_start(
                out=CSG[:, rb * RS:(rb + 1) * RS], out_offset=None,
                in_=dram["CSTAB"][:, :], in_offset=IndirectOffsetOnAxis(ap=selidx[:, rb:rb + 1], axis=0),
            ), qG, 16)
        b.fence()
        tap("w%d" % bi, TMP[:, :])
        tap("selm%d" % bi, selm[:, :])
        tap("ccum%d" % bi, ccum[:, :])
        tap("selcols%d" % bi, selcols[:, :])
        tap("xrs%d" % bi, XRS[:, :])

    fmS = Scope()

    def phaseFmain(bi, xload_next):
        m = fmS
        XLN = m.g("xln", 2 * D)
        HTOK = m.g("htok", 2 * D)
        HT = m.g("hT", 2048)
        QT = m.g("qT", 2048)
        KT = m.g("kT", 2048)
        VT_ = m.g("vT", 2048)
        CTX = m.g("ctxT", 2048)
        COS2 = m.g("cos2", RS)
        SIN2 = m.g("sin2", RS)
        ASB = m.g("asb", RS)
        ATB = m.g("atb", 2 * RS)
        VTOK = m.g("vtok", 2 * 128)
        H1G = m.g("h1g", 4 * RS)
        WBUF = [m.g("wb0", D), m.g("wb1", D)]
        WBUF2 = [m.g("w20", D), m.g("w21", D)]
        RCOL = m.g("rcol", 8)
        FIN = m.g("fin", D)
        DUMP.update(dict(xln=XLN, htok=HTOK, hT=HT, qT=QT, kT=KT, vT=VT_,
                         ctx=CTX, cos2=COS2, sin2=SIN2, h1g=H1G, fin=FIN, rcol=RCOL))
        if xload_next is not None:
            b.w("sync", sS, sS.n)
            t_xn = b.dma("sync", lambda e, xload_next=xload_next: e.dma_start(
                out=x_tok.rearrange("q (n d) -> q n d", d=D),
                in_=x_in[xload_next].rearrange("(n p) d -> p n d", p=128)))
        else:
            t_xn = None
        for rb in range(2):
            b.s(lambda e, rb=rb: e.activation(
                XLN[:, rb * D:(rb + 1) * D], XRS[:, rb * D:(rb + 1) * D], AF.Identity,
                scale=selcols[:, 5 * rb + 2:5 * rb + 3], bias=selcols[:, 5 * rb + 3:5 * rb + 4]))
        for rb in range(2):
            b.s(lambda e, rb=rb: e.activation(HTOK[:, rb * D:(rb + 1) * D], XLN[:, rb * D:(rb + 1) * D],
                                              AF.Square, accum_out=RCOL[:, rb:rb + 1]))
        ts_ = b.s(lambda e: e.activation(RCOL[:, 2:4], RCOL[:, 0:2], AF.Sqrt, scale=epsc[:, 2:3], bias=epsc[:, 1:2]))
        b.w("vector", sS, ts_)
        b.v(lambda e: e.reciprocal(RCOL[:, 2:4], RCOL[:, 2:4]))
        for rb in range(2):
            b.v(lambda e, rb=rb: e.tensor_scalar(
                HTOK[:, rb * D:(rb + 1) * D], XLN[:, rb * D:(rb + 1) * D],
                RCOL[:, 2 + rb:3 + rb], None, ALU.mult))
        tv_hn = sV.n

        def tok2feat(src, dst, scale_cols, tv_ready):
            b.w("tensor", sV, tv_ready)
            evs = {}
            banks = (PS[2], PS[3], PS[4], PS[5])
            for dc in range(ND):
                bank = banks[dc % 4]
                if dc >= 4:
                    b.w("tensor", sV, evs[dc - 4])
                for rb in range(2):
                    b.t(lambda e, dc=dc, rb=rb, bank=bank, src=src: e.transpose(
                        bank[:, rb * 128:(rb + 1) * 128],
                        src[:, rb * D + dc * 128:rb * D + (dc + 1) * 128], S["cIDENT"][:, :]))
                tmm = sT.n
                b.w("vector", sT, tmm)
                if scale_cols is not None:
                    evs[dc] = b.v(lambda e, dc=dc, bank=bank, dst=dst, sc=scale_cols:
                                  e.tensor_scalar(dst[:, dc * RS:(dc + 1) * RS],
                                                  bank[:, 0:256], sc[:, dc:dc + 1], None, ALU.mult))
                else:
                    evs[dc] = b.v(lambda e, dc=dc, bank=bank, dst=dst:
                                  e.tensor_copy(dst[:, dc * RS:(dc + 1) * RS], bank[:, 0:256]))
            b.w("tensor", sV, sV.n)

        tok2feat(HTOK, HT, ln1sb, tv_hn)
        if bi == 0:
            tap("hnT", HT[:, :])
        for rb in range(2):
            for which, dstb in ((0, COS2), (1, SIN2)):
                ps = PS[4][:, 0:128]
                b.t(lambda e, rb=rb, which=which, ps=ps: e.transpose(
                    ps, CSG[:, rb * RS + which * 128:rb * RS + (which + 1) * 128], S["cIDENT"][:, :]))
                tmm = sT.n
                b.w("vector", sT, tmm)
                tv = b.v(lambda e, rb=rb, ps=ps, dstb=dstb: e.tensor_copy(
                    dstb[:, rb * 128:(rb + 1) * 128], ps))
                b.w("tensor", sV, tv)

        def proj(wname, dst, rope):
            mm_done = {}
            b.w("sync", sT, sT.n)
            for k in range(ND):
                if k >= 2:
                    b.w("sync", sT, mm_done[k - 2])
                tl = b.dma("sync", lambda e, wname=wname, k=k: e.dma_start(
                    out=WBUF[k % 2][:, :], in_=dram[wname][k * 128:(k + 1) * 128, :]))
                b.w("tensor", qS, tl)
                for mt in range(ND):
                    ps = PS[mt // 2][:, (mt % 2) * 256:(mt % 2) * 256 + 256]
                    b.t(lambda e, k=k, mt=mt, ps=ps: e.matmul(
                        ps, WBUF[k % 2][:, mt * 128:(mt + 1) * 128], HT[:, k * RS:(k + 1) * RS],
                        start=(k == 0 and mt % 2 == 0), stop=(k == ND - 1 and mt % 2 == 1)))
                mm_done[k] = sT.n
            b.w("vector", sT, sT.n)
            for mt in range(ND):
                ps = PS[mt // 2][:, (mt % 2) * 256:(mt % 2) * 256 + 256]
                b.v(lambda e, mt=mt, ps=ps, dst=dst: e.tensor_copy(dst[:, mt * RS:(mt + 1) * RS], ps))
            b.w("tensor", sV, sV.n)
            if rope:
                rope_ev = {}
                for mt in range(ND):
                    ps = (PS[6] if mt % 2 == 0 else PS[7])[:, 0:256]
                    if mt >= 2:
                        b.w("tensor", sV, rope_ev[mt - 2])
                    b.t(lambda e, mt=mt, ps=ps, dst=dst: e.matmul(
                        ps, S["cROTT"][:, :], dst[:, mt * RS:(mt + 1) * RS], start=True, stop=True))
                    tmm = sT.n
                    b.w("vector", sT, tmm)
                    b.v(lambda e, mt=mt, dst=dst: e.tensor_mul(
                        ASB[:, :], dst[:, mt * RS:(mt + 1) * RS], COS2[:, :]))
                    b.v(lambda e, mt=mt, ps=ps: e.tensor_mul(ATB[:, 0:RS], ps, SIN2[:, :]))
                    rope_ev[mt] = b.v(lambda e, mt=mt, dst=dst: e.tensor_add(
                        dst[:, mt * RS:(mt + 1) * RS], ASB[:, :], ATB[:, 0:RS]))
                b.w("tensor", sV, sV.n)

        if bi == 0:
            tap("xlnF", XLN[:, :])
        proj("WQ", QT, True)
        if bi == 0:
            tap("qTrope", QT[:, :])
            tap("cos2", COS2[:, :])
            tap("sin2", SIN2[:, :])
        proj("WK", KT, True)
        if bi == 0:
            tap("qT_afterK", QT[:, :])
        proj("WV", VT_, False)
        if bi == 0:
            tap("qT_afterV", QT[:, :])
        if bi == 0:
            tap("vT", VT_[:, :])
        for h in range(H):
            mt, po = h // 2, 64 * (h % 2)
            if bi == 0 and h in (0, 2, 4, 6):
                tap("qT_at%d" % h, QT[:, :])
            if h % 2 == 0:
                for kc in range(2):
                    ps = PS[5][:, kc * 128:(kc + 1) * 128]
                    b.t(lambda e, mt=mt, kc=kc, ps=ps: e.transpose(
                        ps, VT_[:, mt * RS + kc * 128:mt * RS + (kc + 1) * 128], S["cIDENT"][:, :]))
                tmm = sT.n
                b.w("vector", sT, tmm)
                b.v(lambda e: e.tensor_copy(VTOK[:, 0:128], PS[5][:, 0:128]))
                tv = b.v(lambda e: e.tensor_copy(VTOK[:, 128:256], PS[5][:, 128:256]))
                b.w("tensor", sV, tv)
            for qc in range(2):
                ps = (PS[4] if qc == 0 else PS[7])[:, 0:256]
                b.t(lambda e, mt=mt, po=po, qc=qc, ps=ps: e.matmul(
                    ps, QT[po:po + 64, mt * RS + qc * 128:mt * RS + (qc + 1) * 128],
                    KT[po:po + 64, mt * RS:(mt + 1) * RS], start=True, stop=True))
                tmm = sT.n
                b.w("vector", sT, tmm)
                b.v(lambda e, qc=qc, ps=ps: e.scalar_tensor_tensor(
                    ASB[:, :], ps, 0.125, S["cCAUSAL"][:, qc * RS:(qc + 1) * RS], ALU.mult, ALU.add))
                tv = b.v(lambda e: e.reduce_max(RCOL[:, 6:7], ASB[:, :], axis=AXX, negate=True))
                b.w("scalar", sV, tv)
                ts_ = b.s(lambda e: e.activation(ASB[:, :], ASB[:, :], AF.Exp, bias=RCOL[:, 6:7]))
                b.w("vector", sS, ts_)
                b.v(lambda e: e.reduce_sum(RCOL[:, 7:8], ASB[:, :], axis=AXX))
                b.v(lambda e: e.reciprocal(RCOL[:, 7:8], RCOL[:, 7:8]))
                tv = b.v(lambda e: e.tensor_scalar(ASB[:, :], ASB[:, :], RCOL[:, 7:8], None, ALU.mult))
                if bi == 0 and h in (0, 7) and qc == 0:
                    tap("asb_h%d" % h, ASB[:, :])
                b.w("tensor", sV, tv)
                for kc in range(2):
                    ps2 = PS[5][:, 256:384]
                    b.t(lambda e, kc=kc, ps2=ps2: e.transpose(ps2, ASB[:, kc * 128:(kc + 1) * 128], S["cIDENT"][:, :]))
                    tmm = sT.n
                    b.w("vector", sT, tmm)
                    tv = b.v(lambda e, kc=kc, qc=qc, ps2=ps2: e.tensor_copy(
                        ATB[:, kc * RS + qc * 128:kc * RS + (qc + 1) * 128], ps2))
                    b.w("tensor", sV, tv)
            for kc in range(2):
                if po == 0:
                    b.t(lambda e, kc=kc: e.matmul(
                        PS[6][0:64, 0:256], VTOK[:, kc * 128:kc * 128 + 64],
                        ATB[:, kc * RS:(kc + 1) * RS], start=(kc == 0), stop=(kc == 1)))
                else:
                    b.t(lambda e, kc=kc: e.matmul(
                        PS[6][64:128, 0:256], VTOK[:, kc * 128 + 64:kc * 128 + 128],
                        ATB[:, kc * RS:(kc + 1) * RS], start=(kc == 0), stop=(kc == 1),
                        tile_position=(0, 64)))
            tmm = sT.n
            b.w("vector", sT, tmm)
            tv = b.v(lambda e, mt=mt, po=po: e.tensor_copy(
                CTX[po:po + 64, mt * RS:(mt + 1) * RS], PS[6][po:po + 64, 0:256]))
            if bi == 0 and h in (0, 7):
                tap("atb_h%d" % h, ATB[:, :])
            b.w("tensor", sV, tv)
        if bi == 0:
            tap("ctxT", CTX[:, :])
        mm_done = {}
        b.w("sync", sT, sT.n)
        for k in range(ND):
            if k >= 2:
                b.w("sync", sT, mm_done[k - 2])
            tl = b.dma("sync", lambda e, k=k: e.dma_start(
                out=WBUF[k % 2][:, :], in_=dram["WOf"][k * 128:(k + 1) * 128, :]))
            b.w("tensor", qS, tl)
            for mt in range(ND):
                ps = PS[mt // 2][:, (mt % 2) * 256:(mt % 2) * 256 + 256]
                b.t(lambda e, k=k, mt=mt, ps=ps: e.matmul(
                    ps, WBUF[k % 2][:, mt * 128:(mt + 1) * 128], CTX[:, k * RS:(k + 1) * RS],
                    start=(k == 0 and mt % 2 == 0), stop=(k == ND - 1 and mt % 2 == 1)))
            mm_done[k] = sT.n
        b.w("vector", sT, sT.n)
        for mt in range(ND):
            ps = PS[mt // 2][:, (mt % 2) * 256:(mt % 2) * 256 + 256]
            b.v(lambda e, mt=mt, ps=ps: e.tensor_copy(KT[:, mt * RS:(mt + 1) * RS], ps))
        b.w("tensor", sV, sV.n)
        evs = {}
        banks = (PS[2], PS[3], PS[4], PS[5])
        XLNV = XLN.rearrange("p (r x) -> p r x", x=D)
        for dc in range(ND):
            bank = banks[dc % 4]
            if dc >= 4:
                b.w("tensor", sV, evs[dc - 4])
            for rb in range(2):
                b.t(lambda e, dc=dc, rb=rb, bank=bank: e.transpose(
                    bank[:, rb * 128:(rb + 1) * 128],
                    KT[:, dc * RS + rb * 128:dc * RS + (rb + 1) * 128], S["cIDENT"][:, :]))
            tmm = sT.n
            b.w("vector", sT, tmm)
            evs[dc] = b.v(lambda e, dc=dc, bank=bank: e.tensor_add(
                XLNV[:, :, dc * 128:(dc + 1) * 128],
                XLNV[:, :, dc * 128:(dc + 1) * 128], bank[:, 0:256]))
        if bi == 0:
            tap("x1tok", XLN[:, :])
        b.w("scalar", sV, sV.n)
        for rb in range(2):
            b.s(lambda e, rb=rb: e.activation(HTOK[:, rb * D:(rb + 1) * D], XLN[:, rb * D:(rb + 1) * D],
                                              AF.Square, accum_out=RCOL[:, 4 + rb:5 + rb]))
        ts_ = b.s(lambda e: e.activation(RCOL[:, 4:6], RCOL[:, 4:6], AF.Sqrt, scale=epsc[:, 2:3], bias=epsc[:, 1:2]))
        b.w("vector", sS, ts_)
        b.v(lambda e: e.reciprocal(RCOL[:, 4:6], RCOL[:, 4:6]))
        for rb in range(2):
            b.v(lambda e, rb=rb: e.tensor_scalar(
                HTOK[:, rb * D:(rb + 1) * D], XLN[:, rb * D:(rb + 1) * D],
                RCOL[:, 4 + rb:5 + rb], None, ALU.mult))
        tok2feat(HTOK, QT, ln2sb, sV.n)
        h1_done = {}
        last_w2_done = 0
        b.w("sync", sT, sT.n)
        b.w("scalar", sT, sT.n)
        for fc in range(NFC):
            for k in range(ND):
                if k >= 2:
                    b.w("sync", sT, h1_done.get((fc, k - 2), 0))
                elif fc > 0 and k == 0:
                    b.w("sync", sT, h1_done[(fc - 1, ND - 2)])
                elif fc > 0:
                    b.w("sync", sT, h1_done[(fc - 1, ND - 1)])
                tl = b.dma("sync", lambda e, fc=fc, k=k: e.dma_start(
                    out=WBUF[k % 2][:, 0:FC], in_=dram["W1"][k * 128:(k + 1) * 128, fc * FC:(fc + 1) * FC]))
                b.w("tensor", qS, tl)
                for mf in range(FC // 128):
                    ps = PS[4 + mf // 2][:, (mf % 2) * 256:(mf % 2) * 256 + 256]
                    b.t(lambda e, k=k, mf=mf, ps=ps: e.matmul(
                        ps, WBUF[k % 2][:, mf * 128:(mf + 1) * 128], QT[:, k * RS:(k + 1) * RS],
                        start=(k == 0 and mf % 2 == 0), stop=(k == ND - 1 and mf % 2 == 1)))
                h1_done[(fc, k)] = sT.n
            b.w("scalar", sT, sT.n)
            for mf in range(FC // 128):
                ps = PS[4 + mf // 2][:, (mf % 2) * 256:(mf % 2) * 256 + 256]
                b.s(lambda e, mf=mf, ps=ps: e.activation(H1G[:, mf * RS:(mf + 1) * RS], ps, AF.Gelu_apprx_tanh))
            b.w("tensor", sS, sS.n)
            w2_done = {}
            for kf in range(FC // 128):
                if kf >= 2:
                    b.w("scalar", sT, w2_done[kf - 2])
                elif last_w2_done:
                    b.w("scalar", sT, last_w2_done)
                tl2 = b.dma("scalar", lambda e, fc=fc, kf=kf: e.dma_start(
                    out=WBUF2[kf % 2][:, :], in_=dram["W2"][fc * FC + kf * 128:fc * FC + (kf + 1) * 128, :]))
                b.w("tensor", qA, tl2)
                for mt in range(ND):
                    ps = PS[mt // 2][:, (mt % 2) * 256:(mt % 2) * 256 + 256]
                    b.t(lambda e, kf=kf, mt=mt, ps=ps, fc=fc: e.matmul(
                        ps, WBUF2[kf % 2][:, mt * 128:(mt + 1) * 128], H1G[:, kf * RS:(kf + 1) * RS],
                        start=(fc == 0 and kf == 0 and mt % 2 == 0),
                        stop=(fc == NFC - 1 and kf == FC // 128 - 1 and mt % 2 == 1)))
                w2_done[kf] = sT.n
            last_w2_done = w2_done[FC // 128 - 1]
        b.w("vector", sT, sT.n)
        for mt in range(ND):
            ps = PS[mt // 2][:, (mt % 2) * 256:(mt % 2) * 256 + 256]
            b.v(lambda e, mt=mt, ps=ps: e.tensor_copy(KT[:, mt * RS:(mt + 1) * RS], ps))
        b.w("tensor", sV, sV.n)
        t_sc0 = None
        evs = {}
        fbanks = (PS[2], PS[3], PS[4], PS[5])
        for rb in range(2):
            for dc in range(ND):
                k = rb * ND + dc
                ps = fbanks[k % 4][:, 0:128]
                if k >= 4:
                    b.w("tensor", sV, evs[k - 4])
                b.t(lambda e, dc=dc, rb=rb, ps=ps: e.transpose(
                    ps, KT[:, dc * RS + rb * 128:dc * RS + (rb + 1) * 128], S["cIDENT"][:, :]))
                tmm = sT.n
                b.w("vector", sT, tmm)
                b.v(lambda e, dc=dc, rb=rb, ps=ps: e.tensor_add(
                    FIN[:, dc * 128:(dc + 1) * 128],
                    XLN[:, rb * D + dc * 128:rb * D + (dc + 1) * 128], ps))
                b.v(lambda e, dc=dc, rb=rb: e.tensor_scalar(
                    FIN[:, dc * 128:(dc + 1) * 128], FIN[:, dc * 128:(dc + 1) * 128],
                    selcols[:, 5 * rb + 1:5 * rb + 2], None, ALU.mult))
                evs[k] = b.v(lambda e, dc=dc, rb=rb: e.tensor_add(
                    FIN[:, dc * 128:(dc + 1) * 128], FIN[:, dc * 128:(dc + 1) * 128],
                    XRS[:, rb * D + dc * 128:rb * D + (dc + 1) * 128]))
            tvf = sV.n
            if bi == 0:
                tap("fin%d" % rb, FIN[:, :])
            b.w("gpsimd", sV, tvf)
            out_flat = out_t.rearrange("b t d -> (b t) d")
            t_sc = b.emit("gpsimd", lambda e, rb=rb, out_flat=out_flat: e.indirect_dma_start(
                out=out_flat, out_offset=IndirectOffsetOnAxis(ap=selidx[:, 2 + rb:3 + rb], axis=0),
                in_=FIN[:, :], in_offset=None,
                bounds_check=BPC * T - 1, oob_is_err=False,
            ), qG, 16)
            if rb == 0:
                t_sc0 = t_sc
                b.w("vector", qG, t_sc0)
        b.fence()
        return t_xn

    stages = ["pre", "A0", "F0", "M0", "A1", "F1", "M1"]
    upto = stages.index(stop_after) if stop_after in stages else len(stages)
    if upto >= 1:
        phaseA(0, t_x0)
    if upto >= 2:
        phaseFpre(0)
    if upto >= 3:
        t_x1 = phaseFmain(0, 1)
    if upto >= 4:
        phaseA(1, t_x1)
    if upto >= 5:
        phaseFpre(1)
    if upto >= 6:
        phaseFmain(1, None)
    DUMP.update(dict(x_tok=x_tok, selcols=selcols, selm=selm, stmu=stmu, stro=stro, wbias_rep=wbias_rep, negs=negs, orvd=orv, invc=invc, wbias_row=wbias_row, wrcd=wrc, srepd=srepc, abic=abic, atric=atric,
                     wraw=wraw, ccum=ccum, acat=acat, outb=outb, xrs=XRS, csg=CSG))
    emit_dumps()
    b.w("sync", qG, qG.n)
    b.w("sync", qA, qA.n)
    b.w("sync", qS, qS.n)

    b.finalize()
    return nc, dram


_NC_CACHE = {}


def get_nc(dbg_names=(), stop_after="all", end_dumps=()):
    key = (tuple(dbg_names), stop_after, tuple(end_dumps))
    if key not in _NC_CACHE:
        _NC_CACHE[key] = build_nc(dbg_names, stop_after, end_dumps)
    return _NC_CACHE[key]


def make_in_map(inputs, core):
    x = np.ascontiguousarray(np.asarray(inputs["L0_attn_x"], dtype=np.float32))
    base = {
        "WQa": inputs["L0_attn_WQ"], "WKa": inputs["L0_attn_WK"],
        "WOa": inputs["L0_attn_WO"], "WINV": inputs["L0_attn_Winv"],
        "UB": inputs["L0_attn_U_b"], "VB": inputs["L0_attn_V_b"], "WB": inputs["L0_attn_W_b"],
        "UT": inputs["L0_attn_U_t"], "VT": inputs["L0_attn_V_t"], "WT": inputs["L0_attn_W_t"],
        "XTm": inputs["L0_attn_X_t"],
        "BIASB": np.asarray(inputs["L0_attn_bias_b"]).reshape(D, 1),
        "BIAST": np.asarray(inputs["L0_attn_bias_t"]).reshape(D, 1),
        "ABI": np.asarray(inputs["L0_attn_alpha_bi"]).reshape(1, 1),
        "ATRI": np.asarray(inputs["L0_attn_alpha_tri"]).reshape(1, 1),
        "RW": np.asarray(inputs["L0_ffn_router_w"]).reshape(D, 1),
        "RB": np.asarray(inputs["L0_ffn_router_b"]).reshape(1, 1),
        "WQ": inputs["L0_ffn_wq"], "WK": inputs["L0_ffn_wk"], "WV": inputs["L0_ffn_wv"],
        "WOf": inputs["L0_ffn_wo"], "W1": inputs["L0_ffn_w1"], "W2": inputs["L0_ffn_w2"],
        "LN1": np.asarray(inputs["L0_ffn_ln1"]).reshape(D, 1),
        "LN2": np.asarray(inputs["L0_ffn_ln2"]).reshape(D, 1),
    }
    base = {k: np.ascontiguousarray(np.asarray(v, dtype=np.float32)) for k, v in base.items()}
    for k, v in CONSTS.items():
        base[k] = v
    base["x"] = np.ascontiguousarray(x[2 * core:2 * core + 2])
    return base


def kernel(**inputs):
    nc, _ = get_nc()
    in_maps = [make_in_map(inputs, c) for c in range(8)]
    res = run_bass_kernel_spmd(nc, in_maps, core_ids=list(range(8)))
    return np.concatenate([r["out"] for r in res.results], axis=0)


# version 05
# speedup vs baseline: 1.1898x; 1.1862x over previous
import numpy as np
import concourse.bass as bass
import concourse.mybir as mybir
from concourse.bass_utils import run_bass_kernel_spmd
from concourse.bass_core_rust import IndirectOffsetOnAxis

F32 = mybir.dt.float32
I32 = mybir.dt.int32
AF = mybir.ActivationFunctionType
ALU = mybir.AluOpType
AXX = mybir.AxisListType.X

T, D, R, H, HD = 2048, 1024, 64, 16, 64
NT, ND = T // 128, D // 128
KC, RS, FF, BPC = 255, 256, 4096, 2
FC = 512
NFC = FF // FC

ENGS = ("tensor", "vector", "scalar", "gpsimd", "sync")


def host_consts():
    f32 = np.float32
    c = {}
    c["IDENT"] = np.eye(128, dtype=f32)
    c["TRIT"] = np.triu(np.ones((128, 128), dtype=f32))
    c["TRIST"] = np.triu(np.ones((128, 128), dtype=f32), k=1)
    c["ONES128"] = np.ones((128, 128), dtype=f32)
    c["ONESROW"] = np.ones((1, 128), dtype=f32)
    c["ONESCOL"] = np.ones((128, 1), dtype=f32)
    c["IOTAR"] = np.broadcast_to(np.arange(RS, dtype=f32), (128, RS)).copy()
    c["ARANGE"] = np.arange(T, dtype=f32).reshape(NT, 128).T.copy()
    c["COUNTS"] = np.arange(1, T + 1, dtype=f32).reshape(NT, 128).T.copy()
    cb = np.zeros((128, 2 * RS), dtype=f32)
    for qt in range(2):
        q = 128 * qt + np.arange(128)[:, None]
        k = np.arange(RS)[None, :]
        cb[:, RS * qt:RS * qt + RS] = np.where(k <= q, 0.0, np.float32(-1e9))
    c["CAUSAL"] = cb
    r2 = np.zeros((128, 128), dtype=f32)
    for h0 in (0, 64):
        for i in range(32):
            r2[h0 + i, h0 + 32 + i] = -1.0
            r2[h0 + 32 + i, h0 + i] = 1.0
    c["ROTT"] = r2.T.copy()
    import jax
    import jax.numpy as jnp
    cpu = jax.devices("cpu")[0]
    with jax.default_device(cpu):
        invj = 1.0 / 10000.0 ** (jnp.arange(0, HD, 2, dtype=jnp.float32) / HD)
        posj = jnp.arange(T, dtype=jnp.int32).astype(jnp.float32)
        frj = posj[:, None] * invj[None, :]
        embj = jnp.concatenate([frj, frj], axis=-1)
        cosj, sinj = jnp.cos(embj), jnp.sin(embj)
    cosd = np.concatenate([np.asarray(cosj), np.asarray(cosj)], axis=1)
    sind = np.concatenate([np.asarray(sinj), np.asarray(sinj)], axis=1)
    tab = np.concatenate([cosd, sind], axis=1).astype(f32)
    c["CSTAB"] = np.concatenate([tab, tab], axis=0)
    return c


CONSTS = host_consts()
SB_CONSTS = [k for k in CONSTS if k != "CSTAB"]


class Sem:
    def __init__(self, h):
        self.h, self.n = h, 0

    def bump(self, k=1):
        self.n += k
        return self.n


class B:
    def __init__(self, nc):
        self.nc = nc
        self.plan = {e: [] for e in ENGS}
        self.sems = {e: Sem(nc.ctx.enter_context(nc.semaphore("s_" + e))) for e in ENGS}
        self.qsems = {e: Sem(nc.ctx.enter_context(nc.semaphore("q_" + e)))
                      for e in ("sync", "scalar", "gpsimd")}
        self._fw = {e: {} for e in ENGS}

    def emit(self, eng, fn, inc=None, inck=1):
        if eng in ("vector", "scalar") and inc is self.sems[eng] and inc.n > 0:
            self.plan[eng].append(lambda e, s=inc.h, t=inc.n: e.wait_ge(s, t))
        tgt = inc.bump(inck) if inc is not None else None
        if inc is not None:
            self.plan[eng].append(lambda e, fn=fn, s=inc.h, k=inck: fn(e).then_inc(s, k))
        else:
            self.plan[eng].append(lambda e, fn=fn: fn(e))
        return tgt

    def w(self, eng, sem, tgt):
        if tgt:
            self.plan[eng].append(lambda e, s=sem.h, t=tgt: e.wait_ge(s, t))

    def t(self, fn):
        return self.emit("tensor", fn, self.sems["tensor"])

    def v(self, fn):
        return self.emit("vector", fn, self.sems["vector"])

    def s(self, fn):
        return self.emit("scalar", fn, self.sems["scalar"])

    def dma(self, eng, fn):
        return self.emit(eng, fn, self.qsems[eng], 16)

    def fence(self):
        snap = list(self.sems.values()) + list(self.qsems.values())
        for e in ENGS:
            for s in snap:
                if s is self.sems[e]:
                    continue
                if s.n > self._fw[e].get(id(s), 0):
                    self.w(e, s, s.n)
                    self._fw[e][id(s)] = s.n

    def finalize(self):
        with self.nc.Block() as blk:
            for name in ENGS:
                getattr(blk, name)((lambda lst: (lambda e: [f(e) for f in lst]))(self.plan[name]))


def build_nc(dbg_names=(), stop_after="all", end_dumps=()):
    nc = bass.Bass("TRN2", target_bir_lowering=False, detect_race_conditions=False)
    dram = {}

    def din(name, shape):
        dram[name] = nc.dram_tensor(name, list(shape), F32, kind="ExternalInput").ap()
        return dram[name]

    x_in = din("x", (BPC, T, D))
    out_t = nc.dram_tensor("out", [BPC, T, D], F32, kind="ExternalOutput").ap()
    for nm, sh in (("WQa", (D, D)), ("WKa", (D, D)), ("WOa", (D, D)), ("WINV", (D, D)),
                   ("UB", (D, R)), ("VB", (D, R)), ("WB", (D, R)),
                   ("UT", (D, R)), ("VT", (D, R)), ("WT", (D, R)), ("XTm", (D, R)),
                   ("BIASB", (D, 1)), ("BIAST", (D, 1)), ("ABI", (1, 1)), ("ATRI", (1, 1)),
                   ("RW", (D, 1)), ("RB", (1, 1)),
                   ("WQ", (D, D)), ("WK", (D, D)), ("WV", (D, D)), ("WOf", (D, D)),
                   ("W1", (D, FF)), ("W2", (FF, D)), ("LN1", (D, 1)), ("LN2", (D, 1))):
        din(nm, sh)
    for k in CONSTS:
        din(k, CONSTS[k].shape)
    DUMP = {}
    dbg = {}
    for name, width in dbg_names:
        dbg[name] = nc.dram_tensor("dbg_" + name, [128, width], F32, kind="ExternalOutput").ap()

    S = {}

    def sb(name, shape, dtype=F32):
        S[name] = nc.alloc_sbuf_tensor(name, list(shape), dtype).ap()
        return S[name]

    x_tok = sb("x_tok", (128, NT * D))
    acat = sb("acat", (128, ND * 449))
    outb = sb("outb", (128, D))
    wbias_row = sb("wbias_row", (1, D))
    wbias_rep = sb("wbias_rep", (128, D))
    for k in SB_CONSTS:
        sb("c" + k, CONSTS[k].shape)
    invc = sb("invc", (128, NT))
    invc2 = sb("invc2", (128, NT))
    abic = sb("abic", (128, 1))
    atric = sb("atric", (128, 1))
    orv = sb("orv", (128, 1))
    wrc = sb("wrc", (128, 1))
    srepc = sb("srepc", (128, 1))
    rbc = sb("rbc", (128, 1))
    rwsb = sb("rwsb", (128, ND))
    ln1sb = sb("ln1sb", (128, ND))
    ln2sb = sb("ln2sb", (128, ND))
    negs = sb("negs", (1, 448))
    scal = sb("scal", (1, 8))
    wraw = sb("wraw", (128, NT))
    stmu = sb("stmu", (128, NT))
    stro = sb("stro", (128, NT))
    b2c = sb("b2c", (128, NT))
    selm = sb("selm", (128, NT))
    ccum = sb("ccum", (128, NT))
    cols5 = sb("cols5", (128, 5 * NT))
    selcols = sb("selcols", (128, 10))
    selidx = sb("selidx", (128, 4), I32)
    carry = sb("carry", (128, 256))
    epsc = sb("epsc", (128, 4))
    XRS = sb("xrs", (128, 2 * D))
    CSG = sb("csg", (128, 2 * RS))
    ARW = 24000
    arena = sb("arena", (128, ARW))
    PS = [nc.alloc_psum_tensor("pb%d" % i, [128, 512], F32).ap() for i in range(8)]

    b = B(nc)
    sT, sV, sS, sG = (b.sems[e] for e in ("tensor", "vector", "scalar", "gpsimd"))
    qS, qA, qG = (b.qsems[e] for e in ("sync", "scalar", "gpsimd"))

    class Scope:
        def __init__(self):
            self.off, self.map = 0, {}

        def g(self, name, width):
            if name not in self.map:
                assert self.off + width <= ARW, (name, self.off, width)
                self.map[name] = (self.off, width)
                self.off += width
            o, wd = self.map[name]
            assert wd == width
            return arena[:, o:o + wd]

        def row(self, name, width):
            return self.g(name, width)[0:1, :]

    def tap(name, ap):
        if name in dbg:
            for sem in (sT, sV, sS, sG):
                b.w("sync", sem, sem.n)
            b.dma("sync", lambda e, ap=ap, name=name: e.dma_start(
                out=dbg[name][:ap.shape[0], :ap.shape[1]], in_=ap))

    for k in SB_CONSTS:
        b.dma("sync", lambda e, k=k: e.dma_start(out=S["c" + k][:], in_=dram[k][:, :]))
    for c in range(ND):
        b.dma("sync", lambda e, c=c: e.dma_start(out=rwsb[:, c:c + 1], in_=dram["RW"][c * 128:(c + 1) * 128, :]))
        b.dma("sync", lambda e, c=c: e.dma_start(out=ln1sb[:, c:c + 1], in_=dram["LN1"][c * 128:(c + 1) * 128, :]))
        b.dma("sync", lambda e, c=c: e.dma_start(out=ln2sb[:, c:c + 1], in_=dram["LN2"][c * 128:(c + 1) * 128, :]))
    b.dma("sync", lambda e: e.dma_start(out=scal[:, 0:1], in_=dram["ABI"][:, :]))
    b.dma("sync", lambda e: e.dma_start(out=scal[:, 1:2], in_=dram["ATRI"][:, :]))
    t_sc3 = b.dma("sync", lambda e: e.dma_start(out=scal[:, 2:3], in_=dram["RB"][:, :]))
    t_x0 = b.dma("sync", lambda e: e.dma_start(
        out=x_tok.rearrange("q (n d) -> q n d", d=D),
        in_=x_in[0].rearrange("(n p) d -> p n d", p=128)))
    t_outinit = b.dma("gpsimd", lambda e: e.dma_start(out=out_t[1], in_=x_in[1]))
    b.v(lambda e: e.memset(epsc[:, 0:1], 1e-5))
    b.v(lambda e: e.memset(epsc[:, 1:2], 1e-6))
    b.v(lambda e: e.memset(epsc[:, 2:3], 1.0 / D))

    pre = Scope()
    SW = {}
    for nm in ("UB", "UT", "XTm"):
        SW[nm] = pre.g("sw_" + nm, ND * R)
        for c in range(ND):
            b.dma("sync", lambda e, nm=nm, c=c: e.dma_start(
                out=SW[nm][:, c * R:(c + 1) * R], in_=dram[nm][c * 128:(c + 1) * 128, :]))
    QCAT = pre.g("qcat", ND * 192)
    KCAT = pre.g("kcat", ND * 192)
    VW = pre.g("vw", ND * 128)
    for c in range(ND):
        b.dma("sync", lambda e, c=c: e.dma_start(
            out=QCAT[:, c * 192:c * 192 + R], in_=dram["VB"][c * 128:(c + 1) * 128, :]))
        b.dma("sync", lambda e, c=c: e.dma_start(
            out=QCAT[:, c * 192 + 128:c * 192 + 192], in_=dram["VT"][c * 128:(c + 1) * 128, :]))
        b.dma("sync", lambda e, c=c: e.dma_start(
            out=KCAT[:, c * 192:c * 192 + R], in_=dram["WB"][c * 128:(c + 1) * 128, :]))
        b.dma("sync", lambda e, c=c: e.dma_start(
            out=KCAT[:, c * 192 + 128:c * 192 + 192], in_=dram["WT"][c * 128:(c + 1) * 128, :]))
        b.dma("sync", lambda e, c=c: e.dma_start(
            out=VW[:, c * 128:c * 128 + R], in_=dram["VB"][c * 128:(c + 1) * 128, :]))
        b.dma("sync", lambda e, c=c: e.dma_start(
            out=VW[:, c * 128 + R:(c + 1) * 128], in_=dram["WB"][c * 128:(c + 1) * 128, :]))
    bbcol = pre.g("biasb", ND)
    btcol = pre.g("biast", ND)
    for c in range(ND):
        b.dma("sync", lambda e, c=c: e.dma_start(out=bbcol[:, c:c + 1], in_=dram["BIASB"][c * 128:(c + 1) * 128, :]))
        b.dma("sync", lambda e, c=c: e.dma_start(out=btcol[:, c:c + 1], in_=dram["BIAST"][c * 128:(c + 1) * 128, :]))
    BW = pre.g("bigw", ND * D)
    WOT = pre.g("wot", ND * D)
    BCOMB = pre.g("bcomb", ND)
    WORC = pre.g("worc", ND)
    WBIC = pre.g("wbic", ND)
    DUMP["acat"] = acat
    DUMP["qcat"] = QCAT
    DUMP["kcat"] = KCAT
    DUMP["wbic"] = WBIC
    DUMP["bcomb"] = BCOMB
    DUMP["bbcol"] = bbcol
    DUMP["btcol"] = btcol
    DUMP["rwsb"] = rwsb
    DUMP["ln1sb"] = ln1sb
    t_pre_loads = qS.n

    _stopped = []

    def emit_dumps():
        for sem in list(b.sems.values()) + [qA, qG]:
            b.w("sync", sem, sem.n)
        for nm in end_dumps:
            ap = DUMP[nm]
            dd = nc.dram_tensor("dump_" + nm, [ap.shape[0], ap.shape[1]], F32,
                                kind="ExternalOutput").ap()
            b.dma("sync", lambda e, ap=ap, dd=dd: e.dma_start(out=dd[:, :], in_=ap))
        b.w("sync", qS, qS.n)

    def maybe_stop(tag):
        if stop_after == tag:
            emit_dumps()
            b.w("sync", qG, qG.n)
            b.finalize()
            _stopped.append(True)
        return bool(_stopped)

    if maybe_stop("load"):
        return nc, dram
    b.w("tensor", qS, t_sc3)
    b.t(lambda e: e.matmul(PS[0][:, 0:1], S["cONESROW"][:, :], scal[0:1, 0:1], start=True, stop=True))
    b.t(lambda e: e.matmul(PS[0][:, 1:2], S["cONESROW"][:, :], scal[0:1, 1:2], start=True, stop=True))
    tm = b.t(lambda e: e.matmul(PS[0][:, 2:3], S["cONESROW"][:, :], scal[0:1, 2:3], start=True, stop=True))
    b.w("vector", sT, tm)
    b.v(lambda e: e.tensor_copy(abic[:], PS[0][:, 0:1]))
    b.v(lambda e: e.tensor_copy(atric[:], PS[0][:, 1:2]))
    b.v(lambda e: e.tensor_copy(rbc[:], PS[0][:, 2:3]))
    b.w("vector", qS, t_pre_loads)
    b.v(lambda e: e.tensor_scalar(BCOMB[:, :], bbcol[:, :], abic[:, 0:1], None, ALU.mult))
    b.v(lambda e: e.tensor_add(BCOMB[:, :], BCOMB[:, :], bbcol[:, :]))
    b.v(lambda e: e.scalar_tensor_tensor(BCOMB[:, :], btcol[:, :], atric[:, 0:1], BCOMB[:, :], ALU.mult, ALU.add))
    b.v(lambda e: e.reciprocal(invc[:], S["cCOUNTS"][:, :]))
    tv = b.v(lambda e: e.tensor_mul(invc2[:], invc[:], invc[:]))
    b.w("tensor", sV, tv)

    if maybe_stop("p1"):
        return nc, dram

    def ld_bigw(nm):
        t = None
        b.w("sync", sT, sT.n)
        for c in range(ND):
            t = b.dma("sync", lambda e, nm=nm, c=c: e.dma_start(
                out=BW[:, c * D:(c + 1) * D], in_=dram[nm][c * 128:(c + 1) * 128, :]))
        return t

    def wsmall(dst_fn, vsm, scale_col=None, tilepos=None):
        ev = {}
        for m in range(ND):
            slot = (PS[1] if m % 2 == 0 else PS[6])[:, 0:R]
            oslot = slot if tilepos is None else slot[64:128, :]
            if m >= 2:
                b.w("tensor", sV, ev[m - 2])
            for j in range(ND):
                b.t(lambda e, m=m, j=j, oslot=oslot: e.matmul(
                    oslot, BW[:, j * D + m * 128:j * D + (m + 1) * 128],
                    vsm[:, j * R:(j + 1) * R], start=(j == 0), stop=(j == ND - 1),
                    tile_position=tilepos))
            tmm = sT.n
            b.w("vector", sT, tmm)
            dst = dst_fn(m)
            if scale_col is None:
                ev[m] = b.v(lambda e, dst=dst, oslot=oslot: e.tensor_copy(dst, oslot))
            else:
                ev[m] = b.v(lambda e, dst=dst, oslot=oslot, sc=scale_col: e.tensor_scalar(
                    dst, oslot, sc, None, ALU.mult))
        b.w("tensor", sV, sV.n)

    tw = ld_bigw("WINV")
    b.w("tensor", qS, tw)
    ev = {}
    for m in range(ND):
        slot = (PS[1] if m % 2 == 0 else PS[6])[:, 0:128]
        if m >= 2:
            b.w("tensor", sV, ev[m - 2])
        for j in range(ND):
            b.t(lambda e, m=m, j=j, slot=slot: e.matmul(
                slot, BW[:, j * D + m * 128:j * D + (m + 1) * 128],
                VW[:, j * 128:(j + 1) * 128], start=(j == 0), stop=(j == ND - 1)))
        tmm = sT.n
        b.w("vector", sT, tmm)
        b.v(lambda e, m=m, slot=slot: e.tensor_copy(
            KCAT[:, m * 192 + R:m * 192 + 128], slot[:, 0:64]))
        ev[m] = b.v(lambda e, m=m, slot=slot: e.tensor_scalar(
            QCAT[:, m * 192 + R:m * 192 + 128], slot[:, 64:128], abic[:, 0:1], None, ALU.mult))
    b.w("tensor", sV, sV.n)
    if maybe_stop("p2"):
        return nc, dram
    for wname, coff in (("WQa", 0), ("WKa", 192)):
        cat = QCAT if wname == "WQa" else KCAT
        tw = ld_bigw(wname)
        b.w("tensor", qS, tw)
        ev = {}
        for m in range(ND):
            slot = (PS[1] if m % 2 == 0 else PS[6])[:, 0:192]
            if m >= 2:
                b.w("tensor", sV, ev[m - 2])
            for j in range(ND):
                b.t(lambda e, m=m, j=j, slot=slot, cat=cat: e.matmul(
                    slot, BW[:, j * D + m * 128:j * D + (m + 1) * 128],
                    cat[:, j * 192:(j + 1) * 192], start=(j == 0), stop=(j == ND - 1)))
            tmm = sT.n
            b.w("vector", sT, tmm)
            ev[m] = b.v(lambda e, m=m, slot=slot, coff=coff: e.tensor_copy(
                acat[:, 449 * m + coff:449 * m + coff + 192], slot))
        b.w("tensor", sV, sV.n)
    for m in range(ND):
        b.v(lambda e, m=m: e.tensor_copy(acat[:, 449 * m + 384:449 * m + 448],
                                         SW["XTm"][:, m * R:(m + 1) * R]))
        b.v(lambda e, m=m: e.tensor_copy(acat[:, 449 * m + 448:449 * m + 449],
                                         rwsb[:, m:m + 1]))
    if maybe_stop("p3"):
        return nc, dram
    tw = ld_bigw("WOa")
    b.w("tensor", qS, tw)
    ev = {}
    for m in range(ND):
        slot = (PS[1] if m % 2 == 0 else PS[6])[:, 128:129]
        if m >= 2:
            b.w("tensor", sV, ev[m - 2])
        for k in range(ND):
            b.t(lambda e, m=m, k=k, slot=slot: e.matmul(
                slot, BW[:, k * D + m * 128:k * D + (m + 1) * 128],
                rwsb[:, k:k + 1], start=(k == 0), stop=(k == ND - 1)))
        tmm = sT.n
        b.w("vector", sT, tmm)
        ev[m] = b.v(lambda e, m=m, slot=slot: e.tensor_copy(WORC[:, m:m + 1], slot))
    b.w("tensor", sV, sV.n)
    if maybe_stop("p3a"):
        return nc, dram
    WOTV = WOT.rearrange("p (j x) -> p j x", x=D)
    ev = {}
    for c in range(ND):
        bankA = PS[2] if c % 2 == 0 else PS[4]
        bankB = PS[3] if c % 2 == 0 else PS[5]
        if c >= 2:
            b.w("tensor", sV, ev[c - 2])
        for j in range(ND):
            ps = (bankA if j < 4 else bankB)[:, (j % 4) * 128:(j % 4) * 128 + 128]
            b.t(lambda e, c=c, j=j, ps=ps: e.transpose(
                ps, BW[:, c * D + j * 128:c * D + (j + 1) * 128], S["cIDENT"][:, :]))
        tmm = sT.n
        b.w("vector", sT, tmm)
        b.v(lambda e, c=c, bankA=bankA: e.tensor_copy(
            WOTV[:, 0:4, c * 128:(c + 1) * 128], bankA[:, 0:512]))
        ev[c] = b.v(lambda e, c=c, bankB=bankB: e.tensor_copy(
            WOTV[:, 4:8, c * 128:(c + 1) * 128], bankB[:, 0:512]))
    b.w("tensor", sV, sV.n)
    if isinstance(stop_after, str) and stop_after.startswith("wot"):
        emit_dumps()
        b.w("sync", qG, qG.n)
        b.finalize()
        return nc, dram
    if maybe_stop("p4"):
        return nc, dram
    for n2 in range(2):
        slot = (PS[3] if n2 == 0 else PS[4])[0:64, 0:512]
        for j in range(ND):
            b.t(lambda e, n2=n2, j=j, slot=slot: e.matmul(
                slot, SW["UB"][:, j * R:(j + 1) * R],
                WOT[:, j * D + n2 * 512:j * D + (n2 + 1) * 512],
                start=(j == 0), stop=(j == ND - 1)))
        tmm = sT.n
        b.w("vector", sT, tmm)
        b.v(lambda e, n2=n2, slot=slot: e.tensor_copy(outb[0:64, n2 * 512:(n2 + 1) * 512], slot))
    for n2 in range(2):
        slot = (PS[5] if n2 == 0 else PS[6])[64:128, 0:512]
        for j in range(ND):
            b.t(lambda e, n2=n2, j=j, slot=slot: e.matmul(
                slot, SW["UT"][:, j * R:(j + 1) * R],
                WOT[:, j * D + n2 * 512:j * D + (n2 + 1) * 512],
                start=(j == 0), stop=(j == ND - 1), tile_position=(0, 64)))
        tmm = sT.n
        b.w("vector", sT, tmm)
        b.v(lambda e, n2=n2, slot=slot: e.tensor_scalar(
            outb[64:128, n2 * 512:(n2 + 1) * 512], slot, atric[64:128, 0:1], None, ALU.mult))
    b.w("tensor", sV, sV.n)
    slot = PS[1][0:64, 130:131]
    for j in range(ND):
        b.t(lambda e, j=j, slot=slot: e.matmul(
            slot, SW["UB"][:, j * R:(j + 1) * R], WORC[:, j:j + 1],
            start=(j == 0), stop=(j == ND - 1)))
    tmm = sT.n
    b.w("vector", sT, tmm)
    b.v(lambda e, slot=slot: e.tensor_copy(orv[0:64, 0:1], slot))
    slot = PS[6][64:128, 132:133]
    for j in range(ND):
        b.t(lambda e, j=j, slot=slot: e.matmul(
            slot, SW["UT"][:, j * R:(j + 1) * R], WORC[:, j:j + 1],
            start=(j == 0), stop=(j == ND - 1), tile_position=(0, 64)))
    tmm = sT.n
    b.w("vector", sT, tmm)
    b.v(lambda e, slot=slot: e.tensor_scalar(orv[64:128, 0:1], slot, atric[64:128, 0:1], None, ALU.mult))
    b.w("tensor", sV, sV.n)
    if maybe_stop("p5"):
        return nc, dram
    ev = {}
    for m in range(ND):
        slot = (PS[1] if m % 2 == 0 else PS[6])[:, 134:135]
        if m >= 2:
            b.w("tensor", sV, ev[m - 2])
        for j in range(ND):
            b.t(lambda e, m=m, j=j, slot=slot: e.matmul(
                slot, WOT[:, j * D + m * 128:j * D + (m + 1) * 128],
                BCOMB[:, j:j + 1], start=(j == 0), stop=(j == ND - 1)))
        tmm = sT.n
        b.w("vector", sT, tmm)
        ev[m] = b.v(lambda e, m=m, slot=slot: e.tensor_copy(WBIC[:, m:m + 1], slot))
    b.w("tensor", sV, sV.n)
    for m in range(ND):
        slot = PS[2][0:1, 0:128]
        b.t(lambda e, m=m, slot=slot: e.transpose(slot, WBIC[:, m:m + 1], S["cIDENT"][:, :]))
        tmm = sT.n
        b.w("vector", sT, tmm)
        tv = b.v(lambda e, m=m, slot=slot: e.tensor_copy(wbias_row[0:1, m * 128:(m + 1) * 128], slot))
        b.w("tensor", sV, tv)
    for n2 in range(2):
        slot = (PS[3] if n2 == 0 else PS[4])[:, 0:512]
        b.t(lambda e, n2=n2, slot=slot: e.matmul(
            slot, S["cONESROW"][:, :], wbias_row[0:1, n2 * 512:(n2 + 1) * 512], start=True, stop=True))
        tmm = sT.n
        b.w("vector", sT, tmm)
        b.v(lambda e, n2=n2, slot=slot: e.tensor_copy(wbias_rep[:, n2 * 512:(n2 + 1) * 512], slot))
    slot = PS[1][0:1, 136:137]
    for m in range(ND):
        b.t(lambda e, m=m, slot=slot: e.matmul(
            slot, WBIC[:, m:m + 1], rwsb[:, m:m + 1], start=(m == 0), stop=(m == ND - 1)))
    slot2 = PS[1][0:1, 138:139]
    for m in range(ND):
        b.t(lambda e, m=m, slot2=slot2: e.matmul(
            slot2, rwsb[:, m:m + 1], S["cONESCOL"][:, 0:1], start=(m == 0), stop=(m == ND - 1)))
    tmm = sT.n
    b.w("vector", sT, tmm)
    b.v(lambda e, slot=slot: e.tensor_copy(scal[0:1, 3:4], slot))
    tv = b.v(lambda e, slot2=slot2: e.tensor_copy(scal[0:1, 4:5], slot2))
    b.w("tensor", sV, tv)
    b.t(lambda e: e.matmul(PS[1][:, 140:141], S["cONESROW"][:, :], scal[0:1, 3:4], start=True, stop=True))
    tmm = b.t(lambda e: e.matmul(PS[1][:, 141:142], S["cONESROW"][:, :], scal[0:1, 4:5], start=True, stop=True))
    b.w("vector", sT, tmm)
    b.v(lambda e: e.tensor_copy(wrc[:], PS[1][:, 140:141]))
    b.v(lambda e: e.tensor_copy(srepc[:], PS[1][:, 141:142]))
    if maybe_stop("p6"):
        return nc, dram
    slot = PS[7][0:1, 0:448]
    for c in range(ND):
        b.t(lambda e, c=c, slot=slot: e.matmul(
            slot, S["cONESCOL"][:, :], acat[:, c * 449:c * 449 + 448],
            start=(c == 0), stop=(c == ND - 1)))
    tmm = sT.n
    b.w("vector", sT, tmm)
    b.v(lambda e, slot=slot: e.tensor_scalar(negs[:, :], slot, -1.0, None, ALU.mult))
    b.fence()
    tap("acat0", acat[:, 0:448])
    tap("acat7", acat[:, 7 * 448:8 * 448])
    tap("outb", outb[:, :])
    tap("wbias_rep", wbias_rep[:, 0:D])
    tap("orv", orv[:, 0:1])
    tap("negs", negs[0:1, :])

    aS = Scope()

    def phaseA(bi, t_xload):
        XT2 = [aS.g("xT0", D), aS.g("xT1", D)]
        XSQ = aS.g("xsq", D)
        PP = [aS.g("pp0", 449), aS.g("pp1", 449)]
        CUM = aS.g("cum", 256)
        RT2 = [aS.g("rt0", 128), aS.g("rt1", 128)]
        MUR = aS.row("murow", 128)
        STC = aS.g("stc", 8)
        BNS = aS.g("bns", 12)
        b.w("tensor", qS, t_xload)
        tv = b.v(lambda e: e.memset(carry[:, :], 0.0))
        b.w("tensor", sV, tv)
        ev_prev = {}
        for t in range(NT):
            xT = XT2[t % 2]
            pp = PP[t % 2]
            rt = RT2[t % 2]
            if t >= 1:
                b.w("tensor", sV, ev_prev["xT"])
            for c in range(ND):
                ps = (PS[2] if c < 4 else PS[3])[:, (c % 4) * 128:(c % 4) * 128 + 128]
                b.t(lambda e, t=t, c=c, ps=ps: e.transpose(
                    ps, x_tok[:, t * D + c * 128:t * D + (c + 1) * 128], S["cIDENT"][:, :]))
            tm_tr = sT.n
            b.w("vector", sT, tm_tr)
            b.v(lambda e, xT=xT: e.tensor_copy(xT[:, 0:512], PS[2][:, 0:512]))
            ev_xT = b.v(lambda e, xT=xT: e.tensor_copy(xT[:, 512:1024], PS[3][:, 0:512]))
            ev_prev["xT"] = ev_xT
            b.v(lambda e, t=t: e.bn_stats(BNS[:, 0:6], x_tok[:, t * D:t * D + 512]))
            b.v(lambda e, t=t: e.bn_stats(BNS[:, 6:12], x_tok[:, t * D + 512:(t + 1) * D]))
            ev_bn = b.v(lambda e: e.bn_aggr(STC[:, 0:2], BNS[:, 0:12]))
            b.w("gpsimd", sT, tm_tr)
            b.w("gpsimd", sV, ev_bn)
            t_wb = b.dma("gpsimd", lambda e, t=t: e.dma_start(
                out=x_tok[:, t * D:(t + 1) * D], in_=wbias_rep[:, :], accum_op=ALU.add))
            b.w("scalar", sV, ev_bn)
            sq2 = b.s(lambda e: e.activation(STC[:, 2:3], STC[:, 1:2], AF.Sqrt, bias=epsc[:, 0:1]))
            b.w("vector", sS, sq2)
            ev_rstd = b.v(lambda e: e.reciprocal(STC[:, 3:4], STC[:, 2:3]))
            if t >= 1:
                b.w("tensor", sV, ev_prev["pp"])
            b.w("tensor", sV, ev_xT)
            for c in range(ND):
                b.t(lambda e, c=c, xT=xT: e.matmul(
                    PS[0][:, 0:449], xT[:, c * 128:(c + 1) * 128],
                    acat[:, c * 449:(c + 1) * 449], start=(c == 0), stop=False))
            b.w("tensor", sV, ev_bn)
            tmu = b.t(lambda e: e.transpose(PS[5][0:1, 0:128], STC[:, 0:1], S["cIDENT"][:, :]))
            b.w("vector", sT, tmu)
            ev_mur = b.v(lambda e: e.tensor_copy(MUR[:, :], PS[5][0:1, 0:128]))
            b.w("tensor", sV, ev_mur)
            b.t(lambda e: e.matmul(PS[0][:, 0:448], MUR[:, :], negs[0:1, :], start=False, stop=True))
            tm_corr = sT.n
            b.w("vector", sT, tm_corr)
            b.w("vector", sV, ev_rstd)
            ev_pp = b.v(lambda e, pp=pp: e.tensor_scalar(
                pp[:, 0:448], PS[0][:, 0:448], STC[:, 3:4], None, ALU.mult))
            ev_prev["pp"] = ev_pp
            if t >= 1:
                b.w("tensor", sV, ev_prev["cum"])
            b.w("tensor", sV, ev_pp)
            b.t(lambda e, pp=pp: e.matmul(PS[4][:, 0:256], S["cTRIT"][:, :], pp[:, 192:448], start=True, stop=True))
            tm_cum = b.t(lambda e, pp=pp: e.matmul(PS[4][:, 256:512], S["cONES128"][:, :], pp[:, 192:448], start=True, stop=True))
            b.w("vector", sT, tm_cum)
            b.v(lambda e: e.tensor_add(CUM[:, :], PS[4][:, 0:256], carry[:, :]))
            ev_cum = b.v(lambda e: e.tensor_add(carry[:, :], carry[:, :], PS[4][:, 256:512]))
            ev_prev["cum"] = ev_cum
            b.v(lambda e, pp=pp: e.tensor_mul(XSQ[:, 0:64], pp[:, 0:64], CUM[:, 0:64]))
            b.v(lambda e, pp=pp: e.tensor_mul(XSQ[:, 64:128], pp[:, 64:128], CUM[:, 64:128]))
            b.v(lambda e: e.tensor_add(XSQ[:, 0:64], XSQ[:, 0:64], XSQ[:, 64:128]))
            b.v(lambda e, rt=rt, t=t: e.tensor_scalar(
                rt[:, 0:64], XSQ[:, 0:64], invc[:, t:t + 1], None, ALU.mult))
            b.v(lambda e, pp=pp: e.tensor_mul(XSQ[:, 64:128], pp[:, 128:192], CUM[:, 128:192]))
            ev_R = b.v(lambda e, rt=rt, t=t: e.scalar_tensor_tensor(
                rt[:, 64:128], XSQ[:, 64:128], invc2[:, t:t + 1], CUM[:, 192:256], ALU.mult, ALU.mult))
            b.w("tensor", sV, ev_R)
            trt = b.t(lambda e, rt=rt: e.transpose(PS[5][:, 128:256], rt[:, :], S["cIDENT"][:, :]))
            b.w("vector", sT, trt)
            ev_rt = b.v(lambda e, rt=rt: e.tensor_copy(rt[:, :], PS[5][:, 128:256]))
            b.w("tensor", sV, ev_rt)
            b.t(lambda e, rt=rt: e.matmul(PS[5][:, 497:498], rt[:, :], orv[:, 0:1], start=True, stop=True))
            tm_w = sT.n
            b.w("vector", sT, tm_w)
            b.v(lambda e, t=t, pp=pp: e.tensor_scalar(
                wraw[:, t:t + 1], PS[0][:, 448:449], wrc[:, 0:1], None, ALU.add))
            ev_wraw = b.v(lambda e, t=t: e.tensor_add(
                wraw[:, t:t + 1], wraw[:, t:t + 1], PS[5][:, 497:498]))
            ev_prev["cum"] = ev_wraw
            ev_prev["pp"] = ev_wraw
            if t >= 1:
                b.w("tensor", sV, ev_prev["out"])
            for n2 in range(2):
                b.t(lambda e, rt=rt, n2=n2: e.matmul(
                    (PS[6] if n2 == 0 else PS[7])[:, 0:512], rt[:, :],
                    outb[:, n2 * 512:(n2 + 1) * 512], start=True, stop=True))
            tm_out = sT.n
            b.w("vector", sT, tm_out)
            b.w("vector", qG, t_wb)
            b.v(lambda e, t=t: e.tensor_add(
                x_tok[:, t * D:t * D + 512], x_tok[:, t * D:t * D + 512], PS[6][:, 0:512]))
            ev_out = b.v(lambda e, t=t: e.tensor_add(
                x_tok[:, t * D + 512:(t + 1) * D], x_tok[:, t * D + 512:(t + 1) * D], PS[7][:, 0:512]))
            ev_prev["out"] = ev_out
        b.fence()
        tap("xA%d" % bi, x_tok[:, 0:4096])
        tap("wrawA%d" % bi, wraw[:, :])
        tap("xrA%d" % bi, b2c[:, :])
        tap("rorvA%d" % bi, ccum[:, :])
        tap("wrcA%d" % bi, wrc[:, :])
        tap("srepA%d" % bi, srepc[:, :])

    fS = Scope()

    def phaseFpre(bi):
        DUMP = fS.g("dump", D)
        WREP = fS.g("wrep", T)
        ST2 = [fS.g("st0", RS), fS.g("st1", RS)]
        WTR = fS.g("wtr", 128)
        WROW = fS.row("wrow", T)
        BASE2 = [fS.g("base0", D), fS.g("base1", D)]
        AC = fS.g("acol", NT + 8)
        BC = fS.g("bcol", NT)
        TMP = fS.g("ftmp", NT)
        BN2 = fS.g("bn2", 12)
        STAT2 = fS.g("stat2", 2)
        for t in range(NT):
            b.v(lambda e, t=t: e.bn_stats(BN2[:, 0:6], x_tok[:, t * D:t * D + 512]))
            b.v(lambda e, t=t: e.bn_stats(BN2[:, 6:12], x_tok[:, t * D + 512:(t + 1) * D]))
            b.v(lambda e: e.bn_aggr(STAT2[:, :], BN2[:, 0:12]))
            b.v(lambda e, t=t: e.tensor_copy(stmu[:, t:t + 1], STAT2[:, 0:1]))
            b.v(lambda e, t=t: e.tensor_copy(stro[:, t:t + 1], STAT2[:, 1:2]))
        b.w("scalar", sV, sV.n)
        ts_ = b.s(lambda e: e.activation(stro[:, :], stro[:, :], AF.Sqrt, bias=epsc[:, 0:1]))
        b.w("vector", sS, ts_)
        b.v(lambda e: e.reciprocal(stro[:, :], stro[:, :]))
        b.v(lambda e: e.tensor_mul(b2c[:, :], stmu[:, :], stro[:, :]))
        b.v(lambda e: e.tensor_scalar(b2c[:, :], b2c[:, :], -1.0, None, ALU.mult))
        b.v(lambda e: e.tensor_scalar(TMP[:, :], stmu[:, :], srepc[:, 0:1], None, ALU.mult))
        b.v(lambda e: e.tensor_sub(TMP[:, :], wraw[:, :], TMP[:, :]))
        b.v(lambda e: e.tensor_mul(TMP[:, :], TMP[:, :], stro[:, :]))
        tv_w = b.v(lambda e: e.tensor_scalar(TMP[:, :], TMP[:, :], rbc[:, 0:1], None, ALU.add))
        b.w("tensor", sV, tv_w)
        tmm = b.t(lambda e: e.transpose(PS[5][0:16, 0:128], TMP[:, 0:16], S["cIDENT"][:, :]))
        b.w("vector", sT, tmm)
        tv = b.v(lambda e: e.tensor_copy(WTR[0:16, :], PS[5][0:16, 0:128]))
        b.w("gpsimd", sV, tv)
        tg = b.dma("gpsimd", lambda e: e.dma_start(out=WROW[:, :], in_=WTR[0:16, :]))
        b.w("tensor", qG, tg)
        wrep_ev = {}
        for n4 in range(4):
            ps = (PS[2] if n4 % 2 == 0 else PS[3])[:, 0:512]
            if n4 >= 2:
                b.w("tensor", sV, wrep_ev[n4 - 2])
            b.t(lambda e, n4=n4, ps=ps: e.matmul(
                ps, S["cONESROW"][:, :], WROW[0:1, n4 * 512:(n4 + 1) * 512], start=True, stop=True))
            tmm = sT.n
            b.w("vector", sT, tmm)
            wrep_ev[n4] = b.v(lambda e, n4=n4, ps=ps: e.tensor_copy(WREP[:, n4 * 512:(n4 + 1) * 512], ps))
        SMC = fS.g("smc", NT)
        RNKD = fS.g("rnkd", T)
        for j in range(NT):
            b.v(lambda e, j=j: e.tensor_scalar(
                RNKD[:, :], WREP[:, :], TMP[:, j:j + 1], 0.0, ALU.is_lt, ALU.add,
                accum_out=SMC[:, j:j + 1]))
        tv = b.v(lambda e: e.tensor_scalar(selm[:, :], SMC[:, :], float(T - KC) - 0.5, None, ALU.is_gt))
        b.w("tensor", sV, tv)
        b.t(lambda e: e.matmul(PS[4][:, 0:16], S["cTRIST"][:, :], selm[:, :], start=True, stop=False))
        tmm = b.t(lambda e: e.matmul(PS[5][0:1, 0:16], S["cONESCOL"][:, :], selm[:, :], start=True, stop=True))
        b.w("vector", sT, tmm)
        tv = b.v(lambda e: e.tensor_copy(WTR[0:1, 0:16], PS[5][0:1, 0:16]))
        b.w("tensor", sV, tv)
        tmm = b.t(lambda e: e.transpose(PS[5][0:16, 16:17], WTR[0:1, 0:16], S["cIDENT"][0:1, 0:1]))
        b.w("vector", sT, tmm)
        tv = b.v(lambda e: e.tensor_copy(WTR[0:16, 16:17], PS[5][0:16, 16:17]))
        b.w("tensor", sV, tv)
        tmm = b.t(lambda e: e.matmul(PS[5][0:16, 17:18], S["cTRIST"][0:16, 0:16], WTR[0:16, 16:17], start=True, stop=True))
        b.w("vector", sT, tmm)
        tv = b.v(lambda e: e.tensor_copy(WTR[0:16, 18:19], PS[5][0:16, 17:18]))
        b.w("tensor", sV, tv)
        tmm = b.t(lambda e: e.transpose(PS[5][0:1, 19:35], WTR[0:16, 18:19], S["cIDENT"][0:16, 0:16]))
        b.w("vector", sT, tmm)
        tv = b.v(lambda e: e.tensor_copy(WTR[0:1, 20:36], PS[5][0:1, 19:35]))
        b.w("tensor", sV, tv)
        b.t(lambda e: e.matmul(PS[4][:, 0:16], S["cONESROW"][:, :], WTR[0:1, 20:36], start=False, stop=True))
        tmm = sT.n
        b.w("vector", sT, tmm)
        tv_cc = b.v(lambda e: e.tensor_copy(ccum[:, :], PS[4][:, 0:16]))
        c4v = cols5.rearrange("p (t f) -> p t f", f=5)
        b.v(lambda e: e.tensor_copy(c4v[:, :, 0], S["cARANGE"][:, :]))
        b.v(lambda e: e.tensor_copy(c4v[:, :, 1], TMP[:, :]))
        b.v(lambda e: e.tensor_copy(c4v[:, :, 2], stro[:, :]))
        b.v(lambda e: e.tensor_copy(c4v[:, :, 3], b2c[:, :]))
        b.v(lambda e: e.tensor_copy(c4v[:, :, 4], S["cONES128"][:, 0:NT]))
        ev_st = {}
        for j in range(NT):
            st = ST2[j % 2]
            if j >= 2:
                b.w("vector", sT, ev_st[j - 2])
            b.v(lambda e, st=st, j=j: e.tensor_scalar(
                st[:, :], S["cIOTAR"][:, :], ccum[:, j:j + 1], None, ALU.is_equal))
            tv = b.v(lambda e, st=st, j=j: e.tensor_scalar(
                st[:, :], st[:, :], selm[:, j:j + 1], None, ALU.mult))
            b.w("tensor", sV, tv)
            for rb in range(2):
                b.t(lambda e, st=st, rb=rb, j=j: e.matmul(
                    PS[2][:, 5 * rb:5 * rb + 5], st[:, rb * 128:(rb + 1) * 128],
                    cols5[:, j * 5:(j + 1) * 5], start=(j == 0 and rb == 0),
                    stop=(j == NT - 1 and rb == 1)))
            ev_st[j] = sT.n
        b.w("vector", sT, sT.n)
        tv = b.v(lambda e: e.tensor_copy(selcols[:, :], PS[2][:, 0:10]))
        b.w("vector", sV, tv)
        for rb in range(2):
            b.v(lambda e, rb=rb, bi=bi: e.tensor_scalar(
                AC[:, 16 + rb:17 + rb], selcols[:, 5 * rb:5 * rb + 1], float(bi * T), None, ALU.add))
            b.v(lambda e, rb=rb: e.tensor_scalar(
                AC[:, 18 + rb:19 + rb], selcols[:, 5 * rb + 4:5 * rb + 5], 1.0, -4096.0, ALU.subtract, ALU.mult))
            b.v(lambda e, rb=rb: e.tensor_add(AC[:, 18 + rb:19 + rb], AC[:, 18 + rb:19 + rb], AC[:, 16 + rb:17 + rb]))
            b.v(lambda e, rb=rb: e.tensor_copy(selidx[:, rb:rb + 1], AC[:, 16 + rb:17 + rb]))
        tv_idx = b.v(lambda e: e.tensor_copy(selidx[:, 2:4], AC[:, 18:20]))
        b.v(lambda e: e.tensor_mul(AC[:, 0:NT], selm[:, :], stro[:, :]))
        b.v(lambda e: e.tensor_sub(AC[:, 0:NT], stro[:, :], AC[:, 0:NT]))
        b.v(lambda e: e.tensor_scalar(AC[:, 0:NT], AC[:, 0:NT], 1.0, None, ALU.add))
        b.v(lambda e: e.tensor_mul(BC[:, :], selm[:, :], b2c[:, :]))
        tv_ab = b.v(lambda e: e.tensor_sub(BC[:, :], b2c[:, :], BC[:, :]))
        b.w("scalar", sV, tv_ab)
        st_prev = {}
        t_store = None
        for t in range(NT):
            bb = BASE2[t % 2]
            if t >= 2:
                b.w("scalar", qS, st_prev[t - 2])
            ts_ = b.s(lambda e, t=t, bb=bb: e.activation(
                bb[:, :], x_tok[:, t * D:(t + 1) * D], AF.Identity,
                bias=BC[:, t:t + 1], scale=AC[:, t:t + 1]))
            b.w("sync", sS, ts_)
            t_store = b.dma("sync", lambda e, t=t, bb=bb, bi=bi: e.dma_start(
                out=out_t[bi, t * 128:(t + 1) * 128, :], in_=bb[:, :]))
            st_prev[t] = t_store
        b.w("gpsimd", qS, t_store)
        b.w("gpsimd", sV, tv_idx)
        b.w("gpsimd", qG, t_outinit)
        out_flat = out_t.rearrange("b t d -> (b t) d")
        for rb in range(2):
            b.emit("gpsimd", lambda e, rb=rb, out_flat=out_flat: e.indirect_dma_start(
                out=XRS[:, rb * D:(rb + 1) * D], out_offset=None,
                in_=out_flat, in_offset=IndirectOffsetOnAxis(ap=selidx[:, rb:rb + 1], axis=0),
            ), qG, 16)
            b.emit("gpsimd", lambda e, rb=rb: e.indirect_dma_start(
                out=CSG[:, rb * RS:(rb + 1) * RS], out_offset=None,
                in_=dram["CSTAB"][:, :], in_offset=IndirectOffsetOnAxis(ap=selidx[:, rb:rb + 1], axis=0),
            ), qG, 16)
        b.fence()
        tap("w%d" % bi, TMP[:, :])
        tap("selm%d" % bi, selm[:, :])
        tap("ccum%d" % bi, ccum[:, :])
        tap("selcols%d" % bi, selcols[:, :])
        tap("xrs%d" % bi, XRS[:, :])

    fmS = Scope()

    def phaseFmain(bi, xload_next):
        m = fmS
        XLN = m.g("xln", 2 * D)
        HTOK = m.g("htok", 2 * D)
        HT = m.g("hT", 2048)
        QT = m.g("qT", 2048)
        KT = m.g("kT", 2048)
        VT_ = m.g("vT", 2048)
        CTX = m.g("ctxT", 2048)
        COS2 = m.g("cos2", RS)
        SIN2 = m.g("sin2", RS)
        ASB = m.g("asb", RS)
        ATB = m.g("atb", 2 * RS)
        VTOK = m.g("vtok", 2 * 128)
        H1G = m.g("h1g", 4 * RS)
        WBUF = [m.g("wb0", D), m.g("wb1", D)]
        WBUF2 = [m.g("w20", D), m.g("w21", D)]
        RCOL = m.g("rcol", 8)
        FIN = m.g("fin", D)
        DUMP.update(dict(xln=XLN, htok=HTOK, hT=HT, qT=QT, kT=KT, vT=VT_,
                         ctx=CTX, cos2=COS2, sin2=SIN2, h1g=H1G, fin=FIN, rcol=RCOL))
        if xload_next is not None:
            b.w("sync", sS, sS.n)
            t_xn = b.dma("sync", lambda e, xload_next=xload_next: e.dma_start(
                out=x_tok.rearrange("q (n d) -> q n d", d=D),
                in_=x_in[xload_next].rearrange("(n p) d -> p n d", p=128)))
        else:
            t_xn = None
        for rb in range(2):
            b.s(lambda e, rb=rb: e.activation(
                XLN[:, rb * D:(rb + 1) * D], XRS[:, rb * D:(rb + 1) * D], AF.Identity,
                scale=selcols[:, 5 * rb + 2:5 * rb + 3], bias=selcols[:, 5 * rb + 3:5 * rb + 4]))
        for rb in range(2):
            b.s(lambda e, rb=rb: e.activation(HTOK[:, rb * D:(rb + 1) * D], XLN[:, rb * D:(rb + 1) * D],
                                              AF.Square, accum_out=RCOL[:, rb:rb + 1]))
        ts_ = b.s(lambda e: e.activation(RCOL[:, 2:4], RCOL[:, 0:2], AF.Sqrt, scale=epsc[:, 2:3], bias=epsc[:, 1:2]))
        b.w("vector", sS, ts_)
        b.v(lambda e: e.reciprocal(RCOL[:, 2:4], RCOL[:, 2:4]))
        for rb in range(2):
            b.v(lambda e, rb=rb: e.tensor_scalar(
                HTOK[:, rb * D:(rb + 1) * D], XLN[:, rb * D:(rb + 1) * D],
                RCOL[:, 2 + rb:3 + rb], None, ALU.mult))
        tv_hn = sV.n

        def tok2feat(src, dst, scale_cols, tv_ready):
            b.w("tensor", sV, tv_ready)
            evs = {}
            banks = (PS[2], PS[3], PS[4], PS[5])
            for dc in range(ND):
                bank = banks[dc % 4]
                if dc >= 4:
                    b.w("tensor", sV, evs[dc - 4])
                for rb in range(2):
                    b.t(lambda e, dc=dc, rb=rb, bank=bank, src=src: e.transpose(
                        bank[:, rb * 128:(rb + 1) * 128],
                        src[:, rb * D + dc * 128:rb * D + (dc + 1) * 128], S["cIDENT"][:, :]))
                tmm = sT.n
                b.w("vector", sT, tmm)
                if scale_cols is not None:
                    evs[dc] = b.v(lambda e, dc=dc, bank=bank, dst=dst, sc=scale_cols:
                                  e.tensor_scalar(dst[:, dc * RS:(dc + 1) * RS],
                                                  bank[:, 0:256], sc[:, dc:dc + 1], None, ALU.mult))
                else:
                    evs[dc] = b.v(lambda e, dc=dc, bank=bank, dst=dst:
                                  e.tensor_copy(dst[:, dc * RS:(dc + 1) * RS], bank[:, 0:256]))
            b.w("tensor", sV, sV.n)

        tok2feat(HTOK, HT, ln1sb, tv_hn)
        if bi == 0:
            tap("hnT", HT[:, :])
        for rb in range(2):
            for which, dstb in ((0, COS2), (1, SIN2)):
                ps = PS[4][:, 0:128]
                b.t(lambda e, rb=rb, which=which, ps=ps: e.transpose(
                    ps, CSG[:, rb * RS + which * 128:rb * RS + (which + 1) * 128], S["cIDENT"][:, :]))
                tmm = sT.n
                b.w("vector", sT, tmm)
                tv = b.v(lambda e, rb=rb, ps=ps, dstb=dstb: e.tensor_copy(
                    dstb[:, rb * 128:(rb + 1) * 128], ps))
                b.w("tensor", sV, tv)

        def proj(wname, dst, rope):
            mm_done = {}
            b.w("sync", sT, sT.n)
            for k in range(ND):
                if k >= 2:
                    b.w("sync", sT, mm_done[k - 2])
                tl = b.dma("sync", lambda e, wname=wname, k=k: e.dma_start(
                    out=WBUF[k % 2][:, :], in_=dram[wname][k * 128:(k + 1) * 128, :]))
                b.w("tensor", qS, tl)
                for mt in range(ND):
                    ps = PS[mt // 2][:, (mt % 2) * 256:(mt % 2) * 256 + 256]
                    b.t(lambda e, k=k, mt=mt, ps=ps: e.matmul(
                        ps, WBUF[k % 2][:, mt * 128:(mt + 1) * 128], HT[:, k * RS:(k + 1) * RS],
                        start=(k == 0 and mt % 2 == 0), stop=(k == ND - 1 and mt % 2 == 1)))
                mm_done[k] = sT.n
            b.w("vector", sT, sT.n)
            for mt in range(ND):
                ps = PS[mt // 2][:, (mt % 2) * 256:(mt % 2) * 256 + 256]
                b.v(lambda e, mt=mt, ps=ps, dst=dst: e.tensor_copy(dst[:, mt * RS:(mt + 1) * RS], ps))
            b.w("tensor", sV, sV.n)
            if rope:
                rope_ev = {}
                for mt in range(ND):
                    ps = (PS[6] if mt % 2 == 0 else PS[7])[:, 0:256]
                    if mt >= 2:
                        b.w("tensor", sV, rope_ev[mt - 2])
                    b.t(lambda e, mt=mt, ps=ps, dst=dst: e.matmul(
                        ps, S["cROTT"][:, :], dst[:, mt * RS:(mt + 1) * RS], start=True, stop=True))
                    tmm = sT.n
                    b.w("vector", sT, tmm)
                    b.v(lambda e, mt=mt, dst=dst: e.tensor_mul(
                        ASB[:, :], dst[:, mt * RS:(mt + 1) * RS], COS2[:, :]))
                    b.v(lambda e, mt=mt, ps=ps: e.tensor_mul(ATB[:, 0:RS], ps, SIN2[:, :]))
                    rope_ev[mt] = b.v(lambda e, mt=mt, dst=dst: e.tensor_add(
                        dst[:, mt * RS:(mt + 1) * RS], ASB[:, :], ATB[:, 0:RS]))
                b.w("tensor", sV, sV.n)

        if bi == 0:
            tap("xlnF", XLN[:, :])
        proj("WQ", QT, True)
        if bi == 0:
            tap("qTrope", QT[:, :])
            tap("cos2", COS2[:, :])
            tap("sin2", SIN2[:, :])
        proj("WK", KT, True)
        if bi == 0:
            tap("qT_afterK", QT[:, :])
        proj("WV", VT_, False)
        if bi == 0:
            tap("qT_afterV", QT[:, :])
        if bi == 0:
            tap("vT", VT_[:, :])
        for h in range(H):
            mt, po = h // 2, 64 * (h % 2)
            if bi == 0 and h in (0, 2, 4, 6):
                tap("qT_at%d" % h, QT[:, :])
            if h % 2 == 0:
                for kc in range(2):
                    ps = PS[5][:, kc * 128:(kc + 1) * 128]
                    b.t(lambda e, mt=mt, kc=kc, ps=ps: e.transpose(
                        ps, VT_[:, mt * RS + kc * 128:mt * RS + (kc + 1) * 128], S["cIDENT"][:, :]))
                tmm = sT.n
                b.w("vector", sT, tmm)
                b.v(lambda e: e.tensor_copy(VTOK[:, 0:128], PS[5][:, 0:128]))
                tv = b.v(lambda e: e.tensor_copy(VTOK[:, 128:256], PS[5][:, 128:256]))
                b.w("tensor", sV, tv)
            for qc in range(2):
                ps = (PS[4] if qc == 0 else PS[7])[:, 0:256]
                b.t(lambda e, mt=mt, po=po, qc=qc, ps=ps: e.matmul(
                    ps, QT[po:po + 64, mt * RS + qc * 128:mt * RS + (qc + 1) * 128],
                    KT[po:po + 64, mt * RS:(mt + 1) * RS], start=True, stop=True))
                tmm = sT.n
                b.w("vector", sT, tmm)
                b.v(lambda e, qc=qc, ps=ps: e.scalar_tensor_tensor(
                    ASB[:, :], ps, 0.125, S["cCAUSAL"][:, qc * RS:(qc + 1) * RS], ALU.mult, ALU.add))
                tv = b.v(lambda e: e.reduce_max(RCOL[:, 6:7], ASB[:, :], axis=AXX, negate=True))
                b.w("scalar", sV, tv)
                ts_ = b.s(lambda e: e.activation(ASB[:, :], ASB[:, :], AF.Exp, bias=RCOL[:, 6:7]))
                b.w("vector", sS, ts_)
                b.v(lambda e: e.reduce_sum(RCOL[:, 7:8], ASB[:, :], axis=AXX))
                b.v(lambda e: e.reciprocal(RCOL[:, 7:8], RCOL[:, 7:8]))
                tv = b.v(lambda e: e.tensor_scalar(ASB[:, :], ASB[:, :], RCOL[:, 7:8], None, ALU.mult))
                if bi == 0 and h in (0, 7) and qc == 0:
                    tap("asb_h%d" % h, ASB[:, :])
                b.w("tensor", sV, tv)
                for kc in range(2):
                    ps2 = PS[5][:, 256:384]
                    b.t(lambda e, kc=kc, ps2=ps2: e.transpose(ps2, ASB[:, kc * 128:(kc + 1) * 128], S["cIDENT"][:, :]))
                    tmm = sT.n
                    b.w("vector", sT, tmm)
                    tv = b.v(lambda e, kc=kc, qc=qc, ps2=ps2: e.tensor_copy(
                        ATB[:, kc * RS + qc * 128:kc * RS + (qc + 1) * 128], ps2))
                    b.w("tensor", sV, tv)
            for kc in range(2):
                if po == 0:
                    b.t(lambda e, kc=kc: e.matmul(
                        PS[6][0:64, 0:256], VTOK[:, kc * 128:kc * 128 + 64],
                        ATB[:, kc * RS:(kc + 1) * RS], start=(kc == 0), stop=(kc == 1)))
                else:
                    b.t(lambda e, kc=kc: e.matmul(
                        PS[6][64:128, 0:256], VTOK[:, kc * 128 + 64:kc * 128 + 128],
                        ATB[:, kc * RS:(kc + 1) * RS], start=(kc == 0), stop=(kc == 1),
                        tile_position=(0, 64)))
            tmm = sT.n
            b.w("vector", sT, tmm)
            tv = b.v(lambda e, mt=mt, po=po: e.tensor_copy(
                CTX[po:po + 64, mt * RS:(mt + 1) * RS], PS[6][po:po + 64, 0:256]))
            if bi == 0 and h in (0, 7):
                tap("atb_h%d" % h, ATB[:, :])
            b.w("tensor", sV, tv)
        if bi == 0:
            tap("ctxT", CTX[:, :])
        mm_done = {}
        b.w("sync", sT, sT.n)
        for k in range(ND):
            if k >= 2:
                b.w("sync", sT, mm_done[k - 2])
            tl = b.dma("sync", lambda e, k=k: e.dma_start(
                out=WBUF[k % 2][:, :], in_=dram["WOf"][k * 128:(k + 1) * 128, :]))
            b.w("tensor", qS, tl)
            for mt in range(ND):
                ps = PS[mt // 2][:, (mt % 2) * 256:(mt % 2) * 256 + 256]
                b.t(lambda e, k=k, mt=mt, ps=ps: e.matmul(
                    ps, WBUF[k % 2][:, mt * 128:(mt + 1) * 128], CTX[:, k * RS:(k + 1) * RS],
                    start=(k == 0 and mt % 2 == 0), stop=(k == ND - 1 and mt % 2 == 1)))
            mm_done[k] = sT.n
        b.w("vector", sT, sT.n)
        for mt in range(ND):
            ps = PS[mt // 2][:, (mt % 2) * 256:(mt % 2) * 256 + 256]
            b.v(lambda e, mt=mt, ps=ps: e.tensor_copy(KT[:, mt * RS:(mt + 1) * RS], ps))
        b.w("tensor", sV, sV.n)
        evs = {}
        banks = (PS[2], PS[3], PS[4], PS[5])
        XLNV = XLN.rearrange("p (r x) -> p r x", x=D)
        for dc in range(ND):
            bank = banks[dc % 4]
            if dc >= 4:
                b.w("tensor", sV, evs[dc - 4])
            for rb in range(2):
                b.t(lambda e, dc=dc, rb=rb, bank=bank: e.transpose(
                    bank[:, rb * 128:(rb + 1) * 128],
                    KT[:, dc * RS + rb * 128:dc * RS + (rb + 1) * 128], S["cIDENT"][:, :]))
            tmm = sT.n
            b.w("vector", sT, tmm)
            evs[dc] = b.v(lambda e, dc=dc, bank=bank: e.tensor_add(
                XLNV[:, :, dc * 128:(dc + 1) * 128],
                XLNV[:, :, dc * 128:(dc + 1) * 128], bank[:, 0:256]))
        if bi == 0:
            tap("x1tok", XLN[:, :])
        b.w("scalar", sV, sV.n)
        for rb in range(2):
            b.s(lambda e, rb=rb: e.activation(HTOK[:, rb * D:(rb + 1) * D], XLN[:, rb * D:(rb + 1) * D],
                                              AF.Square, accum_out=RCOL[:, 4 + rb:5 + rb]))
        ts_ = b.s(lambda e: e.activation(RCOL[:, 4:6], RCOL[:, 4:6], AF.Sqrt, scale=epsc[:, 2:3], bias=epsc[:, 1:2]))
        b.w("vector", sS, ts_)
        b.v(lambda e: e.reciprocal(RCOL[:, 4:6], RCOL[:, 4:6]))
        for rb in range(2):
            b.v(lambda e, rb=rb: e.tensor_scalar(
                HTOK[:, rb * D:(rb + 1) * D], XLN[:, rb * D:(rb + 1) * D],
                RCOL[:, 4 + rb:5 + rb], None, ALU.mult))
        tok2feat(HTOK, QT, ln2sb, sV.n)
        h1_done = {}
        last_w2_done = 0
        b.w("sync", sT, sT.n)
        b.w("scalar", sT, sT.n)
        for fc in range(NFC):
            for k in range(ND):
                if k >= 2:
                    b.w("sync", sT, h1_done.get((fc, k - 2), 0))
                elif fc > 0 and k == 0:
                    b.w("sync", sT, h1_done[(fc - 1, ND - 2)])
                elif fc > 0:
                    b.w("sync", sT, h1_done[(fc - 1, ND - 1)])
                tl = b.dma("sync", lambda e, fc=fc, k=k: e.dma_start(
                    out=WBUF[k % 2][:, 0:FC], in_=dram["W1"][k * 128:(k + 1) * 128, fc * FC:(fc + 1) * FC]))
                b.w("tensor", qS, tl)
                for mf in range(FC // 128):
                    ps = PS[4 + mf // 2][:, (mf % 2) * 256:(mf % 2) * 256 + 256]
                    b.t(lambda e, k=k, mf=mf, ps=ps: e.matmul(
                        ps, WBUF[k % 2][:, mf * 128:(mf + 1) * 128], QT[:, k * RS:(k + 1) * RS],
                        start=(k == 0 and mf % 2 == 0), stop=(k == ND - 1 and mf % 2 == 1)))
                h1_done[(fc, k)] = sT.n
            b.w("scalar", sT, sT.n)
            for mf in range(FC // 128):
                ps = PS[4 + mf // 2][:, (mf % 2) * 256:(mf % 2) * 256 + 256]
                b.s(lambda e, mf=mf, ps=ps: e.activation(H1G[:, mf * RS:(mf + 1) * RS], ps, AF.Gelu_apprx_tanh))
            b.w("tensor", sS, sS.n)
            w2_done = {}
            for kf in range(FC // 128):
                if kf >= 2:
                    b.w("scalar", sT, w2_done[kf - 2])
                elif last_w2_done:
                    b.w("scalar", sT, last_w2_done)
                tl2 = b.dma("scalar", lambda e, fc=fc, kf=kf: e.dma_start(
                    out=WBUF2[kf % 2][:, :], in_=dram["W2"][fc * FC + kf * 128:fc * FC + (kf + 1) * 128, :]))
                b.w("tensor", qA, tl2)
                for mt in range(ND):
                    ps = PS[mt // 2][:, (mt % 2) * 256:(mt % 2) * 256 + 256]
                    b.t(lambda e, kf=kf, mt=mt, ps=ps, fc=fc: e.matmul(
                        ps, WBUF2[kf % 2][:, mt * 128:(mt + 1) * 128], H1G[:, kf * RS:(kf + 1) * RS],
                        start=(fc == 0 and kf == 0 and mt % 2 == 0),
                        stop=(fc == NFC - 1 and kf == FC // 128 - 1 and mt % 2 == 1)))
                w2_done[kf] = sT.n
            last_w2_done = w2_done[FC // 128 - 1]
        b.w("vector", sT, sT.n)
        for mt in range(ND):
            ps = PS[mt // 2][:, (mt % 2) * 256:(mt % 2) * 256 + 256]
            b.v(lambda e, mt=mt, ps=ps: e.tensor_copy(KT[:, mt * RS:(mt + 1) * RS], ps))
        b.w("tensor", sV, sV.n)
        t_sc0 = None
        evs = {}
        fbanks = (PS[2], PS[3], PS[4], PS[5])
        for rb in range(2):
            for dc in range(ND):
                k = rb * ND + dc
                ps = fbanks[k % 4][:, 0:128]
                if k >= 4:
                    b.w("tensor", sV, evs[k - 4])
                b.t(lambda e, dc=dc, rb=rb, ps=ps: e.transpose(
                    ps, KT[:, dc * RS + rb * 128:dc * RS + (rb + 1) * 128], S["cIDENT"][:, :]))
                tmm = sT.n
                b.w("vector", sT, tmm)
                b.v(lambda e, dc=dc, rb=rb, ps=ps: e.tensor_add(
                    FIN[:, dc * 128:(dc + 1) * 128],
                    XLN[:, rb * D + dc * 128:rb * D + (dc + 1) * 128], ps))
                b.v(lambda e, dc=dc, rb=rb: e.tensor_scalar(
                    FIN[:, dc * 128:(dc + 1) * 128], FIN[:, dc * 128:(dc + 1) * 128],
                    selcols[:, 5 * rb + 1:5 * rb + 2], None, ALU.mult))
                evs[k] = b.v(lambda e, dc=dc, rb=rb: e.tensor_add(
                    FIN[:, dc * 128:(dc + 1) * 128], FIN[:, dc * 128:(dc + 1) * 128],
                    XRS[:, rb * D + dc * 128:rb * D + (dc + 1) * 128]))
            tvf = sV.n
            if bi == 0:
                tap("fin%d" % rb, FIN[:, :])
            b.w("gpsimd", sV, tvf)
            out_flat = out_t.rearrange("b t d -> (b t) d")
            t_sc = b.emit("gpsimd", lambda e, rb=rb, out_flat=out_flat: e.indirect_dma_start(
                out=out_flat, out_offset=IndirectOffsetOnAxis(ap=selidx[:, 2 + rb:3 + rb], axis=0),
                in_=FIN[:, :], in_offset=None,
                bounds_check=BPC * T - 1, oob_is_err=False,
            ), qG, 16)
            if rb == 0:
                t_sc0 = t_sc
                b.w("vector", qG, t_sc0)
        b.fence()
        return t_xn

    stages = ["pre", "A0", "F0", "M0", "A1", "F1", "M1"]
    upto = stages.index(stop_after) if stop_after in stages else len(stages)
    if upto >= 1:
        phaseA(0, t_x0)
    if upto >= 2:
        phaseFpre(0)
    if upto >= 3:
        t_x1 = phaseFmain(0, 1)
    if upto >= 4:
        phaseA(1, t_x1)
    if upto >= 5:
        phaseFpre(1)
    if upto >= 6:
        phaseFmain(1, None)
    DUMP.update(dict(x_tok=x_tok, selcols=selcols, selm=selm, stmu=stmu, stro=stro, wbias_rep=wbias_rep, negs=negs, orvd=orv, invc=invc, wbias_row=wbias_row, wrcd=wrc, srepd=srepc, abic=abic, atric=atric,
                     wraw=wraw, ccum=ccum, acat=acat, outb=outb, xrs=XRS, csg=CSG))
    emit_dumps()
    b.w("sync", qG, qG.n)
    b.w("sync", qA, qA.n)
    b.w("sync", qS, qS.n)

    b.finalize()
    return nc, dram


_NC_CACHE = {}


def get_nc(dbg_names=(), stop_after="all", end_dumps=()):
    key = (tuple(dbg_names), stop_after, tuple(end_dumps))
    if key not in _NC_CACHE:
        _NC_CACHE[key] = build_nc(dbg_names, stop_after, end_dumps)
    return _NC_CACHE[key]


def make_in_map(inputs, core):
    x = np.ascontiguousarray(np.asarray(inputs["L0_attn_x"], dtype=np.float32))
    base = {
        "WQa": inputs["L0_attn_WQ"], "WKa": inputs["L0_attn_WK"],
        "WOa": inputs["L0_attn_WO"], "WINV": inputs["L0_attn_Winv"],
        "UB": inputs["L0_attn_U_b"], "VB": inputs["L0_attn_V_b"], "WB": inputs["L0_attn_W_b"],
        "UT": inputs["L0_attn_U_t"], "VT": inputs["L0_attn_V_t"], "WT": inputs["L0_attn_W_t"],
        "XTm": inputs["L0_attn_X_t"],
        "BIASB": np.asarray(inputs["L0_attn_bias_b"]).reshape(D, 1),
        "BIAST": np.asarray(inputs["L0_attn_bias_t"]).reshape(D, 1),
        "ABI": np.asarray(inputs["L0_attn_alpha_bi"]).reshape(1, 1),
        "ATRI": np.asarray(inputs["L0_attn_alpha_tri"]).reshape(1, 1),
        "RW": np.asarray(inputs["L0_ffn_router_w"]).reshape(D, 1),
        "RB": np.asarray(inputs["L0_ffn_router_b"]).reshape(1, 1),
        "WQ": inputs["L0_ffn_wq"], "WK": inputs["L0_ffn_wk"], "WV": inputs["L0_ffn_wv"],
        "WOf": inputs["L0_ffn_wo"], "W1": inputs["L0_ffn_w1"], "W2": inputs["L0_ffn_w2"],
        "LN1": np.asarray(inputs["L0_ffn_ln1"]).reshape(D, 1),
        "LN2": np.asarray(inputs["L0_ffn_ln2"]).reshape(D, 1),
    }
    base = {k: np.ascontiguousarray(np.asarray(v, dtype=np.float32)) for k, v in base.items()}
    for k, v in CONSTS.items():
        base[k] = v
    base["x"] = np.ascontiguousarray(x[2 * core:2 * core + 2])
    return base


def kernel(**inputs):
    nc, _ = get_nc()
    in_maps = [make_in_map(inputs, c) for c in range(8)]
    res = run_bass_kernel_spmd(nc, in_maps, core_ids=list(range(8)))
    return np.concatenate([r["out"] for r in res.results], axis=0)


# version 06
# speedup vs baseline: 1.2063x; 1.0138x over previous
import numpy as np
import concourse.bass as bass
import concourse.mybir as mybir
from concourse.bass_utils import run_bass_kernel_spmd
from concourse.bass_core_rust import IndirectOffsetOnAxis

F32 = mybir.dt.float32
I32 = mybir.dt.int32
AF = mybir.ActivationFunctionType
ALU = mybir.AluOpType
AXX = mybir.AxisListType.X

T, D, R, H, HD = 2048, 1024, 64, 16, 64
NT, ND = T // 128, D // 128
KC, RS, FF, BPC = 255, 256, 4096, 2
FC = 512
NFC = FF // FC

ENGS = ("tensor", "vector", "scalar", "gpsimd", "sync")


def host_consts():
    f32 = np.float32
    c = {}
    c["IDENT"] = np.eye(128, dtype=f32)
    c["TRIT"] = np.triu(np.ones((128, 128), dtype=f32))
    c["TRIST"] = np.triu(np.ones((128, 128), dtype=f32), k=1)
    c["ONES128"] = np.ones((128, 128), dtype=f32)
    c["ONESROW"] = np.ones((1, 128), dtype=f32)
    c["ONESCOL"] = np.ones((128, 1), dtype=f32)
    c["IOTAR"] = np.broadcast_to(np.arange(RS, dtype=f32), (128, RS)).copy()
    c["ARANGE"] = np.arange(T, dtype=f32).reshape(NT, 128).T.copy()
    c["COUNTS"] = np.arange(1, T + 1, dtype=f32).reshape(NT, 128).T.copy()
    cb = np.zeros((128, 2 * RS), dtype=f32)
    for qt in range(2):
        q = 128 * qt + np.arange(128)[:, None]
        k = np.arange(RS)[None, :]
        cb[:, RS * qt:RS * qt + RS] = np.where(k <= q, 0.0, np.float32(-1e9))
    c["CAUSAL"] = cb
    r2 = np.zeros((128, 128), dtype=f32)
    for h0 in (0, 64):
        for i in range(32):
            r2[h0 + i, h0 + 32 + i] = -1.0
            r2[h0 + 32 + i, h0 + i] = 1.0
    c["ROTT"] = r2.T.copy()
    import jax
    import jax.numpy as jnp
    cpu = jax.devices("cpu")[0]
    with jax.default_device(cpu):
        invj = 1.0 / 10000.0 ** (jnp.arange(0, HD, 2, dtype=jnp.float32) / HD)
        posj = jnp.arange(T, dtype=jnp.int32).astype(jnp.float32)
        frj = posj[:, None] * invj[None, :]
        embj = jnp.concatenate([frj, frj], axis=-1)
        cosj, sinj = jnp.cos(embj), jnp.sin(embj)
    cosd = np.concatenate([np.asarray(cosj), np.asarray(cosj)], axis=1)
    sind = np.concatenate([np.asarray(sinj), np.asarray(sinj)], axis=1)
    tab = np.concatenate([cosd, sind], axis=1).astype(f32)
    c["CSTAB"] = np.concatenate([tab, tab], axis=0)
    return c


CONSTS = host_consts()
SB_CONSTS = [k for k in CONSTS if k != "CSTAB"]


class Sem:
    def __init__(self, h):
        self.h, self.n = h, 0

    def bump(self, k=1):
        self.n += k
        return self.n


class B:
    def __init__(self, nc):
        self.nc = nc
        self.plan = {e: [] for e in ENGS}
        self.sems = {e: Sem(nc.ctx.enter_context(nc.semaphore("s_" + e))) for e in ENGS}
        self.qsems = {e: Sem(nc.ctx.enter_context(nc.semaphore("q_" + e)))
                      for e in ("sync", "scalar", "gpsimd")}
        self._fw = {e: {} for e in ENGS}

    def emit(self, eng, fn, inc=None, inck=1):
        if eng in ("vector", "scalar") and inc is self.sems[eng] and inc.n > 0:
            self.plan[eng].append(lambda e, s=inc.h, t=inc.n: e.wait_ge(s, t))
        tgt = inc.bump(inck) if inc is not None else None
        if inc is not None:
            self.plan[eng].append(lambda e, fn=fn, s=inc.h, k=inck: fn(e).then_inc(s, k))
        else:
            self.plan[eng].append(lambda e, fn=fn: fn(e))
        return tgt

    def w(self, eng, sem, tgt):
        if tgt:
            self.plan[eng].append(lambda e, s=sem.h, t=tgt: e.wait_ge(s, t))

    def t(self, fn):
        return self.emit("tensor", fn, self.sems["tensor"])

    def v(self, fn):
        return self.emit("vector", fn, self.sems["vector"])

    def s(self, fn):
        return self.emit("scalar", fn, self.sems["scalar"])

    def dma(self, eng, fn):
        return self.emit(eng, fn, self.qsems[eng], 16)

    def fence(self):
        snap = list(self.sems.values()) + list(self.qsems.values())
        for e in ENGS:
            for s in snap:
                if s is self.sems[e]:
                    continue
                if s.n > self._fw[e].get(id(s), 0):
                    self.w(e, s, s.n)
                    self._fw[e][id(s)] = s.n

    def finalize(self):
        with self.nc.Block() as blk:
            for name in ENGS:
                getattr(blk, name)((lambda lst: (lambda e: [f(e) for f in lst]))(self.plan[name]))


def build_nc(dbg_names=(), stop_after="all", end_dumps=()):
    nc = bass.Bass("TRN2", target_bir_lowering=False, detect_race_conditions=False)
    dram = {}

    def din(name, shape):
        dram[name] = nc.dram_tensor(name, list(shape), F32, kind="ExternalInput").ap()
        return dram[name]

    x_in = din("x", (BPC, T, D))
    out_t = nc.dram_tensor("out", [BPC, T, D], F32, kind="ExternalOutput").ap()
    for nm, sh in (("WQa", (D, D)), ("WKa", (D, D)), ("WOa", (D, D)), ("WINV", (D, D)),
                   ("UB", (D, R)), ("VB", (D, R)), ("WB", (D, R)),
                   ("UT", (D, R)), ("VT", (D, R)), ("WT", (D, R)), ("XTm", (D, R)),
                   ("BIASB", (D, 1)), ("BIAST", (D, 1)), ("ABI", (1, 1)), ("ATRI", (1, 1)),
                   ("RW", (D, 1)), ("RB", (1, 1)),
                   ("WQ", (D, D)), ("WK", (D, D)), ("WV", (D, D)), ("WOf", (D, D)),
                   ("W1", (D, FF)), ("W2", (FF, D)), ("LN1", (D, 1)), ("LN2", (D, 1))):
        din(nm, sh)
    for k in CONSTS:
        din(k, CONSTS[k].shape)
    DUMP = {}
    dbg = {}
    for name, width in dbg_names:
        dbg[name] = nc.dram_tensor("dbg_" + name, [128, width], F32, kind="ExternalOutput").ap()

    S = {}

    def sb(name, shape, dtype=F32):
        S[name] = nc.alloc_sbuf_tensor(name, list(shape), dtype).ap()
        return S[name]

    x_tok = sb("x_tok", (128, NT * D))
    acat = sb("acat", (128, ND * 449))
    outb = sb("outb", (128, D))
    wbias_row = sb("wbias_row", (1, D))
    wbias_rep = sb("wbias_rep", (128, D))
    for k in SB_CONSTS:
        sb("c" + k, CONSTS[k].shape)
    invc = sb("invc", (128, NT))
    invc2 = sb("invc2", (128, NT))
    abic = sb("abic", (128, 1))
    atric = sb("atric", (128, 1))
    orv = sb("orv", (128, 1))
    wrc = sb("wrc", (128, 1))
    srepc = sb("srepc", (128, 1))
    rbc = sb("rbc", (128, 1))
    rwsb = sb("rwsb", (128, ND))
    ln1sb = sb("ln1sb", (128, ND))
    ln2sb = sb("ln2sb", (128, ND))
    negs = sb("negs", (1, 448))
    scal = sb("scal", (1, 8))
    wraw = sb("wraw", (128, NT))
    stmu = sb("stmu", (128, NT))
    stro = sb("stro", (128, NT))
    b2c = sb("b2c", (128, NT))
    selm = sb("selm", (128, NT))
    ccum = sb("ccum", (128, NT))
    cols5 = sb("cols5", (128, 5 * NT))
    selcols = sb("selcols", (128, 10))
    selidx = sb("selidx", (128, 4), I32)
    carry = sb("carry", (128, 256))
    epsc = sb("epsc", (128, 4))
    prb = sb("prb", (1, 16))
    XRS = sb("xrs", (128, 2 * D))
    CSG = sb("csg", (128, 2 * RS))
    ARW = 24200
    arena = sb("arena", (128, ARW))
    PS = [nc.alloc_psum_tensor("pb%d" % i, [128, 512], F32).ap() for i in range(8)]

    b = B(nc)
    sT, sV, sS, sG = (b.sems[e] for e in ("tensor", "vector", "scalar", "gpsimd"))
    qS, qA, qG = (b.qsems[e] for e in ("sync", "scalar", "gpsimd"))

    _PQI = {"sync": 0, "scalar": 1, "gpsimd": 2}

    def qprobe(eng):
        qi = _PQI[eng]
        return b.dma(eng, lambda e, qi=qi: e.dma_start(
            out=prb[0:1, qi:qi + 1], in_=dram["IDENT"][0:1, 0:1]))

    class Scope:
        def __init__(self):
            self.off, self.map = 0, {}

        def g(self, name, width):
            if name not in self.map:
                assert self.off + width <= ARW, (name, self.off, width)
                self.map[name] = (self.off, width)
                self.off += width
            o, wd = self.map[name]
            assert wd == width
            return arena[:, o:o + wd]

        def row(self, name, width):
            return self.g(name, width)[0:1, :]

    def tap(name, ap):
        if name in dbg:
            for sem in (sT, sV, sS, sG):
                b.w("sync", sem, sem.n)
            b.dma("sync", lambda e, ap=ap, name=name: e.dma_start(
                out=dbg[name][:ap.shape[0], :ap.shape[1]], in_=ap))

    for k in SB_CONSTS:
        b.dma("sync", lambda e, k=k: e.dma_start(out=S["c" + k][:], in_=dram[k][:, :]))
    for c in range(ND):
        b.dma("sync", lambda e, c=c: e.dma_start(out=rwsb[:, c:c + 1], in_=dram["RW"][c * 128:(c + 1) * 128, :]))
        b.dma("sync", lambda e, c=c: e.dma_start(out=ln1sb[:, c:c + 1], in_=dram["LN1"][c * 128:(c + 1) * 128, :]))
        b.dma("sync", lambda e, c=c: e.dma_start(out=ln2sb[:, c:c + 1], in_=dram["LN2"][c * 128:(c + 1) * 128, :]))
    b.dma("sync", lambda e: e.dma_start(out=scal[:, 0:1], in_=dram["ABI"][:, :]))
    b.dma("sync", lambda e: e.dma_start(out=scal[:, 1:2], in_=dram["ATRI"][:, :]))
    t_sc3 = b.dma("sync", lambda e: e.dma_start(out=scal[:, 2:3], in_=dram["RB"][:, :]))
    t_x0 = b.dma("sync", lambda e: e.dma_start(
        out=x_tok.rearrange("q (n d) -> q n d", d=D),
        in_=x_in[0].rearrange("(n p) d -> p n d", p=128)))
    t_outinit = b.dma("gpsimd", lambda e: e.dma_start(out=out_t[1], in_=x_in[1]))
    t_outinit = qprobe("gpsimd")
    b.v(lambda e: e.memset(epsc[:, 0:1], 1e-5))
    b.v(lambda e: e.memset(epsc[:, 1:2], 1e-6))
    b.v(lambda e: e.memset(epsc[:, 2:3], 1.0 / D))

    pre = Scope()
    SW = {}
    for nm in ("UB", "UT", "XTm"):
        SW[nm] = pre.g("sw_" + nm, ND * R)
        for c in range(ND):
            b.dma("sync", lambda e, nm=nm, c=c: e.dma_start(
                out=SW[nm][:, c * R:(c + 1) * R], in_=dram[nm][c * 128:(c + 1) * 128, :]))
    QCAT = pre.g("qcat", ND * 192)
    KCAT = pre.g("kcat", ND * 192)
    VW = pre.g("vw", ND * 128)
    for c in range(ND):
        b.dma("sync", lambda e, c=c: e.dma_start(
            out=QCAT[:, c * 192:c * 192 + R], in_=dram["VB"][c * 128:(c + 1) * 128, :]))
        b.dma("sync", lambda e, c=c: e.dma_start(
            out=QCAT[:, c * 192 + 128:c * 192 + 192], in_=dram["VT"][c * 128:(c + 1) * 128, :]))
        b.dma("sync", lambda e, c=c: e.dma_start(
            out=KCAT[:, c * 192:c * 192 + R], in_=dram["WB"][c * 128:(c + 1) * 128, :]))
        b.dma("sync", lambda e, c=c: e.dma_start(
            out=KCAT[:, c * 192 + 128:c * 192 + 192], in_=dram["WT"][c * 128:(c + 1) * 128, :]))
        b.dma("sync", lambda e, c=c: e.dma_start(
            out=VW[:, c * 128:c * 128 + R], in_=dram["VB"][c * 128:(c + 1) * 128, :]))
        b.dma("sync", lambda e, c=c: e.dma_start(
            out=VW[:, c * 128 + R:(c + 1) * 128], in_=dram["WB"][c * 128:(c + 1) * 128, :]))
    bbcol = pre.g("biasb", ND)
    btcol = pre.g("biast", ND)
    for c in range(ND):
        b.dma("sync", lambda e, c=c: e.dma_start(out=bbcol[:, c:c + 1], in_=dram["BIASB"][c * 128:(c + 1) * 128, :]))
        b.dma("sync", lambda e, c=c: e.dma_start(out=btcol[:, c:c + 1], in_=dram["BIAST"][c * 128:(c + 1) * 128, :]))
    BW = pre.g("bigw", ND * D)
    WOT = pre.g("wot", ND * D)
    BCOMB = pre.g("bcomb", ND)
    WORC = pre.g("worc", ND)
    WBIC = pre.g("wbic", ND)
    DUMP["acat"] = acat
    DUMP["qcat"] = QCAT
    DUMP["kcat"] = KCAT
    DUMP["wbic"] = WBIC
    DUMP["bcomb"] = BCOMB
    DUMP["bbcol"] = bbcol
    DUMP["btcol"] = btcol
    DUMP["rwsb"] = rwsb
    DUMP["ln1sb"] = ln1sb
    t_pre_loads = qprobe("sync")

    _stopped = []

    def emit_dumps():
        for sem in list(b.sems.values()) + [qA, qG]:
            b.w("sync", sem, sem.n)
        for nm in end_dumps:
            ap = DUMP[nm]
            dd = nc.dram_tensor("dump_" + nm, [ap.shape[0], ap.shape[1]], F32,
                                kind="ExternalOutput").ap()
            b.dma("sync", lambda e, ap=ap, dd=dd: e.dma_start(out=dd[:, :], in_=ap))
        b.w("sync", qS, qS.n)

    def maybe_stop(tag):
        if stop_after == tag:
            emit_dumps()
            b.w("sync", qG, qG.n)
            b.finalize()
            _stopped.append(True)
        return bool(_stopped)

    if maybe_stop("load"):
        return nc, dram
    b.w("tensor", qS, t_pre_loads)
    b.t(lambda e: e.matmul(PS[0][:, 0:1], S["cONESROW"][:, :], scal[0:1, 0:1], start=True, stop=True))
    b.t(lambda e: e.matmul(PS[0][:, 1:2], S["cONESROW"][:, :], scal[0:1, 1:2], start=True, stop=True))
    tm = b.t(lambda e: e.matmul(PS[0][:, 2:3], S["cONESROW"][:, :], scal[0:1, 2:3], start=True, stop=True))
    b.w("vector", sT, tm)
    b.v(lambda e: e.tensor_copy(abic[:], PS[0][:, 0:1]))
    b.v(lambda e: e.tensor_copy(atric[:], PS[0][:, 1:2]))
    b.v(lambda e: e.tensor_copy(rbc[:], PS[0][:, 2:3]))
    b.w("vector", qS, t_pre_loads)
    b.v(lambda e: e.tensor_scalar(BCOMB[:, :], bbcol[:, :], abic[:, 0:1], None, ALU.mult))
    b.v(lambda e: e.tensor_add(BCOMB[:, :], BCOMB[:, :], bbcol[:, :]))
    b.v(lambda e: e.scalar_tensor_tensor(BCOMB[:, :], btcol[:, :], atric[:, 0:1], BCOMB[:, :], ALU.mult, ALU.add))
    b.v(lambda e: e.reciprocal(invc[:], S["cCOUNTS"][:, :]))
    tv = b.v(lambda e: e.tensor_mul(invc2[:], invc[:], invc[:]))
    b.w("tensor", sV, tv)

    if maybe_stop("p1"):
        return nc, dram

    def ld_bigw(nm):
        b.w("sync", sT, sT.n)
        for c in range(ND):
            b.dma("sync", lambda e, nm=nm, c=c: e.dma_start(
                out=BW[:, c * D:(c + 1) * D], in_=dram[nm][c * 128:(c + 1) * 128, :]))
        return qprobe("sync")

    def wsmall(dst_fn, vsm, scale_col=None, tilepos=None):
        ev = {}
        for m in range(ND):
            slot = (PS[1] if m % 2 == 0 else PS[6])[:, 0:R]
            oslot = slot if tilepos is None else slot[64:128, :]
            if m >= 2:
                b.w("tensor", sV, ev[m - 2])
            for j in range(ND):
                b.t(lambda e, m=m, j=j, oslot=oslot: e.matmul(
                    oslot, BW[:, j * D + m * 128:j * D + (m + 1) * 128],
                    vsm[:, j * R:(j + 1) * R], start=(j == 0), stop=(j == ND - 1),
                    tile_position=tilepos))
            tmm = sT.n
            b.w("vector", sT, tmm)
            dst = dst_fn(m)
            if scale_col is None:
                ev[m] = b.v(lambda e, dst=dst, oslot=oslot: e.tensor_copy(dst, oslot))
            else:
                ev[m] = b.v(lambda e, dst=dst, oslot=oslot, sc=scale_col: e.tensor_scalar(
                    dst, oslot, sc, None, ALU.mult))
        b.w("tensor", sV, sV.n)

    tw = ld_bigw("WINV")
    b.w("tensor", qS, tw)
    ev = {}
    for m in range(ND):
        slot = (PS[1] if m % 2 == 0 else PS[6])[:, 0:128]
        if m >= 2:
            b.w("tensor", sV, ev[m - 2])
        for j in range(ND):
            b.t(lambda e, m=m, j=j, slot=slot: e.matmul(
                slot, BW[:, j * D + m * 128:j * D + (m + 1) * 128],
                VW[:, j * 128:(j + 1) * 128], start=(j == 0), stop=(j == ND - 1)))
        tmm = sT.n
        b.w("vector", sT, tmm)
        b.v(lambda e, m=m, slot=slot: e.tensor_copy(
            KCAT[:, m * 192 + R:m * 192 + 128], slot[:, 0:64]))
        ev[m] = b.v(lambda e, m=m, slot=slot: e.tensor_scalar(
            QCAT[:, m * 192 + R:m * 192 + 128], slot[:, 64:128], abic[:, 0:1], None, ALU.mult))
    b.w("tensor", sV, sV.n)
    if maybe_stop("p2"):
        return nc, dram
    for wname, coff in (("WQa", 0), ("WKa", 192)):
        cat = QCAT if wname == "WQa" else KCAT
        tw = ld_bigw(wname)
        b.w("tensor", qS, tw)
        ev = {}
        for m in range(ND):
            slot = (PS[1] if m % 2 == 0 else PS[6])[:, 0:192]
            if m >= 2:
                b.w("tensor", sV, ev[m - 2])
            for j in range(ND):
                b.t(lambda e, m=m, j=j, slot=slot, cat=cat: e.matmul(
                    slot, BW[:, j * D + m * 128:j * D + (m + 1) * 128],
                    cat[:, j * 192:(j + 1) * 192], start=(j == 0), stop=(j == ND - 1)))
            tmm = sT.n
            b.w("vector", sT, tmm)
            ev[m] = b.v(lambda e, m=m, slot=slot, coff=coff: e.tensor_copy(
                acat[:, 449 * m + coff:449 * m + coff + 192], slot))
        b.w("tensor", sV, sV.n)
    for m in range(ND):
        b.v(lambda e, m=m: e.tensor_copy(acat[:, 449 * m + 384:449 * m + 448],
                                         SW["XTm"][:, m * R:(m + 1) * R]))
        b.v(lambda e, m=m: e.tensor_copy(acat[:, 449 * m + 448:449 * m + 449],
                                         rwsb[:, m:m + 1]))
    if maybe_stop("p3"):
        return nc, dram
    tw = ld_bigw("WOa")
    b.w("tensor", qS, tw)
    ev = {}
    for m in range(ND):
        slot = (PS[1] if m % 2 == 0 else PS[6])[:, 128:129]
        if m >= 2:
            b.w("tensor", sV, ev[m - 2])
        for k in range(ND):
            b.t(lambda e, m=m, k=k, slot=slot: e.matmul(
                slot, BW[:, k * D + m * 128:k * D + (m + 1) * 128],
                rwsb[:, k:k + 1], start=(k == 0), stop=(k == ND - 1)))
        tmm = sT.n
        b.w("vector", sT, tmm)
        ev[m] = b.v(lambda e, m=m, slot=slot: e.tensor_copy(WORC[:, m:m + 1], slot))
    b.w("tensor", sV, sV.n)
    if maybe_stop("p3a"):
        return nc, dram
    WOTV = WOT.rearrange("p (j x) -> p j x", x=D)
    ev = {}
    for c in range(ND):
        bankA = PS[2] if c % 2 == 0 else PS[4]
        bankB = PS[3] if c % 2 == 0 else PS[5]
        if c >= 2:
            b.w("tensor", sV, ev[c - 2])
        for j in range(ND):
            ps = (bankA if j < 4 else bankB)[:, (j % 4) * 128:(j % 4) * 128 + 128]
            b.t(lambda e, c=c, j=j, ps=ps: e.transpose(
                ps, BW[:, c * D + j * 128:c * D + (j + 1) * 128], S["cIDENT"][:, :]))
        tmm = sT.n
        b.w("vector", sT, tmm)
        b.v(lambda e, c=c, bankA=bankA: e.tensor_copy(
            WOTV[:, 0:4, c * 128:(c + 1) * 128], bankA[:, 0:512]))
        ev[c] = b.v(lambda e, c=c, bankB=bankB: e.tensor_copy(
            WOTV[:, 4:8, c * 128:(c + 1) * 128], bankB[:, 0:512]))
    b.w("tensor", sV, sV.n)
    if isinstance(stop_after, str) and stop_after.startswith("wot"):
        emit_dumps()
        b.w("sync", qG, qG.n)
        b.finalize()
        return nc, dram
    if maybe_stop("p4"):
        return nc, dram
    for n2 in range(2):
        slot = (PS[3] if n2 == 0 else PS[4])[0:64, 0:512]
        for j in range(ND):
            b.t(lambda e, n2=n2, j=j, slot=slot: e.matmul(
                slot, SW["UB"][:, j * R:(j + 1) * R],
                WOT[:, j * D + n2 * 512:j * D + (n2 + 1) * 512],
                start=(j == 0), stop=(j == ND - 1)))
        tmm = sT.n
        b.w("vector", sT, tmm)
        b.v(lambda e, n2=n2, slot=slot: e.tensor_copy(outb[0:64, n2 * 512:(n2 + 1) * 512], slot))
    for n2 in range(2):
        slot = (PS[5] if n2 == 0 else PS[6])[64:128, 0:512]
        for j in range(ND):
            b.t(lambda e, n2=n2, j=j, slot=slot: e.matmul(
                slot, SW["UT"][:, j * R:(j + 1) * R],
                WOT[:, j * D + n2 * 512:j * D + (n2 + 1) * 512],
                start=(j == 0), stop=(j == ND - 1), tile_position=(0, 64)))
        tmm = sT.n
        b.w("vector", sT, tmm)
        b.v(lambda e, n2=n2, slot=slot: e.tensor_scalar(
            outb[64:128, n2 * 512:(n2 + 1) * 512], slot, atric[64:128, 0:1], None, ALU.mult))
    b.w("tensor", sV, sV.n)
    slot = PS[1][0:64, 130:131]
    for j in range(ND):
        b.t(lambda e, j=j, slot=slot: e.matmul(
            slot, SW["UB"][:, j * R:(j + 1) * R], WORC[:, j:j + 1],
            start=(j == 0), stop=(j == ND - 1)))
    tmm = sT.n
    b.w("vector", sT, tmm)
    b.v(lambda e, slot=slot: e.tensor_copy(orv[0:64, 0:1], slot))
    slot = PS[6][64:128, 132:133]
    for j in range(ND):
        b.t(lambda e, j=j, slot=slot: e.matmul(
            slot, SW["UT"][:, j * R:(j + 1) * R], WORC[:, j:j + 1],
            start=(j == 0), stop=(j == ND - 1), tile_position=(0, 64)))
    tmm = sT.n
    b.w("vector", sT, tmm)
    b.v(lambda e, slot=slot: e.tensor_scalar(orv[64:128, 0:1], slot, atric[64:128, 0:1], None, ALU.mult))
    b.w("tensor", sV, sV.n)
    if maybe_stop("p5"):
        return nc, dram
    ev = {}
    for m in range(ND):
        slot = (PS[1] if m % 2 == 0 else PS[6])[:, 134:135]
        if m >= 2:
            b.w("tensor", sV, ev[m - 2])
        for j in range(ND):
            b.t(lambda e, m=m, j=j, slot=slot: e.matmul(
                slot, WOT[:, j * D + m * 128:j * D + (m + 1) * 128],
                BCOMB[:, j:j + 1], start=(j == 0), stop=(j == ND - 1)))
        tmm = sT.n
        b.w("vector", sT, tmm)
        ev[m] = b.v(lambda e, m=m, slot=slot: e.tensor_copy(WBIC[:, m:m + 1], slot))
    b.w("tensor", sV, sV.n)
    for m in range(ND):
        slot = PS[2][0:1, 0:128]
        b.t(lambda e, m=m, slot=slot: e.transpose(slot, WBIC[:, m:m + 1], S["cIDENT"][:, :]))
        tmm = sT.n
        b.w("vector", sT, tmm)
        tv = b.v(lambda e, m=m, slot=slot: e.tensor_copy(wbias_row[0:1, m * 128:(m + 1) * 128], slot))
        b.w("tensor", sV, tv)
    for n2 in range(2):
        slot = (PS[3] if n2 == 0 else PS[4])[:, 0:512]
        b.t(lambda e, n2=n2, slot=slot: e.matmul(
            slot, S["cONESROW"][:, :], wbias_row[0:1, n2 * 512:(n2 + 1) * 512], start=True, stop=True))
        tmm = sT.n
        b.w("vector", sT, tmm)
        b.v(lambda e, n2=n2, slot=slot: e.tensor_copy(wbias_rep[:, n2 * 512:(n2 + 1) * 512], slot))
    slot = PS[1][0:1, 136:137]
    for m in range(ND):
        b.t(lambda e, m=m, slot=slot: e.matmul(
            slot, WBIC[:, m:m + 1], rwsb[:, m:m + 1], start=(m == 0), stop=(m == ND - 1)))
    slot2 = PS[1][0:1, 138:139]
    for m in range(ND):
        b.t(lambda e, m=m, slot2=slot2: e.matmul(
            slot2, rwsb[:, m:m + 1], S["cONESCOL"][:, 0:1], start=(m == 0), stop=(m == ND - 1)))
    tmm = sT.n
    b.w("vector", sT, tmm)
    b.v(lambda e, slot=slot: e.tensor_copy(scal[0:1, 3:4], slot))
    tv = b.v(lambda e, slot2=slot2: e.tensor_copy(scal[0:1, 4:5], slot2))
    b.w("tensor", sV, tv)
    b.t(lambda e: e.matmul(PS[1][:, 140:141], S["cONESROW"][:, :], scal[0:1, 3:4], start=True, stop=True))
    tmm = b.t(lambda e: e.matmul(PS[1][:, 141:142], S["cONESROW"][:, :], scal[0:1, 4:5], start=True, stop=True))
    b.w("vector", sT, tmm)
    b.v(lambda e: e.tensor_copy(wrc[:], PS[1][:, 140:141]))
    b.v(lambda e: e.tensor_copy(srepc[:], PS[1][:, 141:142]))
    if maybe_stop("p6"):
        return nc, dram
    slot = PS[7][0:1, 0:448]
    for c in range(ND):
        b.t(lambda e, c=c, slot=slot: e.matmul(
            slot, S["cONESCOL"][:, :], acat[:, c * 449:c * 449 + 448],
            start=(c == 0), stop=(c == ND - 1)))
    tmm = sT.n
    b.w("vector", sT, tmm)
    b.v(lambda e, slot=slot: e.tensor_scalar(negs[:, :], slot, -1.0, None, ALU.mult))
    b.fence()
    tap("acat0", acat[:, 0:448])
    tap("acat7", acat[:, 7 * 448:8 * 448])
    tap("outb", outb[:, :])
    tap("wbias_rep", wbias_rep[:, 0:D])
    tap("orv", orv[:, 0:1])
    tap("negs", negs[0:1, :])

    aS = Scope()

    def phaseA(bi, t_xload):
        XT2 = [aS.g("xT0", D), aS.g("xT1", D)]
        XSQ = aS.g("xsq", D)
        PP = [aS.g("pp0", 449), aS.g("pp1", 449)]
        CUM = aS.g("cum", 256)
        RT2 = [aS.g("rt0", 128), aS.g("rt1", 128)]
        MUR = aS.row("murow", 128)
        STC = aS.g("stc", 8)
        BNS = aS.g("bns", 12)
        b.w("tensor", qS, t_xload)
        tv = b.v(lambda e: e.memset(carry[:, :], 0.0))
        b.w("tensor", sV, tv)
        ev_prev = {}
        for t in range(NT):
            xT = XT2[t % 2]
            pp = PP[t % 2]
            rt = RT2[t % 2]
            if t >= 1:
                b.w("tensor", sV, ev_prev["xT"])
            for c in range(ND):
                ps = (PS[2] if c < 4 else PS[3])[:, (c % 4) * 128:(c % 4) * 128 + 128]
                b.t(lambda e, t=t, c=c, ps=ps: e.transpose(
                    ps, x_tok[:, t * D + c * 128:t * D + (c + 1) * 128], S["cIDENT"][:, :]))
            tm_tr = sT.n
            b.w("vector", sT, tm_tr)
            b.v(lambda e, xT=xT: e.tensor_copy(xT[:, 0:512], PS[2][:, 0:512]))
            ev_xT = b.v(lambda e, xT=xT: e.tensor_copy(xT[:, 512:1024], PS[3][:, 0:512]))
            ev_prev["xT"] = ev_xT
            b.v(lambda e, t=t: e.bn_stats(BNS[:, 0:6], x_tok[:, t * D:t * D + 512]))
            b.v(lambda e, t=t: e.bn_stats(BNS[:, 6:12], x_tok[:, t * D + 512:(t + 1) * D]))
            ev_bn = b.v(lambda e: e.bn_aggr(STC[:, 0:2], BNS[:, 0:12]))
            b.w("gpsimd", sT, tm_tr)
            b.w("gpsimd", sV, ev_bn)
            b.dma("gpsimd", lambda e, t=t: e.dma_start(
                out=x_tok[:, t * D:(t + 1) * D], in_=wbias_rep[:, :], accum_op=ALU.add))
            t_wb = qprobe("gpsimd")
            b.w("scalar", sV, ev_bn)
            sq2 = b.s(lambda e: e.activation(STC[:, 2:3], STC[:, 1:2], AF.Sqrt, bias=epsc[:, 0:1]))
            b.w("vector", sS, sq2)
            ev_rstd = b.v(lambda e: e.reciprocal(STC[:, 3:4], STC[:, 2:3]))
            if t >= 1:
                b.w("tensor", sV, ev_prev["pp"])
            b.w("tensor", sV, ev_xT)
            for c in range(ND):
                b.t(lambda e, c=c, xT=xT: e.matmul(
                    PS[0][:, 0:449], xT[:, c * 128:(c + 1) * 128],
                    acat[:, c * 449:(c + 1) * 449], start=(c == 0), stop=False))
            b.w("tensor", sV, ev_bn)
            tmu = b.t(lambda e: e.transpose(PS[5][0:1, 0:128], STC[:, 0:1], S["cIDENT"][:, :]))
            b.w("vector", sT, tmu)
            ev_mur = b.v(lambda e: e.tensor_copy(MUR[:, :], PS[5][0:1, 0:128]))
            b.w("tensor", sV, ev_mur)
            b.t(lambda e: e.matmul(PS[0][:, 0:448], MUR[:, :], negs[0:1, :], start=False, stop=True))
            tm_corr = sT.n
            b.w("vector", sT, tm_corr)
            b.w("vector", sV, ev_rstd)
            ev_pp = b.v(lambda e, pp=pp: e.tensor_scalar(
                pp[:, 0:448], PS[0][:, 0:448], STC[:, 3:4], None, ALU.mult))
            ev_prev["pp"] = ev_pp
            if t >= 1:
                b.w("tensor", sV, ev_prev["cum"])
            b.w("tensor", sV, ev_pp)
            b.t(lambda e, pp=pp: e.matmul(PS[4][:, 0:256], S["cTRIT"][:, :], pp[:, 192:448], start=True, stop=True))
            tm_cum = b.t(lambda e, pp=pp: e.matmul(PS[4][:, 256:512], S["cONES128"][:, :], pp[:, 192:448], start=True, stop=True))
            b.w("vector", sT, tm_cum)
            b.v(lambda e: e.tensor_add(CUM[:, :], PS[4][:, 0:256], carry[:, :]))
            ev_cum = b.v(lambda e: e.tensor_add(carry[:, :], carry[:, :], PS[4][:, 256:512]))
            ev_prev["cum"] = ev_cum
            b.v(lambda e, pp=pp: e.tensor_mul(XSQ[:, 0:64], pp[:, 0:64], CUM[:, 0:64]))
            b.v(lambda e, pp=pp: e.tensor_mul(XSQ[:, 64:128], pp[:, 64:128], CUM[:, 64:128]))
            b.v(lambda e: e.tensor_add(XSQ[:, 0:64], XSQ[:, 0:64], XSQ[:, 64:128]))
            b.v(lambda e, rt=rt, t=t: e.tensor_scalar(
                rt[:, 0:64], XSQ[:, 0:64], invc[:, t:t + 1], None, ALU.mult))
            b.v(lambda e, pp=pp: e.tensor_mul(XSQ[:, 64:128], pp[:, 128:192], CUM[:, 128:192]))
            ev_R = b.v(lambda e, rt=rt, t=t: e.scalar_tensor_tensor(
                rt[:, 64:128], XSQ[:, 64:128], invc2[:, t:t + 1], CUM[:, 192:256], ALU.mult, ALU.mult))
            b.w("tensor", sV, ev_R)
            trt = b.t(lambda e, rt=rt: e.transpose(PS[5][:, 128:256], rt[:, :], S["cIDENT"][:, :]))
            b.w("vector", sT, trt)
            ev_rt = b.v(lambda e, rt=rt: e.tensor_copy(rt[:, :], PS[5][:, 128:256]))
            b.w("tensor", sV, ev_rt)
            b.t(lambda e, rt=rt: e.matmul(PS[5][:, 497:498], rt[:, :], orv[:, 0:1], start=True, stop=True))
            tm_w = sT.n
            b.w("vector", sT, tm_w)
            b.v(lambda e, t=t, pp=pp: e.tensor_scalar(
                wraw[:, t:t + 1], PS[0][:, 448:449], wrc[:, 0:1], None, ALU.add))
            ev_wraw = b.v(lambda e, t=t: e.tensor_add(
                wraw[:, t:t + 1], wraw[:, t:t + 1], PS[5][:, 497:498]))
            ev_prev["cum"] = ev_wraw
            ev_prev["pp"] = ev_wraw
            if t >= 1:
                b.w("tensor", sV, ev_prev["out"])
            for n2 in range(2):
                b.t(lambda e, rt=rt, n2=n2: e.matmul(
                    (PS[6] if n2 == 0 else PS[7])[:, 0:512], rt[:, :],
                    outb[:, n2 * 512:(n2 + 1) * 512], start=True, stop=True))
            tm_out = sT.n
            b.w("vector", sT, tm_out)
            b.w("vector", qG, t_wb)
            b.v(lambda e, t=t: e.tensor_add(
                x_tok[:, t * D:t * D + 512], x_tok[:, t * D:t * D + 512], PS[6][:, 0:512]))
            ev_out = b.v(lambda e, t=t: e.tensor_add(
                x_tok[:, t * D + 512:(t + 1) * D], x_tok[:, t * D + 512:(t + 1) * D], PS[7][:, 0:512]))
            ev_prev["out"] = ev_out
        b.fence()
        tap("xA%d" % bi, x_tok[:, 0:4096])
        tap("wrawA%d" % bi, wraw[:, :])
        tap("xrA%d" % bi, b2c[:, :])
        tap("rorvA%d" % bi, ccum[:, :])
        tap("wrcA%d" % bi, wrc[:, :])
        tap("srepA%d" % bi, srepc[:, :])

    fS = Scope()

    def phaseFpre(bi):
        DUMP = fS.g("dump", D)
        WREP = fS.g("wrep", T)
        ST2 = [fS.g("st0", RS), fS.g("st1", RS)]
        WTR = fS.g("wtr", 128)
        WROW = fS.row("wrow", T)
        BASE2 = [fS.g("base0", D), fS.g("base1", D)]
        AC = fS.g("acol", NT + 8)
        BC = fS.g("bcol", NT)
        TMP = fS.g("ftmp", NT)
        BN2 = fS.g("bn2", 12)
        STAT2 = fS.g("stat2", 2)
        for t in range(NT):
            b.v(lambda e, t=t: e.bn_stats(BN2[:, 0:6], x_tok[:, t * D:t * D + 512]))
            b.v(lambda e, t=t: e.bn_stats(BN2[:, 6:12], x_tok[:, t * D + 512:(t + 1) * D]))
            b.v(lambda e: e.bn_aggr(STAT2[:, :], BN2[:, 0:12]))
            b.v(lambda e, t=t: e.tensor_copy(stmu[:, t:t + 1], STAT2[:, 0:1]))
            b.v(lambda e, t=t: e.tensor_copy(stro[:, t:t + 1], STAT2[:, 1:2]))
        b.w("scalar", sV, sV.n)
        ts_ = b.s(lambda e: e.activation(stro[:, :], stro[:, :], AF.Sqrt, bias=epsc[:, 0:1]))
        b.w("vector", sS, ts_)
        b.v(lambda e: e.reciprocal(stro[:, :], stro[:, :]))
        b.v(lambda e: e.tensor_mul(b2c[:, :], stmu[:, :], stro[:, :]))
        b.v(lambda e: e.tensor_scalar(b2c[:, :], b2c[:, :], -1.0, None, ALU.mult))
        b.v(lambda e: e.tensor_scalar(TMP[:, :], stmu[:, :], srepc[:, 0:1], None, ALU.mult))
        b.v(lambda e: e.tensor_sub(TMP[:, :], wraw[:, :], TMP[:, :]))
        b.v(lambda e: e.tensor_mul(TMP[:, :], TMP[:, :], stro[:, :]))
        tv_w = b.v(lambda e: e.tensor_scalar(TMP[:, :], TMP[:, :], rbc[:, 0:1], None, ALU.add))
        b.w("tensor", sV, tv_w)
        tmm = b.t(lambda e: e.transpose(PS[5][0:16, 0:128], TMP[:, 0:16], S["cIDENT"][:, :]))
        b.w("vector", sT, tmm)
        tv = b.v(lambda e: e.tensor_copy(WTR[0:16, :], PS[5][0:16, 0:128]))
        b.w("gpsimd", sV, tv)
        b.dma("gpsimd", lambda e: e.dma_start(out=WROW[:, :], in_=WTR[0:16, :]))
        tg = qprobe("gpsimd")
        b.w("tensor", qG, tg)
        wrep_ev = {}
        for n4 in range(4):
            ps = (PS[2] if n4 % 2 == 0 else PS[3])[:, 0:512]
            if n4 >= 2:
                b.w("tensor", sV, wrep_ev[n4 - 2])
            b.t(lambda e, n4=n4, ps=ps: e.matmul(
                ps, S["cONESROW"][:, :], WROW[0:1, n4 * 512:(n4 + 1) * 512], start=True, stop=True))
            tmm = sT.n
            b.w("vector", sT, tmm)
            wrep_ev[n4] = b.v(lambda e, n4=n4, ps=ps: e.tensor_copy(WREP[:, n4 * 512:(n4 + 1) * 512], ps))
        SMC = fS.g("smc", NT)
        RNKD = fS.g("rnkd", T)
        for j in range(NT):
            b.v(lambda e, j=j: e.tensor_scalar(
                RNKD[:, :], WREP[:, :], TMP[:, j:j + 1], 0.0, ALU.is_lt, ALU.add,
                accum_out=SMC[:, j:j + 1]))
        tv = b.v(lambda e: e.tensor_scalar(selm[:, :], SMC[:, :], float(T - KC) - 0.5, None, ALU.is_gt))
        b.w("tensor", sV, tv)
        b.t(lambda e: e.matmul(PS[4][:, 0:16], S["cTRIST"][:, :], selm[:, :], start=True, stop=False))
        tmm = b.t(lambda e: e.matmul(PS[5][0:1, 0:16], S["cONESCOL"][:, :], selm[:, :], start=True, stop=True))
        b.w("vector", sT, tmm)
        tv = b.v(lambda e: e.tensor_copy(WTR[0:1, 0:16], PS[5][0:1, 0:16]))
        b.w("tensor", sV, tv)
        tmm = b.t(lambda e: e.transpose(PS[5][0:16, 16:17], WTR[0:1, 0:16], S["cIDENT"][0:1, 0:1]))
        b.w("vector", sT, tmm)
        tv = b.v(lambda e: e.tensor_copy(WTR[0:16, 16:17], PS[5][0:16, 16:17]))
        b.w("tensor", sV, tv)
        tmm = b.t(lambda e: e.matmul(PS[5][0:16, 17:18], S["cTRIST"][0:16, 0:16], WTR[0:16, 16:17], start=True, stop=True))
        b.w("vector", sT, tmm)
        tv = b.v(lambda e: e.tensor_copy(WTR[0:16, 18:19], PS[5][0:16, 17:18]))
        b.w("tensor", sV, tv)
        tmm = b.t(lambda e: e.transpose(PS[5][0:1, 19:35], WTR[0:16, 18:19], S["cIDENT"][0:16, 0:16]))
        b.w("vector", sT, tmm)
        tv = b.v(lambda e: e.tensor_copy(WTR[0:1, 20:36], PS[5][0:1, 19:35]))
        b.w("tensor", sV, tv)
        b.t(lambda e: e.matmul(PS[4][:, 0:16], S["cONESROW"][:, :], WTR[0:1, 20:36], start=False, stop=True))
        tmm = sT.n
        b.w("vector", sT, tmm)
        tv_cc = b.v(lambda e: e.tensor_copy(ccum[:, :], PS[4][:, 0:16]))
        c4v = cols5.rearrange("p (t f) -> p t f", f=5)
        b.v(lambda e: e.tensor_copy(c4v[:, :, 0], S["cARANGE"][:, :]))
        b.v(lambda e: e.tensor_copy(c4v[:, :, 1], TMP[:, :]))
        b.v(lambda e: e.tensor_copy(c4v[:, :, 2], stro[:, :]))
        b.v(lambda e: e.tensor_copy(c4v[:, :, 3], b2c[:, :]))
        b.v(lambda e: e.tensor_copy(c4v[:, :, 4], S["cONES128"][:, 0:NT]))
        ev_st = {}
        for j in range(NT):
            st = ST2[j % 2]
            if j >= 2:
                b.w("vector", sT, ev_st[j - 2])
            b.v(lambda e, st=st, j=j: e.tensor_scalar(
                st[:, :], S["cIOTAR"][:, :], ccum[:, j:j + 1], None, ALU.is_equal))
            tv = b.v(lambda e, st=st, j=j: e.tensor_scalar(
                st[:, :], st[:, :], selm[:, j:j + 1], None, ALU.mult))
            b.w("tensor", sV, tv)
            for rb in range(2):
                b.t(lambda e, st=st, rb=rb, j=j: e.matmul(
                    PS[2][:, 5 * rb:5 * rb + 5], st[:, rb * 128:(rb + 1) * 128],
                    cols5[:, j * 5:(j + 1) * 5], start=(j == 0 and rb == 0),
                    stop=(j == NT - 1 and rb == 1)))
            ev_st[j] = sT.n
        b.w("vector", sT, sT.n)
        tv = b.v(lambda e: e.tensor_copy(selcols[:, :], PS[2][:, 0:10]))
        b.w("vector", sV, tv)
        for rb in range(2):
            b.v(lambda e, rb=rb, bi=bi: e.tensor_scalar(
                AC[:, 16 + rb:17 + rb], selcols[:, 5 * rb:5 * rb + 1], float(bi * T), None, ALU.add))
            b.v(lambda e, rb=rb: e.tensor_scalar(
                AC[:, 18 + rb:19 + rb], selcols[:, 5 * rb + 4:5 * rb + 5], 1.0, -4096.0, ALU.subtract, ALU.mult))
            b.v(lambda e, rb=rb: e.tensor_add(AC[:, 18 + rb:19 + rb], AC[:, 18 + rb:19 + rb], AC[:, 16 + rb:17 + rb]))
            b.v(lambda e, rb=rb: e.tensor_copy(selidx[:, rb:rb + 1], AC[:, 16 + rb:17 + rb]))
        tv_idx = b.v(lambda e: e.tensor_copy(selidx[:, 2:4], AC[:, 18:20]))
        b.v(lambda e: e.tensor_mul(AC[:, 0:NT], selm[:, :], stro[:, :]))
        b.v(lambda e: e.tensor_sub(AC[:, 0:NT], stro[:, :], AC[:, 0:NT]))
        b.v(lambda e: e.tensor_scalar(AC[:, 0:NT], AC[:, 0:NT], 1.0, None, ALU.add))
        b.v(lambda e: e.tensor_mul(BC[:, :], selm[:, :], b2c[:, :]))
        tv_ab = b.v(lambda e: e.tensor_sub(BC[:, :], b2c[:, :], BC[:, :]))
        b.w("scalar", sV, tv_ab)
        st_prev = {}
        t_store = None
        for t in range(NT):
            bb = BASE2[t % 2]
            if t >= 2:
                b.w("scalar", qS, st_prev[t - 2])
            ts_ = b.s(lambda e, t=t, bb=bb: e.activation(
                bb[:, :], x_tok[:, t * D:(t + 1) * D], AF.Identity,
                bias=BC[:, t:t + 1], scale=AC[:, t:t + 1]))
            b.w("sync", sS, ts_)
            t_store = b.dma("sync", lambda e, t=t, bb=bb, bi=bi: e.dma_start(
                out=out_t[bi, t * 128:(t + 1) * 128, :], in_=bb[:, :]))
            st_prev[t] = t_store
        t_store = qprobe("sync")
        b.w("gpsimd", qS, t_store)
        b.w("gpsimd", sV, tv_idx)
        b.w("gpsimd", qG, t_outinit)
        out_flat = out_t.rearrange("b t d -> (b t) d")
        for rb in range(2):
            b.emit("gpsimd", lambda e, rb=rb, out_flat=out_flat: e.indirect_dma_start(
                out=XRS[:, rb * D:(rb + 1) * D], out_offset=None,
                in_=out_flat, in_offset=IndirectOffsetOnAxis(ap=selidx[:, rb:rb + 1], axis=0),
            ), qG, 16)
            b.emit("gpsimd", lambda e, rb=rb: e.indirect_dma_start(
                out=CSG[:, rb * RS:(rb + 1) * RS], out_offset=None,
                in_=dram["CSTAB"][:, :], in_offset=IndirectOffsetOnAxis(ap=selidx[:, rb:rb + 1], axis=0),
            ), qG, 16)
        qprobe("gpsimd")
        b.fence()
        tap("w%d" % bi, TMP[:, :])
        tap("selm%d" % bi, selm[:, :])
        tap("ccum%d" % bi, ccum[:, :])
        tap("selcols%d" % bi, selcols[:, :])
        tap("xrs%d" % bi, XRS[:, :])

    fmS = Scope()

    def phaseFmain(bi, xload_next):
        m = fmS
        XLN = m.g("xln", 2 * D)
        HTOK = m.g("htok", 2 * D)
        HT = m.g("hT", 2048)
        QT = m.g("qT", 2048)
        KT = m.g("kT", 2048)
        VT_ = m.g("vT", 2048)
        CTX = m.g("ctxT", 2048)
        COS2 = m.g("cos2", RS)
        SIN2 = m.g("sin2", RS)
        ASB = m.g("asb", RS)
        ATB = m.g("atb", 2 * RS)
        VTOK = m.g("vtok", 2 * 128)
        H1G = m.g("h1g", 4 * RS)
        WBUF = [m.g("wb0", D), m.g("wb1", D), m.g("wb2", D)]
        WBUF2 = [m.g("w20", D), m.g("w21", D), m.g("w22", D)]
        RCOL = m.g("rcol", 8)
        FIN = m.g("fin", D)
        DUMP.update(dict(xln=XLN, htok=HTOK, hT=HT, qT=QT, kT=KT, vT=VT_,
                         ctx=CTX, cos2=COS2, sin2=SIN2, h1g=H1G, fin=FIN, rcol=RCOL))
        if xload_next is not None:
            b.w("sync", sS, sS.n)
            b.dma("sync", lambda e, xload_next=xload_next: e.dma_start(
                out=x_tok.rearrange("q (n d) -> q n d", d=D),
                in_=x_in[xload_next].rearrange("(n p) d -> p n d", p=128)))
            t_xn = qprobe("sync")
        else:
            t_xn = None
        for rb in range(2):
            b.s(lambda e, rb=rb: e.activation(
                XLN[:, rb * D:(rb + 1) * D], XRS[:, rb * D:(rb + 1) * D], AF.Identity,
                scale=selcols[:, 5 * rb + 2:5 * rb + 3], bias=selcols[:, 5 * rb + 3:5 * rb + 4]))
        for rb in range(2):
            b.s(lambda e, rb=rb: e.activation(HTOK[:, rb * D:(rb + 1) * D], XLN[:, rb * D:(rb + 1) * D],
                                              AF.Square, accum_out=RCOL[:, rb:rb + 1]))
        ts_ = b.s(lambda e: e.activation(RCOL[:, 2:4], RCOL[:, 0:2], AF.Sqrt, scale=epsc[:, 2:3], bias=epsc[:, 1:2]))
        b.w("vector", sS, ts_)
        b.v(lambda e: e.reciprocal(RCOL[:, 2:4], RCOL[:, 2:4]))
        for rb in range(2):
            b.v(lambda e, rb=rb: e.tensor_scalar(
                HTOK[:, rb * D:(rb + 1) * D], XLN[:, rb * D:(rb + 1) * D],
                RCOL[:, 2 + rb:3 + rb], None, ALU.mult))
        tv_hn = sV.n

        def tok2feat(src, dst, scale_cols, tv_ready):
            b.w("tensor", sV, tv_ready)
            evs = {}
            banks = (PS[2], PS[3], PS[4], PS[5])
            for dc in range(ND):
                bank = banks[dc % 4]
                if dc >= 4:
                    b.w("tensor", sV, evs[dc - 4])
                for rb in range(2):
                    b.t(lambda e, dc=dc, rb=rb, bank=bank, src=src: e.transpose(
                        bank[:, rb * 128:(rb + 1) * 128],
                        src[:, rb * D + dc * 128:rb * D + (dc + 1) * 128], S["cIDENT"][:, :]))
                tmm = sT.n
                b.w("vector", sT, tmm)
                if scale_cols is not None:
                    evs[dc] = b.v(lambda e, dc=dc, bank=bank, dst=dst, sc=scale_cols:
                                  e.tensor_scalar(dst[:, dc * RS:(dc + 1) * RS],
                                                  bank[:, 0:256], sc[:, dc:dc + 1], None, ALU.mult))
                else:
                    evs[dc] = b.v(lambda e, dc=dc, bank=bank, dst=dst:
                                  e.tensor_copy(dst[:, dc * RS:(dc + 1) * RS], bank[:, 0:256]))
            b.w("tensor", sV, sV.n)

        tok2feat(HTOK, HT, ln1sb, tv_hn)
        if bi == 0:
            tap("hnT", HT[:, :])
        for rb in range(2):
            for which, dstb in ((0, COS2), (1, SIN2)):
                ps = PS[4][:, 0:128]
                b.t(lambda e, rb=rb, which=which, ps=ps: e.transpose(
                    ps, CSG[:, rb * RS + which * 128:rb * RS + (which + 1) * 128], S["cIDENT"][:, :]))
                tmm = sT.n
                b.w("vector", sT, tmm)
                tv = b.v(lambda e, rb=rb, ps=ps, dstb=dstb: e.tensor_copy(
                    dstb[:, rb * 128:(rb + 1) * 128], ps))
                b.w("tensor", sV, tv)

        def proj(wname, dst, rope):
            mm_done = {}
            b.w("sync", sT, sT.n)
            for k in range(ND):
                if k >= 3:
                    b.w("sync", sT, mm_done[k - 3])
                tl = b.dma("sync", lambda e, wname=wname, k=k: e.dma_start(
                    out=WBUF[k % 3][:, :], in_=dram[wname][k * 128:(k + 1) * 128, :]))
                b.w("tensor", qS, (tl + 16) if k < ND - 1 else qprobe("sync"))
                for mt in range(ND):
                    ps = PS[mt // 2][:, (mt % 2) * 256:(mt % 2) * 256 + 256]
                    b.t(lambda e, k=k, mt=mt, ps=ps: e.matmul(
                        ps, WBUF[k % 3][:, mt * 128:(mt + 1) * 128], HT[:, k * RS:(k + 1) * RS],
                        start=(k == 0 and mt % 2 == 0), stop=(k == ND - 1 and mt % 2 == 1)))
                mm_done[k] = sT.n
            b.w("vector", sT, sT.n)
            for mt in range(ND):
                ps = PS[mt // 2][:, (mt % 2) * 256:(mt % 2) * 256 + 256]
                b.v(lambda e, mt=mt, ps=ps, dst=dst: e.tensor_copy(dst[:, mt * RS:(mt + 1) * RS], ps))
            b.w("tensor", sV, sV.n)
            if rope:
                rope_ev = {}
                for mt in range(ND):
                    ps = (PS[6] if mt % 2 == 0 else PS[7])[:, 0:256]
                    if mt >= 2:
                        b.w("tensor", sV, rope_ev[mt - 2])
                    b.t(lambda e, mt=mt, ps=ps, dst=dst: e.matmul(
                        ps, S["cROTT"][:, :], dst[:, mt * RS:(mt + 1) * RS], start=True, stop=True))
                    tmm = sT.n
                    b.w("vector", sT, tmm)
                    b.v(lambda e, mt=mt, dst=dst: e.tensor_mul(
                        ASB[:, :], dst[:, mt * RS:(mt + 1) * RS], COS2[:, :]))
                    b.v(lambda e, mt=mt, ps=ps: e.tensor_mul(ATB[:, 0:RS], ps, SIN2[:, :]))
                    rope_ev[mt] = b.v(lambda e, mt=mt, dst=dst: e.tensor_add(
                        dst[:, mt * RS:(mt + 1) * RS], ASB[:, :], ATB[:, 0:RS]))
                b.w("tensor", sV, sV.n)

        if bi == 0:
            tap("xlnF", XLN[:, :])
        proj("WQ", QT, True)
        if bi == 0:
            tap("qTrope", QT[:, :])
            tap("cos2", COS2[:, :])
            tap("sin2", SIN2[:, :])
        proj("WK", KT, True)
        if bi == 0:
            tap("qT_afterK", QT[:, :])
        proj("WV", VT_, False)
        if bi == 0:
            tap("qT_afterV", QT[:, :])
        if bi == 0:
            tap("vT", VT_[:, :])
        for h in range(H):
            mt, po = h // 2, 64 * (h % 2)
            if bi == 0 and h in (0, 2, 4, 6):
                tap("qT_at%d" % h, QT[:, :])
            if h % 2 == 0:
                for kc in range(2):
                    ps = PS[5][:, kc * 128:(kc + 1) * 128]
                    b.t(lambda e, mt=mt, kc=kc, ps=ps: e.transpose(
                        ps, VT_[:, mt * RS + kc * 128:mt * RS + (kc + 1) * 128], S["cIDENT"][:, :]))
                tmm = sT.n
                b.w("vector", sT, tmm)
                b.v(lambda e: e.tensor_copy(VTOK[:, 0:128], PS[5][:, 0:128]))
                tv = b.v(lambda e: e.tensor_copy(VTOK[:, 128:256], PS[5][:, 128:256]))
                b.w("tensor", sV, tv)
            for qc in range(2):
                ps = (PS[4] if qc == 0 else PS[7])[:, 0:256]
                b.t(lambda e, mt=mt, po=po, qc=qc, ps=ps: e.matmul(
                    ps, QT[po:po + 64, mt * RS + qc * 128:mt * RS + (qc + 1) * 128],
                    KT[po:po + 64, mt * RS:(mt + 1) * RS], start=True, stop=True))
                tmm = sT.n
                b.w("vector", sT, tmm)
                b.v(lambda e, qc=qc, ps=ps: e.scalar_tensor_tensor(
                    ASB[:, :], ps, 0.125, S["cCAUSAL"][:, qc * RS:(qc + 1) * RS], ALU.mult, ALU.add))
                tv = b.v(lambda e: e.reduce_max(RCOL[:, 6:7], ASB[:, :], axis=AXX, negate=True))
                b.w("scalar", sV, tv)
                ts_ = b.s(lambda e: e.activation(ASB[:, :], ASB[:, :], AF.Exp, bias=RCOL[:, 6:7]))
                b.w("vector", sS, ts_)
                b.v(lambda e: e.reduce_sum(RCOL[:, 7:8], ASB[:, :], axis=AXX))
                b.v(lambda e: e.reciprocal(RCOL[:, 7:8], RCOL[:, 7:8]))
                tv = b.v(lambda e: e.tensor_scalar(ASB[:, :], ASB[:, :], RCOL[:, 7:8], None, ALU.mult))
                if bi == 0 and h in (0, 7) and qc == 0:
                    tap("asb_h%d" % h, ASB[:, :])
                b.w("tensor", sV, tv)
                for kc in range(2):
                    ps2 = PS[5][:, 256:384]
                    b.t(lambda e, kc=kc, ps2=ps2: e.transpose(ps2, ASB[:, kc * 128:(kc + 1) * 128], S["cIDENT"][:, :]))
                    tmm = sT.n
                    b.w("vector", sT, tmm)
                    tv = b.v(lambda e, kc=kc, qc=qc, ps2=ps2: e.tensor_copy(
                        ATB[:, kc * RS + qc * 128:kc * RS + (qc + 1) * 128], ps2))
                    b.w("tensor", sV, tv)
            for kc in range(2):
                if po == 0:
                    b.t(lambda e, kc=kc: e.matmul(
                        PS[6][0:64, 0:256], VTOK[:, kc * 128:kc * 128 + 64],
                        ATB[:, kc * RS:(kc + 1) * RS], start=(kc == 0), stop=(kc == 1)))
                else:
                    b.t(lambda e, kc=kc: e.matmul(
                        PS[6][64:128, 0:256], VTOK[:, kc * 128 + 64:kc * 128 + 128],
                        ATB[:, kc * RS:(kc + 1) * RS], start=(kc == 0), stop=(kc == 1),
                        tile_position=(0, 64)))
            tmm = sT.n
            b.w("vector", sT, tmm)
            tv = b.v(lambda e, mt=mt, po=po: e.tensor_copy(
                CTX[po:po + 64, mt * RS:(mt + 1) * RS], PS[6][po:po + 64, 0:256]))
            if bi == 0 and h in (0, 7):
                tap("atb_h%d" % h, ATB[:, :])
            b.w("tensor", sV, tv)
        if bi == 0:
            tap("ctxT", CTX[:, :])
        mm_done = {}
        b.w("sync", sT, sT.n)
        for k in range(ND):
            if k >= 3:
                b.w("sync", sT, mm_done[k - 3])
            tl = b.dma("sync", lambda e, k=k: e.dma_start(
                out=WBUF[k % 3][:, :], in_=dram["WOf"][k * 128:(k + 1) * 128, :]))
            b.w("tensor", qS, (tl + 16) if k < ND - 1 else qprobe("sync"))
            for mt in range(ND):
                ps = PS[mt // 2][:, (mt % 2) * 256:(mt % 2) * 256 + 256]
                b.t(lambda e, k=k, mt=mt, ps=ps: e.matmul(
                    ps, WBUF[k % 3][:, mt * 128:(mt + 1) * 128], CTX[:, k * RS:(k + 1) * RS],
                    start=(k == 0 and mt % 2 == 0), stop=(k == ND - 1 and mt % 2 == 1)))
            mm_done[k] = sT.n
        b.w("vector", sT, sT.n)
        for mt in range(ND):
            ps = PS[mt // 2][:, (mt % 2) * 256:(mt % 2) * 256 + 256]
            b.v(lambda e, mt=mt, ps=ps: e.tensor_copy(KT[:, mt * RS:(mt + 1) * RS], ps))
        b.w("tensor", sV, sV.n)
        evs = {}
        banks = (PS[2], PS[3], PS[4], PS[5])
        XLNV = XLN.rearrange("p (r x) -> p r x", x=D)
        for dc in range(ND):
            bank = banks[dc % 4]
            if dc >= 4:
                b.w("tensor", sV, evs[dc - 4])
            for rb in range(2):
                b.t(lambda e, dc=dc, rb=rb, bank=bank: e.transpose(
                    bank[:, rb * 128:(rb + 1) * 128],
                    KT[:, dc * RS + rb * 128:dc * RS + (rb + 1) * 128], S["cIDENT"][:, :]))
            tmm = sT.n
            b.w("vector", sT, tmm)
            evs[dc] = b.v(lambda e, dc=dc, bank=bank: e.tensor_add(
                XLNV[:, :, dc * 128:(dc + 1) * 128],
                XLNV[:, :, dc * 128:(dc + 1) * 128], bank[:, 0:256]))
        if bi == 0:
            tap("x1tok", XLN[:, :])
        b.w("scalar", sV, sV.n)
        for rb in range(2):
            b.s(lambda e, rb=rb: e.activation(HTOK[:, rb * D:(rb + 1) * D], XLN[:, rb * D:(rb + 1) * D],
                                              AF.Square, accum_out=RCOL[:, 4 + rb:5 + rb]))
        ts_ = b.s(lambda e: e.activation(RCOL[:, 4:6], RCOL[:, 4:6], AF.Sqrt, scale=epsc[:, 2:3], bias=epsc[:, 1:2]))
        b.w("vector", sS, ts_)
        b.v(lambda e: e.reciprocal(RCOL[:, 4:6], RCOL[:, 4:6]))
        for rb in range(2):
            b.v(lambda e, rb=rb: e.tensor_scalar(
                HTOK[:, rb * D:(rb + 1) * D], XLN[:, rb * D:(rb + 1) * D],
                RCOL[:, 4 + rb:5 + rb], None, ALU.mult))
        tok2feat(HTOK, QT, ln2sb, sV.n)
        h1_done = {}
        w2g_done = {}
        b.w("sync", sT, sT.n)
        b.w("scalar", sT, sT.n)
        for fc in range(NFC):
            for k in range(ND):
                g = fc * ND + k
                if g >= 3:
                    b.w("sync", sT, h1_done[g - 3])
                tl = b.dma("sync", lambda e, fc=fc, k=k, g=g: e.dma_start(
                    out=WBUF[g % 3][:, 0:FC], in_=dram["W1"][k * 128:(k + 1) * 128, fc * FC:(fc + 1) * FC]))
                b.w("tensor", qS, (tl + 16) if g < NFC * ND - 1 else qprobe("sync"))
                for mf in range(FC // 128):
                    ps = PS[4 + mf // 2][:, (mf % 2) * 256:(mf % 2) * 256 + 256]
                    b.t(lambda e, k=k, mf=mf, g=g, ps=ps: e.matmul(
                        ps, WBUF[g % 3][:, mf * 128:(mf + 1) * 128], QT[:, k * RS:(k + 1) * RS],
                        start=(k == 0 and mf % 2 == 0), stop=(k == ND - 1 and mf % 2 == 1)))
                h1_done[g] = sT.n
            b.w("scalar", sT, sT.n)
            for mf in range(FC // 128):
                ps = PS[4 + mf // 2][:, (mf % 2) * 256:(mf % 2) * 256 + 256]
                b.s(lambda e, mf=mf, ps=ps: e.activation(H1G[:, mf * RS:(mf + 1) * RS], ps, AF.Gelu_apprx_tanh))
            b.w("tensor", sS, sS.n)
            for kf in range(FC // 128):
                g2 = fc * (FC // 128) + kf
                if g2 >= 3:
                    b.w("gpsimd", sT, w2g_done[g2 - 3])
                tl2 = b.dma("gpsimd", lambda e, fc=fc, kf=kf, g2=g2: e.dma_start(
                    out=WBUF2[g2 % 3][:, :], in_=dram["W2"][fc * FC + kf * 128:fc * FC + (kf + 1) * 128, :]))
                b.w("tensor", qG, (tl2 + 16) if g2 < NFC * (FC // 128) - 1 else qprobe("gpsimd"))
                for mt in range(ND):
                    ps = PS[mt // 2][:, (mt % 2) * 256:(mt % 2) * 256 + 256]
                    b.t(lambda e, kf=kf, mt=mt, g2=g2, ps=ps, fc=fc: e.matmul(
                        ps, WBUF2[g2 % 3][:, mt * 128:(mt + 1) * 128], H1G[:, kf * RS:(kf + 1) * RS],
                        start=(fc == 0 and kf == 0 and mt % 2 == 0),
                        stop=(fc == NFC - 1 and kf == FC // 128 - 1 and mt % 2 == 1)))
                w2g_done[g2] = sT.n
        b.w("vector", sT, sT.n)
        for mt in range(ND):
            ps = PS[mt // 2][:, (mt % 2) * 256:(mt % 2) * 256 + 256]
            b.v(lambda e, mt=mt, ps=ps: e.tensor_copy(KT[:, mt * RS:(mt + 1) * RS], ps))
        b.w("tensor", sV, sV.n)
        t_sc0 = None
        evs = {}
        fbanks = (PS[2], PS[3], PS[4], PS[5])
        for rb in range(2):
            for dc in range(ND):
                k = rb * ND + dc
                ps = fbanks[k % 4][:, 0:128]
                if k >= 4:
                    b.w("tensor", sV, evs[k - 4])
                b.t(lambda e, dc=dc, rb=rb, ps=ps: e.transpose(
                    ps, KT[:, dc * RS + rb * 128:dc * RS + (rb + 1) * 128], S["cIDENT"][:, :]))
                tmm = sT.n
                b.w("vector", sT, tmm)
                b.v(lambda e, dc=dc, rb=rb, ps=ps: e.tensor_add(
                    FIN[:, dc * 128:(dc + 1) * 128],
                    XLN[:, rb * D + dc * 128:rb * D + (dc + 1) * 128], ps))
                b.v(lambda e, dc=dc, rb=rb: e.tensor_scalar(
                    FIN[:, dc * 128:(dc + 1) * 128], FIN[:, dc * 128:(dc + 1) * 128],
                    selcols[:, 5 * rb + 1:5 * rb + 2], None, ALU.mult))
                evs[k] = b.v(lambda e, dc=dc, rb=rb: e.tensor_add(
                    FIN[:, dc * 128:(dc + 1) * 128], FIN[:, dc * 128:(dc + 1) * 128],
                    XRS[:, rb * D + dc * 128:rb * D + (dc + 1) * 128]))
            tvf = sV.n
            if bi == 0:
                tap("fin%d" % rb, FIN[:, :])
            b.w("gpsimd", sV, tvf)
            out_flat = out_t.rearrange("b t d -> (b t) d")
            t_sc = b.emit("gpsimd", lambda e, rb=rb, out_flat=out_flat: e.indirect_dma_start(
                out=out_flat, out_offset=IndirectOffsetOnAxis(ap=selidx[:, 2 + rb:3 + rb], axis=0),
                in_=FIN[:, :], in_offset=None,
                bounds_check=BPC * T - 1, oob_is_err=False,
            ), qG, 16)
            if rb == 0:
                t_sc0 = t_sc
                b.w("vector", qG, t_sc0)
        b.fence()
        return t_xn

    stages = ["pre", "A0", "F0", "M0", "A1", "F1", "M1"]
    upto = stages.index(stop_after) if stop_after in stages else len(stages)
    if upto >= 1:
        phaseA(0, t_x0)
    if upto >= 2:
        phaseFpre(0)
    if upto >= 3:
        t_x1 = phaseFmain(0, 1)
    if upto >= 4:
        phaseA(1, t_x1)
    if upto >= 5:
        phaseFpre(1)
    if upto >= 6:
        phaseFmain(1, None)
    DUMP.update(dict(x_tok=x_tok, selcols=selcols, selm=selm, stmu=stmu, stro=stro, wbias_rep=wbias_rep, negs=negs, orvd=orv, invc=invc, wbias_row=wbias_row, wrcd=wrc, srepd=srepc, abic=abic, atric=atric,
                     wraw=wraw, ccum=ccum, acat=acat, outb=outb, xrs=XRS, csg=CSG))
    emit_dumps()
    b.w("sync", qG, qG.n)
    b.w("sync", qA, qA.n)
    b.w("sync", qS, qS.n)

    b.finalize()
    return nc, dram


_NC_CACHE = {}


def get_nc(dbg_names=(), stop_after="all", end_dumps=()):
    key = (tuple(dbg_names), stop_after, tuple(end_dumps))
    if key not in _NC_CACHE:
        _NC_CACHE[key] = build_nc(dbg_names, stop_after, end_dumps)
    return _NC_CACHE[key]


def make_in_map(inputs, core):
    x = np.ascontiguousarray(np.asarray(inputs["L0_attn_x"], dtype=np.float32))
    base = {
        "WQa": inputs["L0_attn_WQ"], "WKa": inputs["L0_attn_WK"],
        "WOa": inputs["L0_attn_WO"], "WINV": inputs["L0_attn_Winv"],
        "UB": inputs["L0_attn_U_b"], "VB": inputs["L0_attn_V_b"], "WB": inputs["L0_attn_W_b"],
        "UT": inputs["L0_attn_U_t"], "VT": inputs["L0_attn_V_t"], "WT": inputs["L0_attn_W_t"],
        "XTm": inputs["L0_attn_X_t"],
        "BIASB": np.asarray(inputs["L0_attn_bias_b"]).reshape(D, 1),
        "BIAST": np.asarray(inputs["L0_attn_bias_t"]).reshape(D, 1),
        "ABI": np.asarray(inputs["L0_attn_alpha_bi"]).reshape(1, 1),
        "ATRI": np.asarray(inputs["L0_attn_alpha_tri"]).reshape(1, 1),
        "RW": np.asarray(inputs["L0_ffn_router_w"]).reshape(D, 1),
        "RB": np.asarray(inputs["L0_ffn_router_b"]).reshape(1, 1),
        "WQ": inputs["L0_ffn_wq"], "WK": inputs["L0_ffn_wk"], "WV": inputs["L0_ffn_wv"],
        "WOf": inputs["L0_ffn_wo"], "W1": inputs["L0_ffn_w1"], "W2": inputs["L0_ffn_w2"],
        "LN1": np.asarray(inputs["L0_ffn_ln1"]).reshape(D, 1),
        "LN2": np.asarray(inputs["L0_ffn_ln2"]).reshape(D, 1),
    }
    base = {k: np.ascontiguousarray(np.asarray(v, dtype=np.float32)) for k, v in base.items()}
    for k, v in CONSTS.items():
        base[k] = v
    base["x"] = np.ascontiguousarray(x[2 * core:2 * core + 2])
    return base


def kernel(**inputs):
    nc, _ = get_nc()
    in_maps = [make_in_map(inputs, c) for c in range(8)]
    res = run_bass_kernel_spmd(nc, in_maps, core_ids=list(range(8)))
    return np.concatenate([r["out"] for r in res.results], axis=0)


# version 07 (final)
# speedup vs baseline: 1.2222x; 1.0132x over previous
import numpy as np
import concourse.bass as bass
import concourse.mybir as mybir
from concourse.bass_utils import run_bass_kernel_spmd
from concourse.bass_core_rust import IndirectOffsetOnAxis

F32 = mybir.dt.float32
I32 = mybir.dt.int32
AF = mybir.ActivationFunctionType
ALU = mybir.AluOpType
AXX = mybir.AxisListType.X

T, D, R, H, HD = 2048, 1024, 64, 16, 64
NT, ND = T // 128, D // 128
KC, RS, FF, BPC = 255, 256, 4096, 2
FC = 1024
NFC = FF // FC

ENGS = ("tensor", "vector", "scalar", "gpsimd", "sync")


def host_consts():
    f32 = np.float32
    c = {}
    c["IDENT"] = np.eye(128, dtype=f32)
    c["TRIT"] = np.triu(np.ones((128, 128), dtype=f32))
    c["TRIST"] = np.triu(np.ones((128, 128), dtype=f32), k=1)
    c["ONES128"] = np.ones((128, 128), dtype=f32)
    c["ONESROW"] = np.ones((1, 128), dtype=f32)
    c["ONESCOL"] = np.ones((128, 1), dtype=f32)
    c["IOTAR"] = np.broadcast_to(np.arange(RS, dtype=f32), (128, RS)).copy()
    c["ARANGE"] = np.arange(T, dtype=f32).reshape(NT, 128).T.copy()
    c["COUNTS"] = np.arange(1, T + 1, dtype=f32).reshape(NT, 128).T.copy()
    cb = np.zeros((128, 2 * RS), dtype=f32)
    for qt in range(2):
        q = 128 * qt + np.arange(128)[:, None]
        k = np.arange(RS)[None, :]
        cb[:, RS * qt:RS * qt + RS] = np.where(k <= q, 0.0, np.float32(-1e9))
    c["CAUSAL"] = cb
    r2 = np.zeros((128, 128), dtype=f32)
    for h0 in (0, 64):
        for i in range(32):
            r2[h0 + i, h0 + 32 + i] = -1.0
            r2[h0 + 32 + i, h0 + i] = 1.0
    c["ROTT"] = r2.T.copy()
    import jax
    import jax.numpy as jnp
    cpu = jax.devices("cpu")[0]
    with jax.default_device(cpu):
        invj = 1.0 / 10000.0 ** (jnp.arange(0, HD, 2, dtype=jnp.float32) / HD)
        posj = jnp.arange(T, dtype=jnp.int32).astype(jnp.float32)
        frj = posj[:, None] * invj[None, :]
        embj = jnp.concatenate([frj, frj], axis=-1)
        cosj, sinj = jnp.cos(embj), jnp.sin(embj)
    cosd = np.concatenate([np.asarray(cosj), np.asarray(cosj)], axis=1)
    sind = np.concatenate([np.asarray(sinj), np.asarray(sinj)], axis=1)
    tab = np.concatenate([cosd, sind], axis=1).astype(f32)
    c["CSTAB"] = np.concatenate([tab, tab], axis=0)
    return c


CONSTS = host_consts()
SB_CONSTS = [k for k in CONSTS if k != "CSTAB"]


class Sem:
    def __init__(self, h):
        self.h, self.n = h, 0

    def bump(self, k=1):
        self.n += k
        return self.n


class B:
    def __init__(self, nc):
        self.nc = nc
        self.plan = {e: [] for e in ENGS}
        self.sems = {e: Sem(nc.ctx.enter_context(nc.semaphore("s_" + e))) for e in ENGS}
        self.qsems = {e: Sem(nc.ctx.enter_context(nc.semaphore("q_" + e)))
                      for e in ("sync", "scalar", "gpsimd")}
        self._fw = {e: {} for e in ENGS}

    def emit(self, eng, fn, inc=None, inck=1):
        if eng in ("vector", "scalar") and inc is self.sems[eng] and inc.n > 0:
            self.plan[eng].append(lambda e, s=inc.h, t=inc.n: e.wait_ge(s, t))
        tgt = inc.bump(inck) if inc is not None else None
        if inc is not None:
            self.plan[eng].append(lambda e, fn=fn, s=inc.h, k=inck: fn(e).then_inc(s, k))
        else:
            self.plan[eng].append(lambda e, fn=fn: fn(e))
        return tgt

    def w(self, eng, sem, tgt):
        if tgt:
            self.plan[eng].append(lambda e, s=sem.h, t=tgt: e.wait_ge(s, t))

    def t(self, fn):
        return self.emit("tensor", fn, self.sems["tensor"])

    def v(self, fn):
        return self.emit("vector", fn, self.sems["vector"])

    def s(self, fn):
        return self.emit("scalar", fn, self.sems["scalar"])

    def dma(self, eng, fn):
        return self.emit(eng, fn, self.qsems[eng], 16)

    def fence(self):
        snap = list(self.sems.values()) + list(self.qsems.values())
        for e in ENGS:
            for s in snap:
                if s is self.sems[e]:
                    continue
                if s.n > self._fw[e].get(id(s), 0):
                    self.w(e, s, s.n)
                    self._fw[e][id(s)] = s.n

    def finalize(self):
        with self.nc.Block() as blk:
            for name in ENGS:
                getattr(blk, name)((lambda lst: (lambda e: [f(e) for f in lst]))(self.plan[name]))


def build_nc(dbg_names=(), stop_after="all", end_dumps=()):
    nc = bass.Bass("TRN2", target_bir_lowering=False, detect_race_conditions=False)
    dram = {}

    def din(name, shape):
        dram[name] = nc.dram_tensor(name, list(shape), F32, kind="ExternalInput").ap()
        return dram[name]

    x_in = din("x", (BPC, T, D))
    out_t = nc.dram_tensor("out", [BPC, T, D], F32, kind="ExternalOutput").ap()
    for nm, sh in (("WQa", (D, D)), ("WKa", (D, D)), ("WOa", (D, D)), ("WINV", (D, D)),
                   ("UB", (D, R)), ("VB", (D, R)), ("WB", (D, R)),
                   ("UT", (D, R)), ("VT", (D, R)), ("WT", (D, R)), ("XTm", (D, R)),
                   ("BIASB", (D, 1)), ("BIAST", (D, 1)), ("ABI", (1, 1)), ("ATRI", (1, 1)),
                   ("RW", (D, 1)), ("RB", (1, 1)),
                   ("WQ", (D, D)), ("WK", (D, D)), ("WV", (D, D)), ("WOf", (D, D)),
                   ("W1", (D, FF)), ("W2", (FF, D)), ("LN1", (D, 1)), ("LN2", (D, 1))):
        din(nm, sh)
    for k in CONSTS:
        din(k, CONSTS[k].shape)
    DUMP = {}
    dbg = {}
    for name, width in dbg_names:
        dbg[name] = nc.dram_tensor("dbg_" + name, [128, width], F32, kind="ExternalOutput").ap()

    S = {}

    def sb(name, shape, dtype=F32):
        S[name] = nc.alloc_sbuf_tensor(name, list(shape), dtype).ap()
        return S[name]

    x_tok = sb("x_tok", (128, NT * D))
    acat = sb("acat", (128, ND * 449))
    outb = sb("outb", (128, D))
    wbias_row = sb("wbias_row", (1, D))
    wbias_rep = sb("wbias_rep", (128, D))
    for k in SB_CONSTS:
        sb("c" + k, CONSTS[k].shape)
    invc = sb("invc", (128, NT))
    invc2 = sb("invc2", (128, NT))
    abic = sb("abic", (128, 1))
    atric = sb("atric", (128, 1))
    orv = sb("orv", (128, 1))
    wrc = sb("wrc", (128, 1))
    srepc = sb("srepc", (128, 1))
    rbc = sb("rbc", (128, 1))
    rwsb = sb("rwsb", (128, ND))
    ln1sb = sb("ln1sb", (128, ND))
    ln2sb = sb("ln2sb", (128, ND))
    negs = sb("negs", (1, 448))
    scal = sb("scal", (1, 8))
    wraw = sb("wraw", (128, NT))
    stmu = sb("stmu", (128, NT))
    stro = sb("stro", (128, NT))
    b2c = sb("b2c", (128, NT))
    selm = sb("selm", (128, NT))
    ccum = sb("ccum", (128, NT))
    cols5 = sb("cols5", (128, 5 * NT))
    selcols = sb("selcols", (128, 10))
    selidx = sb("selidx", (128, 4), I32)
    carry = sb("carry", (128, 256))
    epsc = sb("epsc", (128, 4))
    prb = sb("prb", (1, 16))
    XRS = sb("xrs", (128, 2 * D))
    CSG = sb("csg", (128, 2 * RS))
    ARW = 24200
    arena = sb("arena", (128, ARW))
    PS = [nc.alloc_psum_tensor("pb%d" % i, [128, 512], F32).ap() for i in range(8)]

    b = B(nc)
    sT, sV, sS, sG = (b.sems[e] for e in ("tensor", "vector", "scalar", "gpsimd"))
    qS, qA, qG = (b.qsems[e] for e in ("sync", "scalar", "gpsimd"))

    _PQI = {"sync": 0, "scalar": 1, "gpsimd": 2}

    def qprobe(eng):
        qi = _PQI[eng]
        return b.dma(eng, lambda e, qi=qi: e.dma_start(
            out=prb[0:1, qi:qi + 1], in_=dram["IDENT"][0:1, 0:1]))

    class Scope:
        def __init__(self):
            self.off, self.map = 0, {}

        def g(self, name, width):
            if name not in self.map:
                assert self.off + width <= ARW, (name, self.off, width)
                self.map[name] = (self.off, width)
                self.off += width
            o, wd = self.map[name]
            assert wd == width
            return arena[:, o:o + wd]

        def row(self, name, width):
            return self.g(name, width)[0:1, :]

    def tap(name, ap):
        if name in dbg:
            for sem in (sT, sV, sS, sG):
                b.w("sync", sem, sem.n)
            b.dma("sync", lambda e, ap=ap, name=name: e.dma_start(
                out=dbg[name][:ap.shape[0], :ap.shape[1]], in_=ap))

    for k in SB_CONSTS:
        b.dma("sync", lambda e, k=k: e.dma_start(out=S["c" + k][:], in_=dram[k][:, :]))
    for c in range(ND):
        b.dma("sync", lambda e, c=c: e.dma_start(out=rwsb[:, c:c + 1], in_=dram["RW"][c * 128:(c + 1) * 128, :]))
        b.dma("sync", lambda e, c=c: e.dma_start(out=ln1sb[:, c:c + 1], in_=dram["LN1"][c * 128:(c + 1) * 128, :]))
        b.dma("sync", lambda e, c=c: e.dma_start(out=ln2sb[:, c:c + 1], in_=dram["LN2"][c * 128:(c + 1) * 128, :]))
    b.dma("sync", lambda e: e.dma_start(out=scal[:, 0:1], in_=dram["ABI"][:, :]))
    b.dma("sync", lambda e: e.dma_start(out=scal[:, 1:2], in_=dram["ATRI"][:, :]))
    t_sc3 = b.dma("sync", lambda e: e.dma_start(out=scal[:, 2:3], in_=dram["RB"][:, :]))
    t_x0 = b.dma("sync", lambda e: e.dma_start(
        out=x_tok.rearrange("q (n d) -> q n d", d=D),
        in_=x_in[0].rearrange("(n p) d -> p n d", p=128)))
    t_outinit = b.dma("gpsimd", lambda e: e.dma_start(out=out_t[1], in_=x_in[1]))
    t_outinit = qprobe("gpsimd")
    b.v(lambda e: e.memset(epsc[:, 0:1], 1e-5))
    b.v(lambda e: e.memset(epsc[:, 1:2], 1e-6))
    b.v(lambda e: e.memset(epsc[:, 2:3], 1.0 / D))

    pre = Scope()
    SW = {}
    for nm in ("UB", "UT", "XTm"):
        SW[nm] = pre.g("sw_" + nm, ND * R)
        for c in range(ND):
            b.dma("sync", lambda e, nm=nm, c=c: e.dma_start(
                out=SW[nm][:, c * R:(c + 1) * R], in_=dram[nm][c * 128:(c + 1) * 128, :]))
    QCAT = pre.g("qcat", ND * 192)
    KCAT = pre.g("kcat", ND * 192)
    VW = pre.g("vw", ND * 128)
    for c in range(ND):
        b.dma("sync", lambda e, c=c: e.dma_start(
            out=QCAT[:, c * 192:c * 192 + R], in_=dram["VB"][c * 128:(c + 1) * 128, :]))
        b.dma("sync", lambda e, c=c: e.dma_start(
            out=QCAT[:, c * 192 + 128:c * 192 + 192], in_=dram["VT"][c * 128:(c + 1) * 128, :]))
        b.dma("sync", lambda e, c=c: e.dma_start(
            out=KCAT[:, c * 192:c * 192 + R], in_=dram["WB"][c * 128:(c + 1) * 128, :]))
        b.dma("sync", lambda e, c=c: e.dma_start(
            out=KCAT[:, c * 192 + 128:c * 192 + 192], in_=dram["WT"][c * 128:(c + 1) * 128, :]))
        b.dma("sync", lambda e, c=c: e.dma_start(
            out=VW[:, c * 128:c * 128 + R], in_=dram["VB"][c * 128:(c + 1) * 128, :]))
        b.dma("sync", lambda e, c=c: e.dma_start(
            out=VW[:, c * 128 + R:(c + 1) * 128], in_=dram["WB"][c * 128:(c + 1) * 128, :]))
    bbcol = pre.g("biasb", ND)
    btcol = pre.g("biast", ND)
    for c in range(ND):
        b.dma("sync", lambda e, c=c: e.dma_start(out=bbcol[:, c:c + 1], in_=dram["BIASB"][c * 128:(c + 1) * 128, :]))
        b.dma("sync", lambda e, c=c: e.dma_start(out=btcol[:, c:c + 1], in_=dram["BIAST"][c * 128:(c + 1) * 128, :]))
    BW = pre.g("bigw", ND * D)
    WOT = pre.g("wot", ND * D)
    BCOMB = pre.g("bcomb", ND)
    WORC = pre.g("worc", ND)
    WBIC = pre.g("wbic", ND)
    DUMP["acat"] = acat
    DUMP["qcat"] = QCAT
    DUMP["kcat"] = KCAT
    DUMP["wbic"] = WBIC
    DUMP["bcomb"] = BCOMB
    DUMP["bbcol"] = bbcol
    DUMP["btcol"] = btcol
    DUMP["rwsb"] = rwsb
    DUMP["ln1sb"] = ln1sb
    t_pre_loads = qprobe("sync")

    _stopped = []

    def emit_dumps():
        for sem in list(b.sems.values()) + [qA, qG]:
            b.w("sync", sem, sem.n)
        for nm in end_dumps:
            ap = DUMP[nm]
            dd = nc.dram_tensor("dump_" + nm, [ap.shape[0], ap.shape[1]], F32,
                                kind="ExternalOutput").ap()
            b.dma("sync", lambda e, ap=ap, dd=dd: e.dma_start(out=dd[:, :], in_=ap))
        b.w("sync", qS, qS.n)

    def maybe_stop(tag):
        if stop_after == tag:
            emit_dumps()
            b.w("sync", qG, qG.n)
            b.finalize()
            _stopped.append(True)
        return bool(_stopped)

    if maybe_stop("load"):
        return nc, dram
    b.w("tensor", qS, t_pre_loads)
    b.t(lambda e: e.matmul(PS[0][:, 0:1], S["cONESROW"][:, :], scal[0:1, 0:1], start=True, stop=True))
    b.t(lambda e: e.matmul(PS[0][:, 1:2], S["cONESROW"][:, :], scal[0:1, 1:2], start=True, stop=True))
    tm = b.t(lambda e: e.matmul(PS[0][:, 2:3], S["cONESROW"][:, :], scal[0:1, 2:3], start=True, stop=True))
    b.w("vector", sT, tm)
    b.v(lambda e: e.tensor_copy(abic[:], PS[0][:, 0:1]))
    b.v(lambda e: e.tensor_copy(atric[:], PS[0][:, 1:2]))
    b.v(lambda e: e.tensor_copy(rbc[:], PS[0][:, 2:3]))
    b.w("vector", qS, t_pre_loads)
    b.v(lambda e: e.tensor_scalar(BCOMB[:, :], bbcol[:, :], abic[:, 0:1], None, ALU.mult))
    b.v(lambda e: e.tensor_add(BCOMB[:, :], BCOMB[:, :], bbcol[:, :]))
    b.v(lambda e: e.scalar_tensor_tensor(BCOMB[:, :], btcol[:, :], atric[:, 0:1], BCOMB[:, :], ALU.mult, ALU.add))
    b.v(lambda e: e.reciprocal(invc[:], S["cCOUNTS"][:, :]))
    tv = b.v(lambda e: e.tensor_mul(invc2[:], invc[:], invc[:]))
    b.w("tensor", sV, tv)

    if maybe_stop("p1"):
        return nc, dram

    def ld_bigw(nm):
        b.w("sync", sT, sT.n)
        for c in range(ND):
            b.dma("sync", lambda e, nm=nm, c=c: e.dma_start(
                out=BW[:, c * D:(c + 1) * D], in_=dram[nm][c * 128:(c + 1) * 128, :]))
        return qprobe("sync")

    def wsmall(dst_fn, vsm, scale_col=None, tilepos=None):
        ev = {}
        for m in range(ND):
            slot = (PS[1] if m % 2 == 0 else PS[6])[:, 0:R]
            oslot = slot if tilepos is None else slot[64:128, :]
            if m >= 2:
                b.w("tensor", sV, ev[m - 2])
            for j in range(ND):
                b.t(lambda e, m=m, j=j, oslot=oslot: e.matmul(
                    oslot, BW[:, j * D + m * 128:j * D + (m + 1) * 128],
                    vsm[:, j * R:(j + 1) * R], start=(j == 0), stop=(j == ND - 1),
                    tile_position=tilepos))
            tmm = sT.n
            b.w("vector", sT, tmm)
            dst = dst_fn(m)
            if scale_col is None:
                ev[m] = b.v(lambda e, dst=dst, oslot=oslot: e.tensor_copy(dst, oslot))
            else:
                ev[m] = b.v(lambda e, dst=dst, oslot=oslot, sc=scale_col: e.tensor_scalar(
                    dst, oslot, sc, None, ALU.mult))
        b.w("tensor", sV, sV.n)

    tw = ld_bigw("WINV")
    b.w("tensor", qS, tw)
    ev = {}
    for m in range(ND):
        slot = (PS[1] if m % 2 == 0 else PS[6])[:, 0:128]
        if m >= 2:
            b.w("tensor", sV, ev[m - 2])
        for j in range(ND):
            b.t(lambda e, m=m, j=j, slot=slot: e.matmul(
                slot, BW[:, j * D + m * 128:j * D + (m + 1) * 128],
                VW[:, j * 128:(j + 1) * 128], start=(j == 0), stop=(j == ND - 1)))
        tmm = sT.n
        b.w("vector", sT, tmm)
        b.v(lambda e, m=m, slot=slot: e.tensor_copy(
            KCAT[:, m * 192 + R:m * 192 + 128], slot[:, 0:64]))
        ev[m] = b.v(lambda e, m=m, slot=slot: e.tensor_scalar(
            QCAT[:, m * 192 + R:m * 192 + 128], slot[:, 64:128], abic[:, 0:1], None, ALU.mult))
    b.w("tensor", sV, sV.n)
    if maybe_stop("p2"):
        return nc, dram
    for wname, coff in (("WQa", 0), ("WKa", 192)):
        cat = QCAT if wname == "WQa" else KCAT
        tw = ld_bigw(wname)
        b.w("tensor", qS, tw)
        ev = {}
        for m in range(ND):
            slot = (PS[1] if m % 2 == 0 else PS[6])[:, 0:192]
            if m >= 2:
                b.w("tensor", sV, ev[m - 2])
            for j in range(ND):
                b.t(lambda e, m=m, j=j, slot=slot, cat=cat: e.matmul(
                    slot, BW[:, j * D + m * 128:j * D + (m + 1) * 128],
                    cat[:, j * 192:(j + 1) * 192], start=(j == 0), stop=(j == ND - 1)))
            tmm = sT.n
            b.w("vector", sT, tmm)
            ev[m] = b.v(lambda e, m=m, slot=slot, coff=coff: e.tensor_copy(
                acat[:, 449 * m + coff:449 * m + coff + 192], slot))
        b.w("tensor", sV, sV.n)
    for m in range(ND):
        b.v(lambda e, m=m: e.tensor_copy(acat[:, 449 * m + 384:449 * m + 448],
                                         SW["XTm"][:, m * R:(m + 1) * R]))
        b.v(lambda e, m=m: e.tensor_copy(acat[:, 449 * m + 448:449 * m + 449],
                                         rwsb[:, m:m + 1]))
    if maybe_stop("p3"):
        return nc, dram
    tw = ld_bigw("WOa")
    b.w("tensor", qS, tw)
    ev = {}
    for m in range(ND):
        slot = (PS[1] if m % 2 == 0 else PS[6])[:, 128:129]
        if m >= 2:
            b.w("tensor", sV, ev[m - 2])
        for k in range(ND):
            b.t(lambda e, m=m, k=k, slot=slot: e.matmul(
                slot, BW[:, k * D + m * 128:k * D + (m + 1) * 128],
                rwsb[:, k:k + 1], start=(k == 0), stop=(k == ND - 1)))
        tmm = sT.n
        b.w("vector", sT, tmm)
        ev[m] = b.v(lambda e, m=m, slot=slot: e.tensor_copy(WORC[:, m:m + 1], slot))
    b.w("tensor", sV, sV.n)
    if maybe_stop("p3a"):
        return nc, dram
    WOTV = WOT.rearrange("p (j x) -> p j x", x=D)
    ev = {}
    for c in range(ND):
        bankA = PS[2] if c % 2 == 0 else PS[4]
        bankB = PS[3] if c % 2 == 0 else PS[5]
        if c >= 2:
            b.w("tensor", sV, ev[c - 2])
        for j in range(ND):
            ps = (bankA if j < 4 else bankB)[:, (j % 4) * 128:(j % 4) * 128 + 128]
            b.t(lambda e, c=c, j=j, ps=ps: e.transpose(
                ps, BW[:, c * D + j * 128:c * D + (j + 1) * 128], S["cIDENT"][:, :]))
        tmm = sT.n
        b.w("vector", sT, tmm)
        b.v(lambda e, c=c, bankA=bankA: e.tensor_copy(
            WOTV[:, 0:4, c * 128:(c + 1) * 128], bankA[:, 0:512]))
        ev[c] = b.v(lambda e, c=c, bankB=bankB: e.tensor_copy(
            WOTV[:, 4:8, c * 128:(c + 1) * 128], bankB[:, 0:512]))
    b.w("tensor", sV, sV.n)
    if isinstance(stop_after, str) and stop_after.startswith("wot"):
        emit_dumps()
        b.w("sync", qG, qG.n)
        b.finalize()
        return nc, dram
    if maybe_stop("p4"):
        return nc, dram
    for n2 in range(2):
        slot = (PS[3] if n2 == 0 else PS[4])[0:64, 0:512]
        for j in range(ND):
            b.t(lambda e, n2=n2, j=j, slot=slot: e.matmul(
                slot, SW["UB"][:, j * R:(j + 1) * R],
                WOT[:, j * D + n2 * 512:j * D + (n2 + 1) * 512],
                start=(j == 0), stop=(j == ND - 1)))
        tmm = sT.n
        b.w("vector", sT, tmm)
        b.v(lambda e, n2=n2, slot=slot: e.tensor_copy(outb[0:64, n2 * 512:(n2 + 1) * 512], slot))
    for n2 in range(2):
        slot = (PS[5] if n2 == 0 else PS[6])[64:128, 0:512]
        for j in range(ND):
            b.t(lambda e, n2=n2, j=j, slot=slot: e.matmul(
                slot, SW["UT"][:, j * R:(j + 1) * R],
                WOT[:, j * D + n2 * 512:j * D + (n2 + 1) * 512],
                start=(j == 0), stop=(j == ND - 1), tile_position=(0, 64)))
        tmm = sT.n
        b.w("vector", sT, tmm)
        b.v(lambda e, n2=n2, slot=slot: e.tensor_scalar(
            outb[64:128, n2 * 512:(n2 + 1) * 512], slot, atric[64:128, 0:1], None, ALU.mult))
    b.w("tensor", sV, sV.n)
    slot = PS[1][0:64, 130:131]
    for j in range(ND):
        b.t(lambda e, j=j, slot=slot: e.matmul(
            slot, SW["UB"][:, j * R:(j + 1) * R], WORC[:, j:j + 1],
            start=(j == 0), stop=(j == ND - 1)))
    tmm = sT.n
    b.w("vector", sT, tmm)
    b.v(lambda e, slot=slot: e.tensor_copy(orv[0:64, 0:1], slot))
    slot = PS[6][64:128, 132:133]
    for j in range(ND):
        b.t(lambda e, j=j, slot=slot: e.matmul(
            slot, SW["UT"][:, j * R:(j + 1) * R], WORC[:, j:j + 1],
            start=(j == 0), stop=(j == ND - 1), tile_position=(0, 64)))
    tmm = sT.n
    b.w("vector", sT, tmm)
    b.v(lambda e, slot=slot: e.tensor_scalar(orv[64:128, 0:1], slot, atric[64:128, 0:1], None, ALU.mult))
    b.w("tensor", sV, sV.n)
    if maybe_stop("p5"):
        return nc, dram
    ev = {}
    for m in range(ND):
        slot = (PS[1] if m % 2 == 0 else PS[6])[:, 134:135]
        if m >= 2:
            b.w("tensor", sV, ev[m - 2])
        for j in range(ND):
            b.t(lambda e, m=m, j=j, slot=slot: e.matmul(
                slot, WOT[:, j * D + m * 128:j * D + (m + 1) * 128],
                BCOMB[:, j:j + 1], start=(j == 0), stop=(j == ND - 1)))
        tmm = sT.n
        b.w("vector", sT, tmm)
        ev[m] = b.v(lambda e, m=m, slot=slot: e.tensor_copy(WBIC[:, m:m + 1], slot))
    b.w("tensor", sV, sV.n)
    for m in range(ND):
        slot = PS[2][0:1, 0:128]
        b.t(lambda e, m=m, slot=slot: e.transpose(slot, WBIC[:, m:m + 1], S["cIDENT"][:, :]))
        tmm = sT.n
        b.w("vector", sT, tmm)
        tv = b.v(lambda e, m=m, slot=slot: e.tensor_copy(wbias_row[0:1, m * 128:(m + 1) * 128], slot))
        b.w("tensor", sV, tv)
    for n2 in range(2):
        slot = (PS[3] if n2 == 0 else PS[4])[:, 0:512]
        b.t(lambda e, n2=n2, slot=slot: e.matmul(
            slot, S["cONESROW"][:, :], wbias_row[0:1, n2 * 512:(n2 + 1) * 512], start=True, stop=True))
        tmm = sT.n
        b.w("vector", sT, tmm)
        b.v(lambda e, n2=n2, slot=slot: e.tensor_copy(wbias_rep[:, n2 * 512:(n2 + 1) * 512], slot))
    slot = PS[1][0:1, 136:137]
    for m in range(ND):
        b.t(lambda e, m=m, slot=slot: e.matmul(
            slot, WBIC[:, m:m + 1], rwsb[:, m:m + 1], start=(m == 0), stop=(m == ND - 1)))
    slot2 = PS[1][0:1, 138:139]
    for m in range(ND):
        b.t(lambda e, m=m, slot2=slot2: e.matmul(
            slot2, rwsb[:, m:m + 1], S["cONESCOL"][:, 0:1], start=(m == 0), stop=(m == ND - 1)))
    tmm = sT.n
    b.w("vector", sT, tmm)
    b.v(lambda e, slot=slot: e.tensor_copy(scal[0:1, 3:4], slot))
    tv = b.v(lambda e, slot2=slot2: e.tensor_copy(scal[0:1, 4:5], slot2))
    b.w("tensor", sV, tv)
    b.t(lambda e: e.matmul(PS[1][:, 140:141], S["cONESROW"][:, :], scal[0:1, 3:4], start=True, stop=True))
    tmm = b.t(lambda e: e.matmul(PS[1][:, 141:142], S["cONESROW"][:, :], scal[0:1, 4:5], start=True, stop=True))
    b.w("vector", sT, tmm)
    b.v(lambda e: e.tensor_copy(wrc[:], PS[1][:, 140:141]))
    b.v(lambda e: e.tensor_copy(srepc[:], PS[1][:, 141:142]))
    if maybe_stop("p6"):
        return nc, dram
    slot = PS[7][0:1, 0:448]
    for c in range(ND):
        b.t(lambda e, c=c, slot=slot: e.matmul(
            slot, S["cONESCOL"][:, :], acat[:, c * 449:c * 449 + 448],
            start=(c == 0), stop=(c == ND - 1)))
    tmm = sT.n
    b.w("vector", sT, tmm)
    b.v(lambda e, slot=slot: e.tensor_scalar(negs[:, :], slot, -1.0, None, ALU.mult))
    b.fence()
    tap("acat0", acat[:, 0:448])
    tap("acat7", acat[:, 7 * 448:8 * 448])
    tap("outb", outb[:, :])
    tap("wbias_rep", wbias_rep[:, 0:D])
    tap("orv", orv[:, 0:1])
    tap("negs", negs[0:1, :])

    aS = Scope()

    def phaseA(bi, t_xload):
        XT2 = [aS.g("xT0", D), aS.g("xT1", D)]
        XSQ = aS.g("xsq", D)
        PP = [aS.g("pp0", 449), aS.g("pp1", 449)]
        CUM = aS.g("cum", 256)
        RT2 = [aS.g("rt0", 128), aS.g("rt1", 128)]
        MUR = aS.row("murow", 128)
        STC = aS.g("stc", 8)
        BNS = aS.g("bns", 12)
        b.w("tensor", qS, t_xload)
        tv = b.v(lambda e: e.memset(carry[:, :], 0.0))
        b.w("tensor", sV, tv)
        ev_prev = {}
        for t in range(NT):
            xT = XT2[t % 2]
            pp = PP[t % 2]
            rt = RT2[t % 2]
            if t >= 1:
                b.w("tensor", sV, ev_prev["xT"])
            for c in range(ND):
                ps = (PS[2] if c < 4 else PS[3])[:, (c % 4) * 128:(c % 4) * 128 + 128]
                b.t(lambda e, t=t, c=c, ps=ps: e.transpose(
                    ps, x_tok[:, t * D + c * 128:t * D + (c + 1) * 128], S["cIDENT"][:, :]))
            tm_tr = sT.n
            b.w("vector", sT, tm_tr)
            b.v(lambda e, xT=xT: e.tensor_copy(xT[:, 0:512], PS[2][:, 0:512]))
            ev_xT = b.v(lambda e, xT=xT: e.tensor_copy(xT[:, 512:1024], PS[3][:, 0:512]))
            ev_prev["xT"] = ev_xT
            b.v(lambda e, t=t: e.bn_stats(BNS[:, 0:6], x_tok[:, t * D:t * D + 512]))
            b.v(lambda e, t=t: e.bn_stats(BNS[:, 6:12], x_tok[:, t * D + 512:(t + 1) * D]))
            ev_bn = b.v(lambda e: e.bn_aggr(STC[:, 0:2], BNS[:, 0:12]))
            b.w("gpsimd", sT, tm_tr)
            b.w("gpsimd", sV, ev_bn)
            b.dma("gpsimd", lambda e, t=t: e.dma_start(
                out=x_tok[:, t * D:(t + 1) * D], in_=wbias_rep[:, :], accum_op=ALU.add))
            t_wb = qprobe("gpsimd")
            b.w("scalar", sV, ev_bn)
            sq2 = b.s(lambda e: e.activation(STC[:, 2:3], STC[:, 1:2], AF.Sqrt, bias=epsc[:, 0:1]))
            b.w("vector", sS, sq2)
            ev_rstd = b.v(lambda e: e.reciprocal(STC[:, 3:4], STC[:, 2:3]))
            if t >= 1:
                b.w("tensor", sV, ev_prev["pp"])
            b.w("tensor", sV, ev_xT)
            for c in range(ND):
                b.t(lambda e, c=c, xT=xT: e.matmul(
                    PS[0][:, 0:449], xT[:, c * 128:(c + 1) * 128],
                    acat[:, c * 449:(c + 1) * 449], start=(c == 0), stop=False))
            b.w("tensor", sV, ev_bn)
            tmu = b.t(lambda e: e.transpose(PS[5][0:1, 0:128], STC[:, 0:1], S["cIDENT"][:, :]))
            b.w("vector", sT, tmu)
            ev_mur = b.v(lambda e: e.tensor_copy(MUR[:, :], PS[5][0:1, 0:128]))
            b.w("tensor", sV, ev_mur)
            b.t(lambda e: e.matmul(PS[0][:, 0:448], MUR[:, :], negs[0:1, :], start=False, stop=True))
            tm_corr = sT.n
            b.w("vector", sT, tm_corr)
            b.w("vector", sV, ev_rstd)
            ev_pp = b.v(lambda e, pp=pp: e.tensor_scalar(
                pp[:, 0:448], PS[0][:, 0:448], STC[:, 3:4], None, ALU.mult))
            ev_prev["pp"] = ev_pp
            if t >= 1:
                b.w("tensor", sV, ev_prev["cum"])
            b.w("tensor", sV, ev_pp)
            b.t(lambda e, pp=pp: e.matmul(PS[4][:, 0:256], S["cTRIT"][:, :], pp[:, 192:448], start=True, stop=True))
            tm_cum = b.t(lambda e, pp=pp: e.matmul(PS[4][:, 256:512], S["cONES128"][:, :], pp[:, 192:448], start=True, stop=True))
            b.w("vector", sT, tm_cum)
            b.v(lambda e: e.tensor_add(CUM[:, :], PS[4][:, 0:256], carry[:, :]))
            ev_cum = b.v(lambda e: e.tensor_add(carry[:, :], carry[:, :], PS[4][:, 256:512]))
            ev_prev["cum"] = ev_cum
            b.v(lambda e, pp=pp: e.tensor_mul(XSQ[:, 0:64], pp[:, 0:64], CUM[:, 0:64]))
            b.v(lambda e, pp=pp: e.tensor_mul(XSQ[:, 64:128], pp[:, 64:128], CUM[:, 64:128]))
            b.v(lambda e: e.tensor_add(XSQ[:, 0:64], XSQ[:, 0:64], XSQ[:, 64:128]))
            b.v(lambda e, rt=rt, t=t: e.tensor_scalar(
                rt[:, 0:64], XSQ[:, 0:64], invc[:, t:t + 1], None, ALU.mult))
            b.v(lambda e, pp=pp: e.tensor_mul(XSQ[:, 64:128], pp[:, 128:192], CUM[:, 128:192]))
            ev_R = b.v(lambda e, rt=rt, t=t: e.scalar_tensor_tensor(
                rt[:, 64:128], XSQ[:, 64:128], invc2[:, t:t + 1], CUM[:, 192:256], ALU.mult, ALU.mult))
            b.w("tensor", sV, ev_R)
            trt = b.t(lambda e, rt=rt: e.transpose(PS[5][:, 128:256], rt[:, :], S["cIDENT"][:, :]))
            b.w("vector", sT, trt)
            ev_rt = b.v(lambda e, rt=rt: e.tensor_copy(rt[:, :], PS[5][:, 128:256]))
            b.w("tensor", sV, ev_rt)
            b.t(lambda e, rt=rt: e.matmul(PS[5][:, 497:498], rt[:, :], orv[:, 0:1], start=True, stop=True))
            tm_w = sT.n
            b.w("vector", sT, tm_w)
            b.v(lambda e, t=t, pp=pp: e.tensor_scalar(
                wraw[:, t:t + 1], PS[0][:, 448:449], wrc[:, 0:1], None, ALU.add))
            ev_wraw = b.v(lambda e, t=t: e.tensor_add(
                wraw[:, t:t + 1], wraw[:, t:t + 1], PS[5][:, 497:498]))
            ev_prev["cum"] = ev_wraw
            ev_prev["pp"] = ev_wraw
            if t >= 1:
                b.w("tensor", sV, ev_prev["out"])
            for n2 in range(2):
                b.t(lambda e, rt=rt, n2=n2: e.matmul(
                    (PS[6] if n2 == 0 else PS[7])[:, 0:512], rt[:, :],
                    outb[:, n2 * 512:(n2 + 1) * 512], start=True, stop=True))
            tm_out = sT.n
            b.w("vector", sT, tm_out)
            b.w("vector", qG, t_wb)
            b.v(lambda e, t=t: e.tensor_add(
                x_tok[:, t * D:t * D + 512], x_tok[:, t * D:t * D + 512], PS[6][:, 0:512]))
            ev_out = b.v(lambda e, t=t: e.tensor_add(
                x_tok[:, t * D + 512:(t + 1) * D], x_tok[:, t * D + 512:(t + 1) * D], PS[7][:, 0:512]))
            ev_prev["out"] = ev_out
        b.fence()
        tap("xA%d" % bi, x_tok[:, 0:4096])
        tap("wrawA%d" % bi, wraw[:, :])
        tap("xrA%d" % bi, b2c[:, :])
        tap("rorvA%d" % bi, ccum[:, :])
        tap("wrcA%d" % bi, wrc[:, :])
        tap("srepA%d" % bi, srepc[:, :])

    fS = Scope()

    def phaseFpre(bi):
        DUMP = fS.g("dump", D)
        WREP = fS.g("wrep", T)
        ST2 = [fS.g("st0", RS), fS.g("st1", RS)]
        WTR = fS.g("wtr", 128)
        WROW = fS.row("wrow", T)
        BASE2 = [fS.g("base0", D), fS.g("base1", D)]
        AC = fS.g("acol", NT + 8)
        BC = fS.g("bcol", NT)
        TMP = fS.g("ftmp", NT)
        BN2 = fS.g("bn2", 12)
        STAT2 = fS.g("stat2", 2)
        for t in range(NT):
            b.v(lambda e, t=t: e.bn_stats(BN2[:, 0:6], x_tok[:, t * D:t * D + 512]))
            b.v(lambda e, t=t: e.bn_stats(BN2[:, 6:12], x_tok[:, t * D + 512:(t + 1) * D]))
            b.v(lambda e: e.bn_aggr(STAT2[:, :], BN2[:, 0:12]))
            b.v(lambda e, t=t: e.tensor_copy(stmu[:, t:t + 1], STAT2[:, 0:1]))
            b.v(lambda e, t=t: e.tensor_copy(stro[:, t:t + 1], STAT2[:, 1:2]))
        b.w("scalar", sV, sV.n)
        ts_ = b.s(lambda e: e.activation(stro[:, :], stro[:, :], AF.Sqrt, bias=epsc[:, 0:1]))
        b.w("vector", sS, ts_)
        b.v(lambda e: e.reciprocal(stro[:, :], stro[:, :]))
        b.v(lambda e: e.tensor_mul(b2c[:, :], stmu[:, :], stro[:, :]))
        b.v(lambda e: e.tensor_scalar(b2c[:, :], b2c[:, :], -1.0, None, ALU.mult))
        b.v(lambda e: e.tensor_scalar(TMP[:, :], stmu[:, :], srepc[:, 0:1], None, ALU.mult))
        b.v(lambda e: e.tensor_sub(TMP[:, :], wraw[:, :], TMP[:, :]))
        b.v(lambda e: e.tensor_mul(TMP[:, :], TMP[:, :], stro[:, :]))
        tv_w = b.v(lambda e: e.tensor_scalar(TMP[:, :], TMP[:, :], rbc[:, 0:1], None, ALU.add))
        b.w("tensor", sV, tv_w)
        tmm = b.t(lambda e: e.transpose(PS[5][0:16, 0:128], TMP[:, 0:16], S["cIDENT"][:, :]))
        b.w("vector", sT, tmm)
        tv = b.v(lambda e: e.tensor_copy(WTR[0:16, :], PS[5][0:16, 0:128]))
        b.w("gpsimd", sV, tv)
        b.dma("gpsimd", lambda e: e.dma_start(out=WROW[:, :], in_=WTR[0:16, :]))
        tg = qprobe("gpsimd")
        b.w("tensor", qG, tg)
        wrep_ev = {}
        for n4 in range(4):
            ps = (PS[2] if n4 % 2 == 0 else PS[3])[:, 0:512]
            if n4 >= 2:
                b.w("tensor", sV, wrep_ev[n4 - 2])
            b.t(lambda e, n4=n4, ps=ps: e.matmul(
                ps, S["cONESROW"][:, :], WROW[0:1, n4 * 512:(n4 + 1) * 512], start=True, stop=True))
            tmm = sT.n
            b.w("vector", sT, tmm)
            wrep_ev[n4] = b.v(lambda e, n4=n4, ps=ps: e.tensor_copy(WREP[:, n4 * 512:(n4 + 1) * 512], ps))
        SMC = fS.g("smc", NT)
        RNKD = fS.g("rnkd", T)
        for j in range(NT):
            b.v(lambda e, j=j: e.tensor_scalar(
                RNKD[:, :], WREP[:, :], TMP[:, j:j + 1], 0.0, ALU.is_lt, ALU.add,
                accum_out=SMC[:, j:j + 1]))
        tv = b.v(lambda e: e.tensor_scalar(selm[:, :], SMC[:, :], float(T - KC) - 0.5, None, ALU.is_gt))
        b.w("tensor", sV, tv)
        b.t(lambda e: e.matmul(PS[4][:, 0:16], S["cTRIST"][:, :], selm[:, :], start=True, stop=False))
        tmm = b.t(lambda e: e.matmul(PS[5][0:1, 0:16], S["cONESCOL"][:, :], selm[:, :], start=True, stop=True))
        b.w("vector", sT, tmm)
        tv = b.v(lambda e: e.tensor_copy(WTR[0:1, 0:16], PS[5][0:1, 0:16]))
        b.w("tensor", sV, tv)
        tmm = b.t(lambda e: e.transpose(PS[5][0:16, 16:17], WTR[0:1, 0:16], S["cIDENT"][0:1, 0:1]))
        b.w("vector", sT, tmm)
        tv = b.v(lambda e: e.tensor_copy(WTR[0:16, 16:17], PS[5][0:16, 16:17]))
        b.w("tensor", sV, tv)
        tmm = b.t(lambda e: e.matmul(PS[5][0:16, 17:18], S["cTRIST"][0:16, 0:16], WTR[0:16, 16:17], start=True, stop=True))
        b.w("vector", sT, tmm)
        tv = b.v(lambda e: e.tensor_copy(WTR[0:16, 18:19], PS[5][0:16, 17:18]))
        b.w("tensor", sV, tv)
        tmm = b.t(lambda e: e.transpose(PS[5][0:1, 19:35], WTR[0:16, 18:19], S["cIDENT"][0:16, 0:16]))
        b.w("vector", sT, tmm)
        tv = b.v(lambda e: e.tensor_copy(WTR[0:1, 20:36], PS[5][0:1, 19:35]))
        b.w("tensor", sV, tv)
        b.t(lambda e: e.matmul(PS[4][:, 0:16], S["cONESROW"][:, :], WTR[0:1, 20:36], start=False, stop=True))
        tmm = sT.n
        b.w("vector", sT, tmm)
        tv_cc = b.v(lambda e: e.tensor_copy(ccum[:, :], PS[4][:, 0:16]))
        c4v = cols5.rearrange("p (t f) -> p t f", f=5)
        b.v(lambda e: e.tensor_copy(c4v[:, :, 0], S["cARANGE"][:, :]))
        b.v(lambda e: e.tensor_copy(c4v[:, :, 1], TMP[:, :]))
        b.v(lambda e: e.tensor_copy(c4v[:, :, 2], stro[:, :]))
        b.v(lambda e: e.tensor_copy(c4v[:, :, 3], b2c[:, :]))
        b.v(lambda e: e.tensor_copy(c4v[:, :, 4], S["cONES128"][:, 0:NT]))
        ev_st = {}
        for j in range(NT):
            st = ST2[j % 2]
            if j >= 2:
                b.w("vector", sT, ev_st[j - 2])
            b.v(lambda e, st=st, j=j: e.tensor_scalar(
                st[:, :], S["cIOTAR"][:, :], ccum[:, j:j + 1], None, ALU.is_equal))
            tv = b.v(lambda e, st=st, j=j: e.tensor_scalar(
                st[:, :], st[:, :], selm[:, j:j + 1], None, ALU.mult))
            b.w("tensor", sV, tv)
            for rb in range(2):
                b.t(lambda e, st=st, rb=rb, j=j: e.matmul(
                    PS[2][:, 5 * rb:5 * rb + 5], st[:, rb * 128:(rb + 1) * 128],
                    cols5[:, j * 5:(j + 1) * 5], start=(j == 0 and rb == 0),
                    stop=(j == NT - 1 and rb == 1)))
            ev_st[j] = sT.n
        b.w("vector", sT, sT.n)
        tv = b.v(lambda e: e.tensor_copy(selcols[:, :], PS[2][:, 0:10]))
        b.w("vector", sV, tv)
        for rb in range(2):
            b.v(lambda e, rb=rb, bi=bi: e.tensor_scalar(
                AC[:, 16 + rb:17 + rb], selcols[:, 5 * rb:5 * rb + 1], float(bi * T), None, ALU.add))
            b.v(lambda e, rb=rb: e.tensor_scalar(
                AC[:, 18 + rb:19 + rb], selcols[:, 5 * rb + 4:5 * rb + 5], 1.0, -4096.0, ALU.subtract, ALU.mult))
            b.v(lambda e, rb=rb: e.tensor_add(AC[:, 18 + rb:19 + rb], AC[:, 18 + rb:19 + rb], AC[:, 16 + rb:17 + rb]))
            b.v(lambda e, rb=rb: e.tensor_copy(selidx[:, rb:rb + 1], AC[:, 16 + rb:17 + rb]))
        tv_idx = b.v(lambda e: e.tensor_copy(selidx[:, 2:4], AC[:, 18:20]))
        b.v(lambda e: e.tensor_mul(AC[:, 0:NT], selm[:, :], stro[:, :]))
        b.v(lambda e: e.tensor_sub(AC[:, 0:NT], stro[:, :], AC[:, 0:NT]))
        b.v(lambda e: e.tensor_scalar(AC[:, 0:NT], AC[:, 0:NT], 1.0, None, ALU.add))
        b.v(lambda e: e.tensor_mul(BC[:, :], selm[:, :], b2c[:, :]))
        tv_ab = b.v(lambda e: e.tensor_sub(BC[:, :], b2c[:, :], BC[:, :]))
        b.w("scalar", sV, tv_ab)
        st_prev = {}
        t_store = None
        for t in range(NT):
            bb = BASE2[t % 2]
            if t >= 2:
                b.w("scalar", qS, st_prev[t - 2])
            ts_ = b.s(lambda e, t=t, bb=bb: e.activation(
                bb[:, :], x_tok[:, t * D:(t + 1) * D], AF.Identity,
                bias=BC[:, t:t + 1], scale=AC[:, t:t + 1]))
            b.w("sync", sS, ts_)
            t_store = b.dma("sync", lambda e, t=t, bb=bb, bi=bi: e.dma_start(
                out=out_t[bi, t * 128:(t + 1) * 128, :], in_=bb[:, :]))
            st_prev[t] = t_store
        t_store = qprobe("sync")
        b.w("gpsimd", qS, t_store)
        b.w("gpsimd", sV, tv_idx)
        b.w("gpsimd", qG, t_outinit)
        out_flat = out_t.rearrange("b t d -> (b t) d")
        for rb in range(2):
            b.emit("gpsimd", lambda e, rb=rb, out_flat=out_flat: e.indirect_dma_start(
                out=XRS[:, rb * D:(rb + 1) * D], out_offset=None,
                in_=out_flat, in_offset=IndirectOffsetOnAxis(ap=selidx[:, rb:rb + 1], axis=0),
            ), qG, 16)
            b.emit("gpsimd", lambda e, rb=rb: e.indirect_dma_start(
                out=CSG[:, rb * RS:(rb + 1) * RS], out_offset=None,
                in_=dram["CSTAB"][:, :], in_offset=IndirectOffsetOnAxis(ap=selidx[:, rb:rb + 1], axis=0),
            ), qG, 16)
        qprobe("gpsimd")
        b.fence()
        tap("w%d" % bi, TMP[:, :])
        tap("selm%d" % bi, selm[:, :])
        tap("ccum%d" % bi, ccum[:, :])
        tap("selcols%d" % bi, selcols[:, :])
        tap("xrs%d" % bi, XRS[:, :])

    fmS = Scope()

    def phaseFmain(bi, xload_next):
        m = fmS
        XLN = m.g("xln", 2 * D)
        HTOK = m.g("htok", 2 * D)
        HT = m.g("hT", 2048)
        QT = m.g("qT", 2048)
        KT = m.g("kT", 2048)
        VT_ = m.g("vT", 2048)
        CTX = m.g("ctxT", 2048)
        COS2 = m.g("cos2", RS)
        SIN2 = m.g("sin2", RS)
        ASB = m.g("asb", RS)
        ATB = m.g("atb", 2 * RS)
        VTOK = m.g("vtok", 2 * 128)
        H1G = m.g("h1g", (FC // 128) * RS)
        WBUF = [m.g("wb0", D), m.g("wb1", D), m.g("wb2", D)]
        WBUF2 = [m.g("w20", D), m.g("w21", D), m.g("fin", D)]
        RCOL = m.g("rcol", 8)
        FIN = m.g("fin", D)
        DUMP.update(dict(xln=XLN, htok=HTOK, hT=HT, qT=QT, kT=KT, vT=VT_,
                         ctx=CTX, cos2=COS2, sin2=SIN2, h1g=H1G, fin=FIN, rcol=RCOL))
        if xload_next is not None:
            b.w("sync", sS, sS.n)
            b.dma("sync", lambda e, xload_next=xload_next: e.dma_start(
                out=x_tok.rearrange("q (n d) -> q n d", d=D),
                in_=x_in[xload_next].rearrange("(n p) d -> p n d", p=128)))
            t_xn = qprobe("sync")
        else:
            t_xn = None
        for rb in range(2):
            b.s(lambda e, rb=rb: e.activation(
                XLN[:, rb * D:(rb + 1) * D], XRS[:, rb * D:(rb + 1) * D], AF.Identity,
                scale=selcols[:, 5 * rb + 2:5 * rb + 3], bias=selcols[:, 5 * rb + 3:5 * rb + 4]))
        for rb in range(2):
            b.s(lambda e, rb=rb: e.activation(HTOK[:, rb * D:(rb + 1) * D], XLN[:, rb * D:(rb + 1) * D],
                                              AF.Square, accum_out=RCOL[:, rb:rb + 1]))
        ts_ = b.s(lambda e: e.activation(RCOL[:, 2:4], RCOL[:, 0:2], AF.Sqrt, scale=epsc[:, 2:3], bias=epsc[:, 1:2]))
        b.w("vector", sS, ts_)
        b.v(lambda e: e.reciprocal(RCOL[:, 2:4], RCOL[:, 2:4]))
        for rb in range(2):
            b.v(lambda e, rb=rb: e.tensor_scalar(
                HTOK[:, rb * D:(rb + 1) * D], XLN[:, rb * D:(rb + 1) * D],
                RCOL[:, 2 + rb:3 + rb], None, ALU.mult))
        tv_hn = sV.n

        def tok2feat(src, dst, scale_cols, tv_ready):
            b.w("tensor", sV, tv_ready)
            evs = {}
            banks = (PS[2], PS[3], PS[4], PS[5])
            for dc in range(ND):
                bank = banks[dc % 4]
                if dc >= 4:
                    b.w("tensor", sV, evs[dc - 4])
                for rb in range(2):
                    b.t(lambda e, dc=dc, rb=rb, bank=bank, src=src: e.transpose(
                        bank[:, rb * 128:(rb + 1) * 128],
                        src[:, rb * D + dc * 128:rb * D + (dc + 1) * 128], S["cIDENT"][:, :]))
                tmm = sT.n
                b.w("vector", sT, tmm)
                if scale_cols is not None:
                    evs[dc] = b.v(lambda e, dc=dc, bank=bank, dst=dst, sc=scale_cols:
                                  e.tensor_scalar(dst[:, dc * RS:(dc + 1) * RS],
                                                  bank[:, 0:256], sc[:, dc:dc + 1], None, ALU.mult))
                else:
                    evs[dc] = b.v(lambda e, dc=dc, bank=bank, dst=dst:
                                  e.tensor_copy(dst[:, dc * RS:(dc + 1) * RS], bank[:, 0:256]))
            b.w("tensor", sV, sV.n)

        tok2feat(HTOK, HT, ln1sb, tv_hn)
        if bi == 0:
            tap("hnT", HT[:, :])
        for rb in range(2):
            for which, dstb in ((0, COS2), (1, SIN2)):
                ps = PS[4][:, 0:128]
                b.t(lambda e, rb=rb, which=which, ps=ps: e.transpose(
                    ps, CSG[:, rb * RS + which * 128:rb * RS + (which + 1) * 128], S["cIDENT"][:, :]))
                tmm = sT.n
                b.w("vector", sT, tmm)
                tv = b.v(lambda e, rb=rb, ps=ps, dstb=dstb: e.tensor_copy(
                    dstb[:, rb * 128:(rb + 1) * 128], ps))
                b.w("tensor", sV, tv)

        def proj(wname, dst, rope):
            mm_done = {}
            b.w("sync", sT, sT.n)
            for k in range(ND):
                if k >= 3:
                    b.w("sync", sT, mm_done[k - 3])
                tl = b.dma("sync", lambda e, wname=wname, k=k: e.dma_start(
                    out=WBUF[k % 3][:, :], in_=dram[wname][k * 128:(k + 1) * 128, :]))
                b.w("tensor", qS, (tl + 16) if k < ND - 1 else qprobe("sync"))
                for mt in range(ND):
                    ps = PS[mt // 2][:, (mt % 2) * 256:(mt % 2) * 256 + 256]
                    b.t(lambda e, k=k, mt=mt, ps=ps: e.matmul(
                        ps, WBUF[k % 3][:, mt * 128:(mt + 1) * 128], HT[:, k * RS:(k + 1) * RS],
                        start=(k == 0 and mt % 2 == 0), stop=(k == ND - 1 and mt % 2 == 1)))
                mm_done[k] = sT.n
            b.w("vector", sT, sT.n)
            for mt in range(ND):
                ps = PS[mt // 2][:, (mt % 2) * 256:(mt % 2) * 256 + 256]
                b.v(lambda e, mt=mt, ps=ps, dst=dst: e.tensor_copy(dst[:, mt * RS:(mt + 1) * RS], ps))
            b.w("tensor", sV, sV.n)
            if rope:
                rope_ev = {}
                for mt in range(ND):
                    ps = (PS[6] if mt % 2 == 0 else PS[7])[:, 0:256]
                    if mt >= 2:
                        b.w("tensor", sV, rope_ev[mt - 2])
                    b.t(lambda e, mt=mt, ps=ps, dst=dst: e.matmul(
                        ps, S["cROTT"][:, :], dst[:, mt * RS:(mt + 1) * RS], start=True, stop=True))
                    tmm = sT.n
                    b.w("vector", sT, tmm)
                    b.v(lambda e, mt=mt, dst=dst: e.tensor_mul(
                        ASB[:, :], dst[:, mt * RS:(mt + 1) * RS], COS2[:, :]))
                    b.v(lambda e, mt=mt, ps=ps: e.tensor_mul(ATB[:, 0:RS], ps, SIN2[:, :]))
                    rope_ev[mt] = b.v(lambda e, mt=mt, dst=dst: e.tensor_add(
                        dst[:, mt * RS:(mt + 1) * RS], ASB[:, :], ATB[:, 0:RS]))
                b.w("tensor", sV, sV.n)

        if bi == 0:
            tap("xlnF", XLN[:, :])
        proj("WQ", QT, True)
        if bi == 0:
            tap("qTrope", QT[:, :])
            tap("cos2", COS2[:, :])
            tap("sin2", SIN2[:, :])
        proj("WK", KT, True)
        if bi == 0:
            tap("qT_afterK", QT[:, :])
        proj("WV", VT_, False)
        if bi == 0:
            tap("qT_afterV", QT[:, :])
        if bi == 0:
            tap("vT", VT_[:, :])
        for h in range(H):
            mt, po = h // 2, 64 * (h % 2)
            if bi == 0 and h in (0, 2, 4, 6):
                tap("qT_at%d" % h, QT[:, :])
            if h % 2 == 0:
                for kc in range(2):
                    ps = PS[5][:, kc * 128:(kc + 1) * 128]
                    b.t(lambda e, mt=mt, kc=kc, ps=ps: e.transpose(
                        ps, VT_[:, mt * RS + kc * 128:mt * RS + (kc + 1) * 128], S["cIDENT"][:, :]))
                tmm = sT.n
                b.w("vector", sT, tmm)
                b.v(lambda e: e.tensor_copy(VTOK[:, 0:128], PS[5][:, 0:128]))
                tv = b.v(lambda e: e.tensor_copy(VTOK[:, 128:256], PS[5][:, 128:256]))
                b.w("tensor", sV, tv)
            for qc in range(2):
                ps = (PS[4] if qc == 0 else PS[7])[:, 0:256]
                b.t(lambda e, mt=mt, po=po, qc=qc, ps=ps: e.matmul(
                    ps, QT[po:po + 64, mt * RS + qc * 128:mt * RS + (qc + 1) * 128],
                    KT[po:po + 64, mt * RS:(mt + 1) * RS], start=True, stop=True))
                tmm = sT.n
                b.w("vector", sT, tmm)
                b.v(lambda e, qc=qc, ps=ps: e.scalar_tensor_tensor(
                    ASB[:, :], ps, 0.125, S["cCAUSAL"][:, qc * RS:(qc + 1) * RS], ALU.mult, ALU.add))
                tv = b.v(lambda e: e.reduce_max(RCOL[:, 6:7], ASB[:, :], axis=AXX, negate=True))
                b.w("scalar", sV, tv)
                ts_ = b.s(lambda e: e.activation(ASB[:, :], ASB[:, :], AF.Exp, bias=RCOL[:, 6:7]))
                b.w("vector", sS, ts_)
                b.v(lambda e: e.reduce_sum(RCOL[:, 7:8], ASB[:, :], axis=AXX))
                b.v(lambda e: e.reciprocal(RCOL[:, 7:8], RCOL[:, 7:8]))
                tv = b.v(lambda e: e.tensor_scalar(ASB[:, :], ASB[:, :], RCOL[:, 7:8], None, ALU.mult))
                if bi == 0 and h in (0, 7) and qc == 0:
                    tap("asb_h%d" % h, ASB[:, :])
                b.w("tensor", sV, tv)
                for kc in range(2):
                    ps2 = PS[5][:, 256:384]
                    b.t(lambda e, kc=kc, ps2=ps2: e.transpose(ps2, ASB[:, kc * 128:(kc + 1) * 128], S["cIDENT"][:, :]))
                    tmm = sT.n
                    b.w("vector", sT, tmm)
                    tv = b.v(lambda e, kc=kc, qc=qc, ps2=ps2: e.tensor_copy(
                        ATB[:, kc * RS + qc * 128:kc * RS + (qc + 1) * 128], ps2))
                    b.w("tensor", sV, tv)
            for kc in range(2):
                if po == 0:
                    b.t(lambda e, kc=kc: e.matmul(
                        PS[6][0:64, 0:256], VTOK[:, kc * 128:kc * 128 + 64],
                        ATB[:, kc * RS:(kc + 1) * RS], start=(kc == 0), stop=(kc == 1)))
                else:
                    b.t(lambda e, kc=kc: e.matmul(
                        PS[6][64:128, 0:256], VTOK[:, kc * 128 + 64:kc * 128 + 128],
                        ATB[:, kc * RS:(kc + 1) * RS], start=(kc == 0), stop=(kc == 1),
                        tile_position=(0, 64)))
            tmm = sT.n
            b.w("vector", sT, tmm)
            tv = b.v(lambda e, mt=mt, po=po: e.tensor_copy(
                CTX[po:po + 64, mt * RS:(mt + 1) * RS], PS[6][po:po + 64, 0:256]))
            if bi == 0 and h in (0, 7):
                tap("atb_h%d" % h, ATB[:, :])
            b.w("tensor", sV, tv)
        if bi == 0:
            tap("ctxT", CTX[:, :])
        mm_done = {}
        b.w("sync", sT, sT.n)
        for k in range(ND):
            if k >= 3:
                b.w("sync", sT, mm_done[k - 3])
            tl = b.dma("sync", lambda e, k=k: e.dma_start(
                out=WBUF[k % 3][:, :], in_=dram["WOf"][k * 128:(k + 1) * 128, :]))
            b.w("tensor", qS, (tl + 16) if k < ND - 1 else qprobe("sync"))
            for mt in range(ND):
                ps = PS[mt // 2][:, (mt % 2) * 256:(mt % 2) * 256 + 256]
                b.t(lambda e, k=k, mt=mt, ps=ps: e.matmul(
                    ps, WBUF[k % 3][:, mt * 128:(mt + 1) * 128], CTX[:, k * RS:(k + 1) * RS],
                    start=(k == 0 and mt % 2 == 0), stop=(k == ND - 1 and mt % 2 == 1)))
            mm_done[k] = sT.n
        b.w("vector", sT, sT.n)
        for mt in range(ND):
            ps = PS[mt // 2][:, (mt % 2) * 256:(mt % 2) * 256 + 256]
            b.v(lambda e, mt=mt, ps=ps: e.tensor_copy(KT[:, mt * RS:(mt + 1) * RS], ps))
        b.w("tensor", sV, sV.n)
        evs = {}
        banks = (PS[2], PS[3], PS[4], PS[5])
        XLNV = XLN.rearrange("p (r x) -> p r x", x=D)
        for dc in range(ND):
            bank = banks[dc % 4]
            if dc >= 4:
                b.w("tensor", sV, evs[dc - 4])
            for rb in range(2):
                b.t(lambda e, dc=dc, rb=rb, bank=bank: e.transpose(
                    bank[:, rb * 128:(rb + 1) * 128],
                    KT[:, dc * RS + rb * 128:dc * RS + (rb + 1) * 128], S["cIDENT"][:, :]))
            tmm = sT.n
            b.w("vector", sT, tmm)
            evs[dc] = b.v(lambda e, dc=dc, bank=bank: e.tensor_add(
                XLNV[:, :, dc * 128:(dc + 1) * 128],
                XLNV[:, :, dc * 128:(dc + 1) * 128], bank[:, 0:256]))
        if bi == 0:
            tap("x1tok", XLN[:, :])
        b.w("scalar", sV, sV.n)
        for rb in range(2):
            b.s(lambda e, rb=rb: e.activation(HTOK[:, rb * D:(rb + 1) * D], XLN[:, rb * D:(rb + 1) * D],
                                              AF.Square, accum_out=RCOL[:, 4 + rb:5 + rb]))
        ts_ = b.s(lambda e: e.activation(RCOL[:, 4:6], RCOL[:, 4:6], AF.Sqrt, scale=epsc[:, 2:3], bias=epsc[:, 1:2]))
        b.w("vector", sS, ts_)
        b.v(lambda e: e.reciprocal(RCOL[:, 4:6], RCOL[:, 4:6]))
        for rb in range(2):
            b.v(lambda e, rb=rb: e.tensor_scalar(
                HTOK[:, rb * D:(rb + 1) * D], XLN[:, rb * D:(rb + 1) * D],
                RCOL[:, 4 + rb:5 + rb], None, ALU.mult))
        tok2feat(HTOK, QT, ln2sb, sV.n)
        h1_done = {}
        w2g_done = {}
        b.w("sync", sT, sT.n)
        b.w("scalar", sT, sT.n)
        for fc in range(NFC):
            for k in range(ND):
                g = fc * ND + k
                if g >= 3:
                    b.w("sync", sT, h1_done[g - 3])
                tl = b.dma("sync", lambda e, fc=fc, k=k, g=g: e.dma_start(
                    out=WBUF[g % 3][:, 0:FC], in_=dram["W1"][k * 128:(k + 1) * 128, fc * FC:(fc + 1) * FC]))
                b.w("tensor", qS, (tl + 16) if g < NFC * ND - 1 else qprobe("sync"))
                for mf in range(FC // 128):
                    ps = PS[4 + mf // 2][:, (mf % 2) * 256:(mf % 2) * 256 + 256]
                    b.t(lambda e, k=k, mf=mf, g=g, ps=ps: e.matmul(
                        ps, WBUF[g % 3][:, mf * 128:(mf + 1) * 128], QT[:, k * RS:(k + 1) * RS],
                        start=(k == 0 and mf % 2 == 0), stop=(k == ND - 1 and mf % 2 == 1)))
                h1_done[g] = sT.n
            b.w("scalar", sT, sT.n)
            for mf in range(FC // 128):
                ps = PS[4 + mf // 2][:, (mf % 2) * 256:(mf % 2) * 256 + 256]
                b.s(lambda e, mf=mf, ps=ps: e.activation(H1G[:, mf * RS:(mf + 1) * RS], ps, AF.Gelu_apprx_tanh))
            b.w("tensor", sS, sS.n)
            for kf in range(FC // 128):
                g2 = fc * (FC // 128) + kf
                if g2 >= 3:
                    b.w("gpsimd", sT, w2g_done[g2 - 3])
                tl2 = b.dma("gpsimd", lambda e, fc=fc, kf=kf, g2=g2: e.dma_start(
                    out=WBUF2[g2 % 3][:, :], in_=dram["W2"][fc * FC + kf * 128:fc * FC + (kf + 1) * 128, :]))
                b.w("tensor", qG, (tl2 + 16) if g2 < NFC * (FC // 128) - 1 else qprobe("gpsimd"))
                for mt in range(ND):
                    ps = PS[mt // 2][:, (mt % 2) * 256:(mt % 2) * 256 + 256]
                    b.t(lambda e, kf=kf, mt=mt, g2=g2, ps=ps, fc=fc: e.matmul(
                        ps, WBUF2[g2 % 3][:, mt * 128:(mt + 1) * 128], H1G[:, kf * RS:(kf + 1) * RS],
                        start=(fc == 0 and kf == 0 and mt % 2 == 0),
                        stop=(fc == NFC - 1 and kf == FC // 128 - 1 and mt % 2 == 1)))
                w2g_done[g2] = sT.n
        b.w("vector", sT, sT.n)
        for mt in range(ND):
            ps = PS[mt // 2][:, (mt % 2) * 256:(mt % 2) * 256 + 256]
            b.v(lambda e, mt=mt, ps=ps: e.tensor_copy(KT[:, mt * RS:(mt + 1) * RS], ps))
        b.w("tensor", sV, sV.n)
        t_sc0 = None
        evs = {}
        fbanks = (PS[2], PS[3], PS[4], PS[5])
        for rb in range(2):
            for dc in range(ND):
                k = rb * ND + dc
                ps = fbanks[k % 4][:, 0:128]
                if k >= 4:
                    b.w("tensor", sV, evs[k - 4])
                b.t(lambda e, dc=dc, rb=rb, ps=ps: e.transpose(
                    ps, KT[:, dc * RS + rb * 128:dc * RS + (rb + 1) * 128], S["cIDENT"][:, :]))
                tmm = sT.n
                b.w("vector", sT, tmm)
                b.v(lambda e, dc=dc, rb=rb, ps=ps: e.tensor_add(
                    FIN[:, dc * 128:(dc + 1) * 128],
                    XLN[:, rb * D + dc * 128:rb * D + (dc + 1) * 128], ps))
                b.v(lambda e, dc=dc, rb=rb: e.tensor_scalar(
                    FIN[:, dc * 128:(dc + 1) * 128], FIN[:, dc * 128:(dc + 1) * 128],
                    selcols[:, 5 * rb + 1:5 * rb + 2], None, ALU.mult))
                evs[k] = b.v(lambda e, dc=dc, rb=rb: e.tensor_add(
                    FIN[:, dc * 128:(dc + 1) * 128], FIN[:, dc * 128:(dc + 1) * 128],
                    XRS[:, rb * D + dc * 128:rb * D + (dc + 1) * 128]))
            tvf = sV.n
            if bi == 0:
                tap("fin%d" % rb, FIN[:, :])
            b.w("gpsimd", sV, tvf)
            out_flat = out_t.rearrange("b t d -> (b t) d")
            t_sc = b.emit("gpsimd", lambda e, rb=rb, out_flat=out_flat: e.indirect_dma_start(
                out=out_flat, out_offset=IndirectOffsetOnAxis(ap=selidx[:, 2 + rb:3 + rb], axis=0),
                in_=FIN[:, :], in_offset=None,
                bounds_check=BPC * T - 1, oob_is_err=False,
            ), qG, 16)
            if rb == 0:
                t_sc0 = t_sc
                b.w("vector", qG, t_sc0)
        b.fence()
        return t_xn

    stages = ["pre", "A0", "F0", "M0", "A1", "F1", "M1"]
    upto = stages.index(stop_after) if stop_after in stages else len(stages)
    if upto >= 1:
        phaseA(0, t_x0)
    if upto >= 2:
        phaseFpre(0)
    if upto >= 3:
        t_x1 = phaseFmain(0, 1)
    if upto >= 4:
        phaseA(1, t_x1)
    if upto >= 5:
        phaseFpre(1)
    if upto >= 6:
        phaseFmain(1, None)
    DUMP.update(dict(x_tok=x_tok, selcols=selcols, selm=selm, stmu=stmu, stro=stro, wbias_rep=wbias_rep, negs=negs, orvd=orv, invc=invc, wbias_row=wbias_row, wrcd=wrc, srepd=srepc, abic=abic, atric=atric,
                     wraw=wraw, ccum=ccum, acat=acat, outb=outb, xrs=XRS, csg=CSG))
    emit_dumps()
    b.w("sync", qG, qG.n)
    b.w("sync", qA, qA.n)
    b.w("sync", qS, qS.n)

    b.finalize()
    return nc, dram


_NC_CACHE = {}


def get_nc(dbg_names=(), stop_after="all", end_dumps=()):
    key = (tuple(dbg_names), stop_after, tuple(end_dumps))
    if key not in _NC_CACHE:
        _NC_CACHE[key] = build_nc(dbg_names, stop_after, end_dumps)
    return _NC_CACHE[key]


def make_in_map(inputs, core):
    x = np.ascontiguousarray(np.asarray(inputs["L0_attn_x"], dtype=np.float32))
    base = {
        "WQa": inputs["L0_attn_WQ"], "WKa": inputs["L0_attn_WK"],
        "WOa": inputs["L0_attn_WO"], "WINV": inputs["L0_attn_Winv"],
        "UB": inputs["L0_attn_U_b"], "VB": inputs["L0_attn_V_b"], "WB": inputs["L0_attn_W_b"],
        "UT": inputs["L0_attn_U_t"], "VT": inputs["L0_attn_V_t"], "WT": inputs["L0_attn_W_t"],
        "XTm": inputs["L0_attn_X_t"],
        "BIASB": np.asarray(inputs["L0_attn_bias_b"]).reshape(D, 1),
        "BIAST": np.asarray(inputs["L0_attn_bias_t"]).reshape(D, 1),
        "ABI": np.asarray(inputs["L0_attn_alpha_bi"]).reshape(1, 1),
        "ATRI": np.asarray(inputs["L0_attn_alpha_tri"]).reshape(1, 1),
        "RW": np.asarray(inputs["L0_ffn_router_w"]).reshape(D, 1),
        "RB": np.asarray(inputs["L0_ffn_router_b"]).reshape(1, 1),
        "WQ": inputs["L0_ffn_wq"], "WK": inputs["L0_ffn_wk"], "WV": inputs["L0_ffn_wv"],
        "WOf": inputs["L0_ffn_wo"], "W1": inputs["L0_ffn_w1"], "W2": inputs["L0_ffn_w2"],
        "LN1": np.asarray(inputs["L0_ffn_ln1"]).reshape(D, 1),
        "LN2": np.asarray(inputs["L0_ffn_ln2"]).reshape(D, 1),
    }
    base = {k: np.ascontiguousarray(np.asarray(v, dtype=np.float32)) for k, v in base.items()}
    for k, v in CONSTS.items():
        base[k] = v
    base["x"] = np.ascontiguousarray(x[2 * core:2 * core + 2])
    return base


def kernel(**inputs):
    nc, _ = get_nc()
    in_maps = [make_in_map(inputs, c) for c in range(8)]
    res = run_bass_kernel_spmd(nc, in_maps, core_ids=list(range(8)))
    return np.concatenate([r["out"] for r in res.results], axis=0)
